# version 06
# baseline (speedup 1.0000x reference)
import contextlib
import math

import numpy as np
import ml_dtypes

import concourse.bass as bass
import concourse.mybir as mybir
from concourse.bass_utils import run_bass_kernel_spmd

F32 = mybir.dt.float32
BF16 = mybir.dt.bfloat16
AF = mybir.ActivationFunctionType
ALU = mybir.AluOpType
AX = mybir.AxisListType

NCORE = 8
T = 4096
D = 1024
NH = 16
HS = 64
L = 512
QL = 512
TILE = 512
NT = T // TILE
E = 16
IFF = 4096
ISH = 2048
NIC = IFF // 128
NISC = ISH // 128
K2 = 640
K1 = 128
CGLOB = 4096
EPS = 1e-5
SCALE = 1.0 / math.sqrt(HS)
DEV_DUMPS = False


class _Op:
    __slots__ = ("eng", "fn", "reads", "writes", "slot", "dma_val", "deps",
                 "needs_inc", "inc_val", "block", "kind", "tag", "_waits",
                 "wout", "eng_seq", "_drain")

    def __init__(self, eng, fn, reads, writes, slot, block, kind, tag):
        self.eng = eng
        self.fn = fn
        self.reads = reads
        self.writes = writes
        self.slot = slot
        self.dma_val = 0
        self.deps = []
        self.needs_inc = False
        self.inc_val = 0
        self.block = block
        self.kind = kind
        self.tag = tag
        self.wout = 0
        self.eng_seq = 0


class Em:
    ENGS = ("t", "v", "s", "g", "sp")

    def __init__(self, nc, es):
        self.nc = nc
        self.ops = []
        self.blocks = 1
        self.last_writer = {}
        self.readers = {}
        self.slot_counts = {}
        self.slot_policy = {}
        self.exec_sems = {e: es.enter_context(nc.semaphore(f"ex_{e}"))
                         for e in self.ENGS}
        self.slot_sems = {}
        self.cc_sem = es.enter_context(nc.semaphore("cc_sem"))
        self.cc_count = 0
        self._eseq = {}
        self._es = es

    def op(self, eng, fn, reads=(), writes=(), tag="", wout=0):
        o = self._rec(eng, fn, reads, writes, None, "op", tag)
        o.wout = wout

    def dma(self, eng, fn, reads=(), writes=(), slot=None, policy="seq", tag=""):
        assert slot is not None
        if slot not in self.slot_sems:
            self.slot_sems[slot] = self._es.enter_context(
                self.nc.semaphore(f"dm_{slot}"))
        self.slot_policy.setdefault(slot, policy)
        assert self.slot_policy[slot] == policy, slot
        self._rec(eng, fn, reads, writes, slot, "dma", tag)

    def cc(self, fn, reads=(), writes=(), tag="cc"):
        self._rec("g", fn, reads, writes, None, "cc", tag)

    def dma_burst(self, eng, fns_reads, writes, slot, policy="seq", tag=""):
        assert slot is not None
        if slot not in self.slot_sems:
            self.slot_sems[slot] = self._es.enter_context(
                self.nc.semaphore(f"dm_{slot}"))
        self.slot_policy.setdefault(slot, policy)
        prev_deps = []
        for w_ in writes:
            pw = self.last_writer.get(w_)
            if pw is not None:
                prev_deps.append((pw, False))
            prev_deps.extend((rd, False) for rd in self.readers.get(w_, ()))
        ops = []
        for fn, reads in fns_reads:
            o = _Op(eng, fn, tuple(reads), (), slot, self.blocks, "dma", tag)
            self.slot_counts[slot] = self.slot_counts.get(slot, 0) + 1
            o.dma_val = 16 * self.slot_counts[slot]
            o.deps.extend(prev_deps)
            for r in o.reads:
                w = self.last_writer.get(r)
                if w is not None:
                    o.deps.append((w, True))
                self.readers.setdefault(r, []).append(o)
            o.eng_seq = self._eseq.get(eng, 0)
            self._eseq[eng] = o.eng_seq + 1
            self.ops.append(o)
            ops.append(o)
        for w_ in writes:
            self.last_writer[w_] = ops[-1]
            self.readers[w_] = []

    def _rec(self, eng, fn, reads, writes, slot, kind, tag):
        o = _Op(eng, fn, tuple(reads), tuple(writes), slot, self.blocks, kind, tag)
        if kind == "dma":
            self.slot_counts[slot] = self.slot_counts.get(slot, 0) + 1
            o.dma_val = 16 * self.slot_counts[slot]
        elif kind == "cc":
            self.cc_count += 1
            o.dma_val = self.cc_count
        deps = o.deps
        for r in o.reads:
            w = self.last_writer.get(r)
            if w is not None:
                deps.append((w, True))
            self.readers.setdefault(r, []).append(o)
        for w_ in o.writes:
            pw = self.last_writer.get(w_)
            if pw is not None:
                deps.append((pw, False))
            for rd in self.readers.get(w_, ()):
                if rd is not o:
                    deps.append((rd, False))
            self.last_writer[w_] = o
            self.readers[w_] = [o] if w_ in o.reads else []
        o.eng_seq = self._eseq.get(eng, 0)
        self._eseq[eng] = o.eng_seq + 1
        self.ops.append(o)
        return o

    def new_block(self):
        self.blocks += 1

    def emit(self):
        for o in self.ops:
            for d, _raw in o.deps:
                if d.kind == "op" and d.eng != o.eng:
                    d.needs_inc = True
        counts = {e: 0 for e in self.ENGS}
        for o in self.ops:
            if o.kind == "op" and o.needs_inc:
                counts[o.eng] += 1
                o.inc_val = counts[o.eng]
        for o in self.ops:
            waits = {}
            need_drain = 0
            for d, raw in o.deps:
                if d is o:
                    continue
                if d.kind == "dma":
                    sem = self.slot_sems[d.slot]
                    val = (16 * self.slot_counts[d.slot]
                           if self.slot_policy[d.slot] == "total" else d.dma_val)
                elif d.kind == "cc":
                    sem, val = self.cc_sem, d.dma_val
                else:
                    if d.eng == o.eng:
                        if (raw and o.kind == "op" and d.kind == "op"
                                and d.wout < 256):
                            need_drain = max(need_drain, d.eng_seq + 1)
                        continue
                    sem, val = self.exec_sems[d.eng], d.inc_val
                k = sem.name
                if k not in waits or waits[k][1] < val:
                    waits[k] = (sem, val)
            o._waits = list(waits.values())
            o._drain = need_drain
        nc = self.nc
        watermark = {e: {} for e in self.ENGS}
        drained = {}
        for b in range(1, self.blocks + 1):
            ops_b = [o for o in self.ops if o.block == b]
            if not ops_b:
                continue
            per_eng = {e: [o for o in ops_b if o.eng == e] for e in self.ENGS}
            with nc.Block() as block:
                for ename, section in (("g", block.gpsimd), ("s", block.scalar),
                                       ("t", block.tensor), ("v", block.vector),
                                       ("sp", block.sync)):
                    lst = per_eng[ename]
                    if not lst:
                        continue

                    def body(eng, lst=lst, ename=ename):
                        wm = watermark[ename]
                        for o in lst:
                            if getattr(o, "_drain", 0) > drained.get(ename, 0):
                                eng.drain()
                                drained[ename] = o.eng_seq
                            for sem, val in o._waits:
                                if wm.get(sem.name, -1) >= val:
                                    continue
                                eng.wait_ge(sem, val)
                                wm[sem.name] = val
                            inst = o.fn(eng)
                            if o.kind == "dma":
                                inst.then_inc(self.slot_sems[o.slot], 16)
                            elif o.kind == "cc":
                                inst.then_inc(self.cc_sem, 1)
                            elif o.needs_inc:
                                inst.then_inc(self.exec_sems[ename], 1)
                    section(body)


def build_program():
    nc = bass.Bass("TRN2", target_bir_lowering=False)

    def din(name, shape, dtype=F32):
        return nc.dram_tensor(name, list(shape), dtype, kind="ExternalInput").ap()

    xT = din("xT", (8, 128, T))
    wdq = din("wdq", (4, 128, D))
    wdqT = din("wdqT", (8, 128, QL))
    wuq = din("wuq", (8, 128, QL))
    wuqT = din("wuqT", (4, 128, D))
    wuk = din("wuk", (8, 128, L))
    wuv = din("wuv", (8, 128, L))
    woT = din("woT", (8, 128, D))
    wdkvT = din("wdkvT", (8, 128, L))
    gwT = din("gwT", (8, 128, E))
    sg_dev = din("sg_dev", (NISC, 128, 1024))
    su_dev = din("su_dev", (NISC, 128, 1024))
    sd_dev = din("sd_dev", (NISC, 128, 1024))
    wg_dev = din("wg_dev", (E, NIC, 128, 1024))
    wu_dev = din("wu_dev", (E, NIC, 128, 1024))
    wd_dev = din("wd_dev", (E, 2, NIC, 128, 512))

    out = nc.dram_tensor("out", [T, D], F32, kind="ExternalOutput").ap()
    if DEV_DUMPS:
        d_x1 = nc.dram_tensor("d_x1", [NT, 8, 128, TILE], F32,
                              kind="ExternalOutput").ap()
        d_sh = nc.dram_tensor("d_sh", [NT, 8, 128, TILE], F32,
                              kind="ExternalOutput").ap()
        d_sc = nc.dram_tensor("d_sc", [128, T], F32,
                              kind="ExternalOutput").ap()
        d_th = nc.dram_tensor("d_th", [128, 8], F32,
                              kind="ExternalOutput").ap()
        d_rt = nc.dram_tensor("d_rt", [128, 6 * TILE], F32,
                              kind="ExternalOutput").ap()
        d_s2 = nc.dram_tensor("d_s2", [E, NT, 128, D], F32,
                              kind="ExternalOutput").ap()

    ckvTh = nc.dram_tensor("ckvTh", [NT, 128, 4 * TILE], BF16).ap()
    ckvKh = nc.dram_tensor("ckvKh", [NT, 128, 4 * TILE], BF16).ap()
    x1T_f32 = nc.dram_tensor("x1T_f32", [NT, 8, 128, TILE], F32).ap()
    x1T_b16 = nc.dram_tensor("x1T_b16", [NT, 8, 128, TILE], BF16).ap()
    x1tokh = nc.dram_tensor("x1tokh", [NT, 4, 128, D], BF16).ap()
    shTh = nc.dram_tensor("shTh", [NT, 8, 128, TILE], F32).ap()
    s2h = nc.dram_tensor("s2h", [E, NT, 128, D], BF16).ap()
    sc_inh = nc.dram_tensor("sc_inh", [E, T], F32)
    sc_outh = nc.dram_tensor("sc_outh", [128, T], F32)

    def const(name, arr):
        return nc.inline_tensor(np.ascontiguousarray(arr), name)

    c_ones_bf = const("c_ones_bf", np.ones((128, 1), ml_dtypes.bfloat16))
    c_ones1f = const("c_ones1f", np.ones((128, 1), np.float32))
    c_onesr_f = const("c_onesr_f", np.ones((1, 128), np.float32))
    c_ident = const("c_ident", np.eye(128, dtype=np.float32))
    c_iota_row = const("c_iota_row",
                       np.broadcast_to(np.arange(128, dtype=np.float32),
                                       (128, 128)))
    c_iota768 = const("c_iota768",
                      np.broadcast_to(np.arange(K2, dtype=np.float32), (128, K2)))
    mk_np = np.zeros((128, 4 * TILE), np.float32)
    for kc in range(4):
        kk = kc * 128 + np.arange(128)[:, None]
        qq = np.arange(TILE)[None, :]
        mk_np[:, kc * TILE:(kc + 1) * TILE] = np.where(kk <= qq, 0.0, -1.0e9)
    c_mask = const("c_mask", mk_np)
    g2_np = (np.arange(128)[None, :] % 16 == np.arange(16)[:, None]).astype(
        np.float32)
    c_g2 = const("c_g2", g2_np)
    c_g16 = const("c_g16", g2_np.T.copy())
    hm = np.zeros((128, 2), np.float32)
    hm[0:64, 0] = 1.0
    hm[64:128, 1] = 1.0
    c_halfmask = const("c_halfmask", hm)
    pj = np.arange(128)
    trij_np = ((pj[None, :] % 16 == pj[:, None] % 16)
               & (pj[:, None] // 16 < pj[None, :] // 16)).astype(np.float32)
    c_trij = const("c_trij", trij_np)

    gstack = contextlib.ExitStack()

    def mksb(stack):
        def sb(name, shape, dtype):
            return stack.enter_context(nc.sbuf_tensor(name, list(shape), dtype))
        return sb

    gsb = mksb(gstack)

    ones_bf = gsb("ones_bf", (128, 1), BF16)
    ones1f = gsb("ones1f", (128, 1), F32)
    onesr_f = gsb("onesr_f", (1, 128), F32)
    ident = gsb("ident", (128, 128), F32)
    iota_row = gsb("iota_row", (128, 128), F32)
    sco = gsb("sco", (128, TILE), F32)

    PS = [gstack.enter_context(nc.psum_tensor(f"ps{i}", [128, 512], F32))
          for i in range(8)]

    em = Em(nc, gstack)

    def mm(ps_ap, ps_reg, lhsT_ap, rhs_ap, start, stop, reads, tag=""):
        em.op("t", lambda e: e.matmul(ps_ap, lhsT_ap, rhs_ap,
                                      start=start, stop=stop),
              reads=reads, writes=[ps_reg], tag=tag)

    def ev(dst_ap, src_ap, reads, writes, tag="", wout=0):
        em.op("v", lambda e: e.tensor_copy(dst_ap, src_ap),
              reads=reads, writes=writes, tag=tag, wout=wout)

    astack = contextlib.ExitStack()
    asb = mksb(astack)

    maskt = asb("maskt", (128, 4 * TILE), F32)
    halfmask = asb("halfmask", (128, 2), F32)
    kerow2 = asb("kerow2", (128, 16), BF16)
    ch16row = asb("ch16row", (1, 16), F32)
    KEs = asb("KEs", (128, NH * 4 * 128), BF16)
    VEb = asb("VEb", (128, 4 * D), BF16)
    wdkvT_b = asb("wdkvT_b", (128, 8 * L), BF16)
    wdqT_b = asb("wdqT_b", (128, 8 * QL), BF16)
    wuqT_b = asb("wuqT_b", (128, 4 * D), BF16)
    gwT_b = asb("gwT_b", (128, 8 * E), BF16)

    p0stack = contextlib.ExitStack()
    p0sb = mksb(p0stack)
    wuq_b = p0sb("wuq_b", (128, 8 * QL), BF16)
    wuk_b = p0sb("wuk_b", (128, 8 * L), BF16)
    wuv_b = p0sb("wuv_b", (128, 8 * L), BF16)
    woT_b = p0sb("woT_b", (128, 8 * D), BF16)
    wdq_b = p0sb("wdq_b", (128, 4 * D), BF16)
    Mb = p0sb("Mb", (128, 4 * L), BF16)
    sq_p0 = p0sb("sq_p0", (128, L), BF16)
    kerowc = p0sb("kerowc", (128, 8), F32)
    KEb = p0sb("KEb", (128, 8 * L), BF16)

    _cl = [(ones_bf, c_ones_bf, "ones_bf"), (ones1f, c_ones1f, "ones1f"),
           (onesr_f, c_onesr_f, "onesr_f"), (ident, c_ident, "ident"),
           (iota_row, c_iota_row, "iota_row"),
           (maskt, c_mask, "maskt"), (halfmask, c_halfmask, "halfmask")]
    em.dma_burst("sp", [(lambda e, d=d_, s=s_: e.dma_start(out=d[:], in_=s.ap()),
                         ()) for d_, s_, _ in _cl],
                 writes=[r for _, _, r in _cl], slot="p0c", policy="total",
                 tag="cload")

    _wl = []
    _wlregs = set()
    for dc in range(8):
        for dst_ap, src_ap, reg in (
                (wuq_b[:, dc * QL:(dc + 1) * QL], wuq[dc], "wuq_b"),
                (wuk_b[:, dc * L:(dc + 1) * L], wuk[dc], "wuk_b"),
                (wuv_b[:, dc * L:(dc + 1) * L], wuv[dc], "wuv_b"),
                (woT_b[:, dc * D:(dc + 1) * D], woT[dc], "woT_b"),
                (wdkvT_b[:, dc * L:(dc + 1) * L], wdkvT[dc], "wdkvT_b"),
                (wdqT_b[:, dc * QL:(dc + 1) * QL], wdqT[dc], "wdqT_b"),
                (gwT_b[:, dc * E:(dc + 1) * E], gwT[dc], "gwT_b")):
            _wl.append((lambda e, d=dst_ap, s=src_ap: e.dma_start(out=d, in_=s),
                        ()))
            _wlregs.add(reg)
    for qc in range(4):
        for dst_ap, src_ap, reg in (
                (wdq_b[:, qc * D:(qc + 1) * D], wdq[qc], "wdq_b"),
                (wuqT_b[:, qc * D:(qc + 1) * D], wuqT[qc], "wuqT_b")):
            _wl.append((lambda e, d=dst_ap, s=src_ap: e.dma_start(out=d, in_=s),
                        ()))
            _wlregs.add(reg)
    em.dma_burst("g", _wl, writes=sorted(_wlregs), slot="p0w", policy="total",
                 tag="wload")

    for qlc in range(4):
        for dc in range(8):
            mm(PS[0][:, 0:L], "ps0",
               wuq_b[:, dc * QL + qlc * 128: dc * QL + qlc * 128 + 128],
               wuk_b[:, dc * L:(dc + 1) * L],
               dc == 0, dc == 7, ["wuq_b", "wuk_b"], "Mmm")
        ev(Mb[:, qlc * L:(qlc + 1) * L], PS[0][:, 0:L], ["ps0"], ["Mb"], "Mev", wout=512)
    for dc in range(8):
        for qlc in range(4):
            mm(PS[1][:, 0:L], "ps1",
               wdq_b[:, qlc * D + dc * 128: qlc * D + dc * 128 + 128],
               Mb[:, qlc * L:(qlc + 1) * L],
               qlc == 0, qlc == 3, ["wdq_b", "Mb"], "KEmm")
        ev(KEb[:, dc * L:(dc + 1) * L], PS[1][:, 0:L], ["ps1"], ["KEb"], "KEev", wout=512)
    for dc in range(8):
        em.op("v", lambda e, dc=dc: e.scalar_tensor_tensor(
            sq_p0[:], KEb[:, dc * L:(dc + 1) * L], -1.0,
            KEb[:, dc * L:(dc + 1) * L], ALU.mult, ALU.max),
            reads=["KEb"], writes=["sq_p0"], tag="keabs", wout=512)
        em.op("v", lambda e, dc=dc: e.tensor_reduce(
            kerowc[:, dc: dc + 1], sq_p0[:], AX.X, ALU.add),
            reads=["sq_p0"], writes=["kerowc"], tag="kerow")
    for dc in range(8):
        for r_ in range(2):
            em.op("v", lambda e, dc=dc, r_=r_: e.tensor_mul(
                kerow2[:, 2 * dc + r_: 2 * dc + r_ + 1],
                halfmask[:, r_: r_ + 1], kerowc[:, dc: dc + 1]),
                reads=["halfmask", "kerowc"], writes=["kerow2"], tag="ker2")
    em.op("v", lambda e: e.memset(KEs[:], 0.0), writes=["KEs"], tag="KEsz")
    for h in range(NH):
        dc = h // 2
        r0 = 64 * (h % 2)
        for lc in range(4):
            ev(KEs[r0:r0 + 64, (h * 4 + lc) * 128:(h * 4 + lc) * 128 + 128],
               KEb[r0:r0 + 64, dc * L + lc * 128: dc * L + lc * 128 + 128],
               ["KEb"], ["KEs"], "KEsc")
    for lc in range(4):
        for half in range(2):
            for dc in range(8):
                mm(PS[2][:, 0:512], "ps2",
                   wuv_b[:, dc * L + lc * 128: dc * L + lc * 128 + 128],
                   woT_b[:, dc * D + half * 512: dc * D + half * 512 + 512],
                   dc == 0, dc == 7, ["wuv_b", "woT_b"], "VEmm")
            ev(VEb[:, lc * D + half * 512: lc * D + half * 512 + 512],
               PS[2][:, 0:512], ["ps2"], ["VEb"], "VEev", wout=512)

    p0stack.close()
    em.new_block()

    xt_f = [asb(f"xt_f{i}", (128, 8 * TILE), F32) for i in range(2)]
    xt_b = asb("xt_b", (128, 8 * TILE), BF16)
    ckvstT = asb("ckvstT", (128, 4 * TILE), BF16)
    ckvstK = asb("ckvstK", (128, 4 * TILE), BF16)
    ckvT_s = [asb(f"ckvT_s{i}", (128, 4 * TILE), BF16) for i in range(3)]
    ckvK_s = [asb(f"ckvK_s{i}", (128, 4 * TILE), BF16) for i in range(3)]
    qx_b = asb("qx_b", (128, 4 * TILE), BF16)
    q_b = asb("q_b", (128, 8 * TILE), BF16)
    qlat = [asb(f"qlat{i}", (128, 4 * TILE), BF16) for i in range(2)]
    qla = asb("qla", (128, TILE), BF16)
    pT = [asb(f"pT{i}", (128, TILE), BF16) for i in range(3)]
    ylat_b = asb("ylat_b", (128, 4 * TILE), BF16)
    yT = asb("yT", (128, 8 * TILE), F32)
    x1b = asb("x1b", (128, 8 * TILE), BF16)
    x1tok_sb = asb("x1tok_sb", (128, 2 * D), BF16)
    bca = asb("bca", (128, TILE), F32)
    bcb = asb("bcb", (128, TILE), F32)
    sq_t = asb("sq_t", (128, TILE), F32)
    srow = asb("srow", (1, 2 * TILE + 16), F32)
    s1row = asb("s1row", (1, TILE), F32)
    cmaxb = asb("cmaxb", (1, 8), F32)
    cmcol = asb("cmcol", (128, 2), F32)
    chb = asb("chb", (128, 1), F32)
    smt = asb("smt", (16, TILE), F32)
    scn = asb("scn", (16, TILE), F32)

    def p1_xload(j):
        cur = xt_f[j % 2]
        curR = f"xt_f{j % 2}"
        em.dma_burst("sp", [
            (lambda e, j=j, dc=dc, cur=cur: e.dma_start(
                out=cur[:, dc * TILE:(dc + 1) * TILE],
                in_=xT[dc, :, j * TILE:(j + 1) * TILE]), ())
            for dc in range(8)], writes=[curR], slot=curR, tag="xload")

    def p1_xcast(j):
        cur = xt_f[j % 2]
        curR = f"xt_f{j % 2}"
        for dc in range(8):
            ev(xt_b[:, dc * TILE:(dc + 1) * TILE],
               cur[:, dc * TILE:(dc + 1) * TILE], [curR], ["xt_b"], "xcast", wout=512)

    def p1_ckv(j):
        cur = xt_f[j % 2]
        curR = f"xt_f{j % 2}"
        for lc in range(4):
            for dc in range(8):
                mm(PS[4][:, 0:TILE], "ps4",
                   wdkvT_b[:, dc * L + lc * 128: dc * L + lc * 128 + 128],
                   xt_b[:, dc * TILE:(dc + 1) * TILE],
                   dc == 0, dc == 7, ["wdkvT_b", "xt_b"], "ckvT")
            ev(ckvstT[:, lc * TILE:(lc + 1) * TILE], PS[4][:, 0:TILE],
               ["ps4"], ["ckvstT"], "ckvTev", wout=512)
        em.dma("sp", lambda e, j=j: e.dma_start(out=ckvTh[j], in_=ckvstT[:]),
               reads=["ckvstT"], writes=[f"h:ckvT:{j}"], slot="ckvstT",
               tag="ckvTst")
        for tc in range(4):
            for dc in range(8):
                mm(PS[5][:, 0:L], "ps5",
                   xt_b[:, dc * TILE + tc * 128: dc * TILE + tc * 128 + 128],
                   wdkvT_b[:, dc * L:(dc + 1) * L],
                   dc == 0, dc == 7, ["xt_b", "wdkvT_b"], "ckvK")
            ev(ckvstK[:, tc * L:(tc + 1) * L], PS[5][:, 0:L],
               ["ps5"], ["ckvstK"], "ckvKev", wout=512)
        em.dma("sp", lambda e, j=j: e.dma_start(out=ckvKh[j], in_=ckvstK[:]),
               reads=["ckvstK"], writes=[f"h:ckvK:{j}"], slot="ckvstK",
               tag="ckvKst")
        em.op("v", lambda e: e.tensor_reduce(
            cmcol[:, 0:1], ckvstT[:], AX.X, ALU.max,
            apply_absolute_value=True),
            reads=["ckvstT"], writes=["cmcol"], tag="cabs")
        mm(PS[6][0:1, 0:128], "ps6", cmcol[:, 0:1], ident[:],
           True, True, ["cmcol", "ident"], "cmtr")
        em.op("v", lambda e: e.tensor_reduce(
            cmaxb[0:1, 1:2], PS[6][0:1, 0:128], AX.X, ALU.max),
            reads=["ps6"], writes=["cmaxb1"], tag="cmred")
        if j == 0:
            ev(cmaxb[0:1, 0:1], cmaxb[0:1, 1:2], ["cmaxb1"], ["cmaxb"], "cm0")
        else:
            em.op("v", lambda e: e.tensor_max(
                cmaxb[0:1, 0:1], cmaxb[0:1, 0:1], cmaxb[0:1, 1:2]),
                reads=["cmaxb", "cmaxb1"], writes=["cmaxb"], tag="cmu")

    def p1_qchain(j):
        cur = xt_f[j % 2]
        curR = f"xt_f{j % 2}"
        for qlc in range(4):
            for dc in range(8):
                mm(PS[6][:, 0:TILE], "ps6",
                   wdqT_b[:, dc * QL + qlc * 128: dc * QL + qlc * 128 + 128],
                   xt_b[:, dc * TILE:(dc + 1) * TILE],
                   dc == 0, dc == 7, ["wdqT_b", "xt_b"], "qx")
            ev(qx_b[:, qlc * TILE:(qlc + 1) * TILE], PS[6][:, 0:TILE],
               ["ps6"], ["qx_b"], "qxev", wout=512)
        for dc in range(8):
            for qlc in range(4):
                mm(PS[7][:, 0:TILE], "ps7",
                   wuqT_b[:, qlc * D + dc * 128: qlc * D + dc * 128 + 128],
                   qx_b[:, qlc * TILE:(qlc + 1) * TILE],
                   qlc == 0, qlc == 3, ["wuqT_b", "qx_b"], "qmm")
            ev(q_b[:, dc * TILE:(dc + 1) * TILE], PS[7][:, 0:TILE],
               ["ps7"], ["q_b"], "qev", wout=512)

    def p1_bounds(j):
        cur = xt_f[j % 2]
        curR = f"xt_f{j % 2}"
        em.op("v", lambda e: e.tensor_scalar(
            cmaxb[0:1, 3:4], cmaxb[0:1, 0:1], -SCALE, None, ALU.mult),
            reads=["cmaxb"], writes=["cmaxb3"], tag="cmneg")
        for dc in range(8):
            em.op("v", lambda e, dc=dc: e.scalar_tensor_tensor(
                qla[:], q_b[:, dc * TILE:(dc + 1) * TILE], -1.0,
                q_b[:, dc * TILE:(dc + 1) * TILE], ALU.mult, ALU.max),
                reads=["q_b"], writes=["qla"], tag="qabs", wout=512)
            for r_ in range(2):
                mm(PS[3][0:1, 0:TILE], "ps3",
                   kerow2[:, 2 * dc + r_: 2 * dc + r_ + 1], qla[:],
                   True, True, ["qla", "kerow2"], "bndmm")
                em.op("v", lambda e, dc=dc, r_=r_: e.tensor_reduce(
                    ch16row[0:1, 2 * dc + r_: 2 * dc + r_ + 1],
                    PS[3][0:1, 0:TILE], AX.X, ALU.max),
                    reads=["ps3"], writes=["ch16row"], tag="bndred")

    def p1_heads(j):
        cur = xt_f[j % 2]
        curR = f"xt_f{j % 2}"
        def emit_qlat(h):
            ql_s = qlat[h % 2]
            qlR = f"qlat{h % 2}"
            dch = h // 2
            for lc in range(4):
                mm(PS[6][:, 0:TILE], "ps6",
                   KEs[:, (h * 4 + lc) * 128:(h * 4 + lc) * 128 + 128],
                   q_b[:, dch * TILE:(dch + 1) * TILE],
                   True, True, ["KEs", "q_b"], "qlat")
                em.op("v", lambda e, lc=lc, ql_s=ql_s: e.tensor_copy(
                    ql_s[:, lc * TILE:(lc + 1) * TILE], PS[6][:, 0:TILE]),
                    reads=["ps6"], writes=[qlR], tag="qlatev", wout=512)

        def emit_ch(h):
            em.op("v", lambda e, h=h: e.tensor_tensor(
                cmaxb[0:1, 4:5], ch16row[0:1, h: h + 1], cmaxb[0:1, 3:4],
                ALU.mult),
                reads=["ch16row", "cmaxb3"], writes=["cmaxb4"], tag="chmul")
            mm(PS[3][:, 0:1], "ps3", onesr_f[:], cmaxb[0:1, 4:5], True, True,
               ["cmaxb4", "onesr_f"], "chbc")
            ev(chb[:, 0:1], PS[3][:, 0:1], ["ps3"], ["chb"], "chbev")

        def emit_kloop(h, j=j):
            ql_s = qlat[h % 2]
            qlR = f"qlat{h % 2}"
            nkc = j + 1
            nkg = nkc * 4

            def emit_S(kg):
                kc, ksc = kg // 4, kg % 4
                sl = kc % 3
                ctR = f"ckvT_s{sl}"
                if ksc == 0:
                    em.dma("sp", lambda e, kc=kc, sl=sl: e.dma_start(
                        out=ckvT_s[sl][:], in_=ckvTh[kc]),
                        reads=[f"h:ckvT:{kc}"], writes=[ctR], slot=ctR,
                        tag="ckvTld")
                    em.dma("g", lambda e, kc=kc, sl=sl: e.dma_start(
                        out=ckvK_s[sl][:], in_=ckvKh[kc]),
                        reads=[f"h:ckvK:{kc}"], writes=[f"ckvK_s{sl}"],
                        slot=f"ckvK_s{sl}", tag="ckvKld")
                sps = PS[4] if (kg % 2 == 0) else PS[5]
                spsR = "ps4" if (kg % 2 == 0) else "ps5"
                for lc in range(4):
                    mm(sps[:, 0:TILE], spsR,
                       ckvT_s[sl][:, lc * TILE + ksc * 128:
                                   lc * TILE + ksc * 128 + 128],
                       ql_s[:, lc * TILE:(lc + 1) * TILE],
                       lc == 0, lc == 3, [ctR, qlR], "Smm")
                if kc == j:
                    em.op("v", lambda e, sps=sps, ksc=ksc, spsR=spsR:
                          e.tensor_tensor(
                              sps[:, 0:TILE], sps[:, 0:TILE],
                              maskt[:, ksc * TILE:(ksc + 1) * TILE],
                              ALU.add),
                          reads=[spsR, "maskt"], writes=[spsR],
                          tag="maskadd", wout=512)
                pbuf = pT[kg % 3]
                pbR = f"pT{kg % 3}"
                em.op("s", lambda e, sps=sps, pbuf=pbuf: e.activation(
                    pbuf[:], sps[:, 0:TILE], AF.Exp,
                    bias=chb[:, 0:1], scale=SCALE),
                    reads=[spsR, "chb"], writes=[pbR], tag="exp", wout=512)

            def emit_PV(kg):
                kc, ksc = kg // 4, kg % 4
                sl = kc % 3
                ckR = f"ckvK_s{sl}"
                pbuf = pT[kg % 3]
                pbR = f"pT{kg % 3}"
                mm(PS[3][0:1, 0:TILE], "ps3", ones_bf[:], pbuf[:],
                   kg == 0, kg == nkg - 1, ["ones_bf", pbR], "denmm")
                for lc in range(4):
                    tgt, tgtR = ((PS[lc], f"ps{lc}") if lc < 3
                                 else (PS[7], "ps7"))
                    mm(tgt[:, 0:TILE], tgtR,
                       ckvK_s[sl][:, ksc * TILE + lc * 128:
                                  ksc * TILE + lc * 128 + 128],
                       pbuf[:],
                       kg == 0, kg == nkg - 1, [ckR, pbR], "PVmm")

            for kg in range(nkg):
                emit_S(kg)
                if kg > 0:
                    emit_PV(kg - 1)
            emit_PV(nkg - 1)

        def emit_tail(h):
            em.op("v", lambda e: e.reciprocal(s1row[0:1, :],
                                              PS[3][0:1, 0:TILE]),
                  reads=["ps3"], writes=["s1row"], tag="recip", wout=512)
            mm(PS[3][:, 0:TILE], "ps3", onesr_f[:], s1row[0:1, :], True, True,
               ["s1row", "onesr_f"], "rbc")
            ev(bca[:], PS[3][:, 0:TILE], ["ps3"], ["bca"], "rbcev", wout=512)
            for lc in range(4):
                src_, srcR = (PS[lc], f"ps{lc}") if lc < 3 else (PS[7], "ps7")
                ev(ylat_b[:, lc * TILE:(lc + 1) * TILE], src_[:, 0:TILE],
                   [srcR], ["ylat_b"], "ylatev", wout=512)
            for lc in range(4):
                mm(PS[3][0:64, 0:TILE], "ps3",
                   VEb[:, lc * D + h * 64: lc * D + h * 64 + 64],
                   ylat_b[:, lc * TILE:(lc + 1) * TILE],
                   lc == 0, lc == 3, ["VEb", "ylat_b"], "yfmm")
            r0 = 64 * (h % 2)
            em.op("v", lambda e, h=h, r0=r0: e.tensor_tensor(
                yT[r0:r0 + 64, (h // 2) * TILE:(h // 2 + 1) * TILE],
                PS[3][0:64, 0:TILE], bca[0:64, :], ALU.mult),
                reads=["ps3", "bca"], writes=["yT"], tag="yev", wout=512)

        for h in range(NH):
            emit_qlat(h)
            if h > 0:
                emit_tail(h - 1)
            emit_ch(h)
            emit_kloop(h)
        emit_tail(NH - 1)

    def p1_epilog(j):
        cur = xt_f[j % 2]
        curR = f"xt_f{j % 2}"
        for dc in range(8):
            em.op("v", lambda e, dc=dc, cur=cur: e.tensor_add(
                yT[:, dc * TILE:(dc + 1) * TILE],
                yT[:, dc * TILE:(dc + 1) * TILE],
                cur[:, dc * TILE:(dc + 1) * TILE]),
                reads=["yT", curR], writes=["yT"], tag="resid", wout=512)
        for dc in range(8):
            mm(PS[4][0:1, 0:TILE], "ps4", ones1f[:],
               yT[:, dc * TILE:(dc + 1) * TILE],
               dc == 0, dc == 7, ["yT", "ones1f"], "s1mm")
        for dc in range(8):
            em.op("v", lambda e, dc=dc: e.tensor_mul(
                sq_t[:], yT[:, dc * TILE:(dc + 1) * TILE],
                yT[:, dc * TILE:(dc + 1) * TILE]),
                reads=["yT"], writes=["sq_t"], tag="sq", wout=512)
            mm(PS[5][0:1, 0:TILE], "ps5", ones1f[:], sq_t[:],
               dc == 0, dc == 7, ["sq_t", "ones1f"], "s2mm")
        em.op("v", lambda e: e.tensor_scalar(
            s1row[0:1, :], PS[4][0:1, 0:TILE], 1.0 / D, None, ALU.mult),
            reads=["ps4"], writes=["s1row"], tag="mu", wout=512)
        em.op("v", lambda e: e.tensor_scalar(
            srow[0:1, 0:TILE], PS[5][0:1, 0:TILE], 1.0 / D, None, ALU.mult),
            reads=["ps5"], writes=["srow"], tag="ex2", wout=512)
        em.op("v", lambda e: e.tensor_mul(
            srow[0:1, TILE:2 * TILE], s1row[0:1, :], s1row[0:1, :]),
            reads=["s1row"], writes=["srow"], tag="mu2", wout=512)
        em.op("v", lambda e: e.tensor_sub(
            srow[0:1, 0:TILE], srow[0:1, 0:TILE], srow[0:1, TILE:2 * TILE]),
            reads=["srow"], writes=["srow"], tag="var", wout=512)
        em.op("v", lambda e: e.tensor_scalar(
            srow[0:1, 0:TILE], srow[0:1, 0:TILE], EPS, None, ALU.add),
            reads=["srow"], writes=["srow"], tag="vareps", wout=512)
        em.op("s", lambda e: e.activation(
            srow[0:1, TILE:2 * TILE], srow[0:1, 0:TILE], AF.Ln),
            reads=["srow"], writes=["srow"], tag="lnv", wout=512)
        em.op("s", lambda e: e.activation(
            srow[0:1, 0:TILE], srow[0:1, TILE:2 * TILE], AF.Exp, scale=-0.5),
            reads=["srow"], writes=["srow"], tag="rstd", wout=512)
        mm(PS[4][:, 0:TILE], "ps4", onesr_f[:], s1row[0:1, :], True, True,
           ["s1row", "onesr_f"], "mubc")
        ev(bca[:], PS[4][:, 0:TILE], ["ps4"], ["bca"], "mubcev", wout=512)
        mm(PS[5][:, 0:TILE], "ps5", onesr_f[:], srow[0:1, 0:TILE], True, True,
           ["srow", "onesr_f"], "rsbc")
        ev(bcb[:], PS[5][:, 0:TILE], ["ps5"], ["bcb"], "rsbcev", wout=512)
        for dc in range(8):
            em.op("v", lambda e, dc=dc: e.tensor_sub(
                yT[:, dc * TILE:(dc + 1) * TILE],
                yT[:, dc * TILE:(dc + 1) * TILE], bca[:]),
                reads=["yT", "bca"], writes=["yT"], tag="lnsub", wout=512)
            em.op("v", lambda e, dc=dc: e.tensor_mul(
                yT[:, dc * TILE:(dc + 1) * TILE],
                yT[:, dc * TILE:(dc + 1) * TILE], bcb[:]),
                reads=["yT", "bcb"], writes=["yT"], tag="lnmul", wout=512)
            em.op("v", lambda e, dc=dc: e.tensor_copy(
                x1b[:, dc * TILE:(dc + 1) * TILE],
                yT[:, dc * TILE:(dc + 1) * TILE]),
                reads=["yT"], writes=["x1b"], tag="x1cast", wout=512)
        em.dma_burst("sp", [
            (lambda e, j=j, dc=dc: e.dma_start(
                out=x1T_f32[j, dc], in_=yT[:, dc * TILE:(dc + 1) * TILE]),
             ("yT",)) for dc in range(8)],
            writes=[f"h:x1f:{j}"], slot="yT", tag="x1fst")
        em.dma_burst("sp", [
            (lambda e, j=j, dc=dc: e.dma_start(
                out=x1T_b16[j, dc], in_=x1b[:, dc * TILE:(dc + 1) * TILE]),
             ("x1b",)) for dc in range(8)],
            writes=[f"h:x1b:{j}"], slot="x1b", tag="x1bst")
        for tc in range(4):
            tsl = tc % 2
            for dc in range(8):
                em.op("t", lambda e, tc=tc, dc=dc: e.transpose(
                    PS[6][:, 0:128],
                    yT[:, dc * TILE + tc * 128: dc * TILE + tc * 128 + 128],
                    ident[:]),
                    reads=["yT", "ident"], writes=["ps6"], tag="x1tr")
                em.op("v", lambda e, tsl=tsl, dc=dc: e.tensor_copy(
                    x1tok_sb[:, tsl * D + dc * 128: tsl * D + dc * 128 + 128],
                    PS[6][:, 0:128]),
                    reads=["ps6"], writes=[f"x1tok_sb{tsl}"], tag="x1trev", wout=512)
            em.dma("sp", lambda e, j=j, tc=tc, tsl=tsl: e.dma_start(
                out=x1tokh[j, tc], in_=x1tok_sb[:, tsl * D:(tsl + 1) * D]),
                reads=[f"x1tok_sb{tsl}"], writes=[f"h:x1tok:{j}:{tc}"],
                slot=f"x1tok_sb{tsl}", tag="x1tokst")
        for dc in range(8):
            mm(PS[4][0:16, 0:TILE], "ps4", gwT_b[:, dc * E:(dc + 1) * E],
               x1b[:, dc * TILE:(dc + 1) * TILE],
               dc == 0, dc == 7, ["gwT_b", "x1b"], "logit")
        em.op("s", lambda e: e.activation(smt[:, :], PS[4][0:16, 0:TILE],
                                          AF.Exp),
              reads=["ps4"], writes=["smt"], tag="gexp", wout=512)
        mm(PS[5][0:1, 0:TILE], "ps5", ones1f[0:16, :], smt[:, :], True, True,
           ["smt", "ones1f"], "gsum")
        em.op("v", lambda e: e.reciprocal(s1row[0:1, :], PS[5][0:1, 0:TILE]),
              reads=["ps5"], writes=["s1row"], tag="grecip", wout=512)
        mm(PS[6][0:16, 0:TILE], "ps6", onesr_f[:, 0:16], s1row[0:1, :],
           True, True, ["s1row", "onesr_f"], "gbc")
        em.op("v", lambda e, j=j: e.tensor_tensor(
            scn[:, :], smt[:, :], PS[6][0:16, 0:TILE], ALU.mult),
            reads=["smt", "ps6"], writes=["scn"], tag="gmul", wout=512)
        em.dma("sp", lambda e, j=j: e.dma_start(
            out=sc_inh.ap()[:, j * TILE:(j + 1) * TILE], in_=scn[:, :]),
            reads=["scn"], writes=["h:sc_in"], slot="scn", tag="scst")
        em.dma("sp", lambda e, j=j: e.dma_start(
            out=sco[16 * j:16 * j + 16, :], in_=scn[:, :]),
            reads=["scn"], writes=["sco"], slot="scn", tag="sccp")

    p1_xload(0)
    p1_xcast(0)
    p1_ckv(0)
    p1_qchain(0)
    p1_bounds(0)
    for j in range(NT):
        if j + 1 < NT:
            p1_xload(j + 1)
        p1_heads(j)
        if j + 1 < NT:
            p1_xcast(j + 1)
            p1_ckv(j + 1)
            p1_qchain(j + 1)
            p1_bounds(j + 1)
        p1_epilog(j)

    em.cc(lambda e: e.collective_compute(
        "AllGather", ALU.bypass, replica_groups=[list(range(NCORE))],
        ins=[sc_inh.ap().opt()], outs=[sc_outh.ap().opt()]),
        reads=["h:sc_in"], writes=["h:sc_out"], tag="ag")

    astack.close()
    em.new_block()

    lgstack = contextlib.ExitStack()
    lsb = mksb(lgstack)
    rt = lsb("rt", (128, 6 * TILE), F32)
    r1tok = lsb("r1tok", (128, 4 * 128), F32)
    wtok = lsb("wtok", (128, 4 * 128), BF16)
    r2v = lsb("r2v", (128, 128), F32)
    r2tok = lsb("r2tok", (128, 128), F32)
    th = lsb("th", (128, 8), F32)
    th16 = lsb("th16", (16, 4), F32)
    g2 = lsb("g2", (16, 128), F32)
    g16 = lsb("g16", (128, 16), F32)
    trij = lsb("trij", (128, 128), F32)
    em.dma_burst("sp", [
        (lambda e, d=d_, s=s_: e.dma_start(out=d[:], in_=s.ap()), ())
        for d_, s_ in ((g2, c_g2), (g16, c_g16), (trij, c_trij))],
        writes=["g2", "g16", "trij"], slot="p3c", policy="total", tag="cload3")

    b1stack = contextlib.ExitStack()
    b1sb = mksb(b1stack)
    sc_all = b1sb("sc_all", (128, T), F32)
    bis = b1sb("bis", (128, T), BF16)
    sgw = b1sb("sgw", (128, 2 * 1024), BF16)
    suw = b1sb("suw", (128, 2 * 1024), BF16)
    sdw = b1sb("sdw", (128, 2 * 1024), BF16)
    x1bt = [b1sb(f"x1bt{i}", (128, 8 * TILE), BF16) for i in range(2)]
    hsT = b1sb("hsT", (128, NISC * TILE), BF16)
    gg = b1sb("gg", (128, TILE), BF16)
    shev = b1sb("shev", (128, 2 * TILE), F32)

    em.dma("sp", lambda e: e.dma_start(out=sc_all[:], in_=sc_outh.ap()),
           reads=["h:sc_out"], writes=["sc_all"], slot="sc_all", tag="scld")

    p3_thunks = []
    em.op("v", lambda e: e.memset(th[:, 0:1], 0.0), writes=["th"], tag="bis0")
    em.op("v", lambda e: e.memset(th[:, 1:2], 1.0), writes=["th"], tag="bis1")
    def p3_iter(it):
        em.op("v", lambda e: e.tensor_add(th[:, 2:3], th[:, 0:1], th[:, 1:2]),
              reads=["th"], writes=["th"], tag="bmid1")
        em.op("v", lambda e: e.tensor_scalar(
            th[:, 2:3], th[:, 2:3], 0.5, None, ALU.mult),
            reads=["th"], writes=["th"], tag="bmid2")
        em.op("v", lambda e: e.tensor_scalar(
            bis[:], sc_all[:], th[:, 2:3], 0.0, ALU.is_ge, ALU.add,
            accum_out=th[:, 3:4]),
            reads=["sc_all", "th"], writes=["bis", "th"], tag="bcmp", wout=512)
        mm(PS[0][0:16, 0:1], "ps0", g16[:], th[:, 3:4], True, True,
           ["th", "g16"], "bcnt")
        em.op("v", lambda e: e.tensor_scalar(
            th16[:, 0:1], PS[0][0:16, 0:1], float(CGLOB), None, ALU.is_ge),
            reads=["ps0"], writes=["th16"], tag="bdir")
        mm(PS[1][:, 0:1], "ps1", g2[:], th16[:, 0:1], True, True,
           ["th16", "g2"], "bdirbc")
        ev(th[:, 4:5], PS[1][:, 0:1], ["ps1"], ["th"], "bdirev")
        em.op("v", lambda e: e.tensor_sub(th[:, 5:6], th[:, 2:3], th[:, 0:1]),
              reads=["th"], writes=["th"], tag="blo1")
        em.op("v", lambda e: e.tensor_mul(th[:, 5:6], th[:, 5:6], th[:, 4:5]),
              reads=["th"], writes=["th"], tag="blo2")
        em.op("v", lambda e: e.tensor_add(th[:, 0:1], th[:, 0:1], th[:, 5:6]),
              reads=["th"], writes=["th"], tag="blo3")
        em.op("v", lambda e: e.tensor_sub(th[:, 6:7], th[:, 1:2], th[:, 2:3]),
              reads=["th"], writes=["th"], tag="bhi1")
        em.op("v", lambda e: e.tensor_mul(th[:, 6:7], th[:, 6:7], th[:, 4:5]),
              reads=["th"], writes=["th"], tag="bhi2")
        em.op("v", lambda e: e.tensor_add(th[:, 1:2], th[:, 2:3], th[:, 6:7]),
              reads=["th"], writes=["th"], tag="bhi3")

    for _it in range(44):
        p3_thunks.append(lambda _it=_it: p3_iter(_it))

    def p3_rank_all():
        em.op("v", lambda e: e.tensor_scalar(
            rt[:, 1024:1536], sco[:], th[:, 0:1], None, ALU.is_ge),
            reads=["sco", "th"], writes=["rt"], tag="mask", wout=512)
        em.op("v", lambda e: e.tensor_mul(
            rt[:, 512:1024], sco[:], rt[:, 1024:1536]),
            reads=["sco", "rt"], writes=["rt"], tag="wmask", wout=512)
        em.op("v", lambda e: e.memset(rt[:, 2048:2560], 1.0),
              writes=["rt"], tag="sone")
        em.op("v", lambda e: e.tensor_tensor_scan(
            rt[:, 0:512], rt[:, 2048:2560], rt[:, 1024:1536], 0.0,
            ALU.mult, ALU.add),
            reads=["rt"], writes=["rt"], tag="scan", wout=512)
        em.op("v", lambda e: e.tensor_sub(rt[:, 0:512], rt[:, 0:512],
                                          rt[:, 1024:1536]),
              reads=["rt"], writes=["rt"], tag="rexcl", wout=512)
        em.op("v", lambda e: e.tensor_scalar(
            rt[:, 1536:2048], rt[:, 0:512], float(K1), None, ALU.is_lt),
            reads=["rt"], writes=["rt"], tag="rlt", wout=512)
        em.op("v", lambda e: e.tensor_mul(
            rt[:, 1536:2048], rt[:, 1536:2048], rt[:, 1024:1536]),
            reads=["rt"], writes=["rt"], tag="rvalid", wout=512)
        em.op("v", lambda e: e.tensor_mul(rt[:, 0:512], rt[:, 0:512],
                                          rt[:, 1536:2048]),
              reads=["rt"], writes=["rt"], tag="rv1", wout=512)
        em.op("v", lambda e: e.tensor_scalar(
            rt[:, 2048:2560], rt[:, 1536:2048], -1.0, None, ALU.add),
            reads=["rt"], writes=["rt"], tag="rv2", wout=512)
        em.op("v", lambda e: e.tensor_add(rt[:, 0:512], rt[:, 0:512],
                                          rt[:, 2048:2560]),
              reads=["rt"], writes=["rt"], tag="rank1v", wout=512)
        em.op("v", lambda e: e.tensor_reduce(
            th[:, 5:6], rt[:, 1536:2048], AX.X, ALU.add),
            reads=["rt"], writes=["th"], tag="tilecnt")
        mm(PS[0][:, 0:1], "ps0", trij[:], th[:, 5:6], True, True,
           ["th", "trij"], "base2")
        ev(th[:, 6:7], PS[0][:, 0:1], ["ps0"], ["th"], "base2ev")
        em.op("v", lambda e: e.tensor_scalar(
            r2v[:], iota_row[:], th[:, 6:7], None, ALU.add),
            reads=["th", "iota_row"], writes=["r2v"], tag="r2a")
        em.op("v", lambda e: e.tensor_scalar(
            rt[:, 1536:1664], iota_row[:, 0:128], th[:, 5:6], None, ALU.is_lt),
            reads=["th", "iota_row"], writes=["rt"], tag="r2m1")
        em.op("v", lambda e: e.tensor_scalar(
            rt[:, 1664:1792], r2v[:], float(K2), None, ALU.is_lt),
            reads=["r2v"], writes=["rt"], tag="r2m2")
        em.op("v", lambda e: e.tensor_mul(
            rt[:, 1536:1664], rt[:, 1536:1664], rt[:, 1664:1792]),
            reads=["rt"], writes=["rt"], tag="r2m3")
        em.op("v", lambda e: e.tensor_mul(r2v[:], r2v[:], rt[:, 1536:1664]),
              reads=["rt", "r2v"], writes=["r2v"], tag="r2m4")
        em.op("v", lambda e: e.tensor_scalar(
            rt[:, 1664:1792], rt[:, 1536:1664], -1.0, None, ALU.add),
            reads=["rt"], writes=["rt"], tag="r2m5")
        em.op("v", lambda e: e.tensor_add(r2v[:], r2v[:], rt[:, 1664:1792]),
              reads=["rt", "r2v"], writes=["r2v"], tag="rank2v")
        for tsub in range(4):
            em.op("t", lambda e, tsub=tsub: e.transpose(
                PS[2][:, 0:128], rt[:, tsub * 128:(tsub + 1) * 128], ident[:]),
                reads=["rt", "ident"], writes=["ps2"], tag="r1tr")
            ev(r1tok[:, tsub * 128:(tsub + 1) * 128], PS[2][:, 0:128],
               ["ps2"], ["r1tok"], "r1trev", wout=512)
            em.op("t", lambda e, tsub=tsub: e.transpose(
                PS[2][:, 0:128], rt[:, 512 + tsub * 128: 512 + (tsub + 1) * 128],
                ident[:]),
                reads=["rt", "ident"], writes=["ps2"], tag="wtr")
            ev(wtok[:, tsub * 128:(tsub + 1) * 128], PS[2][:, 0:128],
               ["ps2"], ["wtok"], "wtrev", wout=512)
        em.op("t", lambda e: e.transpose(PS[2][:, 0:128], r2v[:], ident[:]),
              reads=["r2v", "ident"], writes=["ps2"], tag="r2tr")
        ev(r2tok[:], PS[2][:, 0:128], ["ps2"], ["r2tok"], "r2trev", wout=512)

    p3_thunks.append(p3_rank_all)

    for j in range(NT):
        xsl = x1bt[j % 2]
        xR = f"x1bt{j % 2}"
        em.dma_burst("sp", [
            (lambda e, j=j, dc=dc, xsl=xsl: e.dma_start(
                out=xsl[:, dc * TILE:(dc + 1) * TILE], in_=x1T_b16[j, dc]),
             (f"h:x1b:{j}",)) for dc in range(8)],
            writes=[xR], slot=xR, tag="x1bld")
        for isc in range(NISC):
            sl = isc % 2
            sgR, suR = f"sgw{sl}", f"suw{sl}"
            em.dma("g", lambda e, isc=isc, sl=sl: e.dma_start(
                out=sgw[:, sl * 1024:(sl + 1) * 1024], in_=sg_dev[isc]),
                writes=[sgR], slot=sgR, tag="sgld")
            em.dma("g", lambda e, isc=isc, sl=sl: e.dma_start(
                out=suw[:, sl * 1024:(sl + 1) * 1024], in_=su_dev[isc]),
                writes=[suR], slot=suR, tag="suld")
            for dc in range(8):
                mm(PS[3][:, 0:TILE], "ps3",
                   sgw[:, sl * 1024 + dc * 128: sl * 1024 + dc * 128 + 128],
                   xsl[:, dc * TILE:(dc + 1) * TILE],
                   dc == 0, dc == 7, [sgR, xR], "sgmm")
            for dc in range(8):
                mm(PS[4][:, 0:TILE], "ps4",
                   suw[:, sl * 1024 + dc * 128: sl * 1024 + dc * 128 + 128],
                   xsl[:, dc * TILE:(dc + 1) * TILE],
                   dc == 0, dc == 7, [suR, xR], "summ")
            em.op("s", lambda e: e.activation(gg[:], PS[3][:, 0:TILE], AF.Gelu),
                  reads=["ps3"], writes=["gg"], tag="sgelu", wout=512)
            em.op("v", lambda e, isc=isc: e.tensor_tensor(
                hsT[:, isc * TILE:(isc + 1) * TILE], gg[:], PS[4][:, 0:TILE],
                ALU.mult),
                reads=["gg", "ps4"], writes=["hsT"], tag="smul", wout=512)
            if p3_thunks:
                p3_thunks.pop(0)()
        for isc in range(NISC):
            sl = isc % 2
            sdR = f"sdw{sl}"
            em.dma("g", lambda e, isc=isc, sl=sl: e.dma_start(
                out=sdw[:, sl * 1024:(sl + 1) * 1024], in_=sd_dev[isc]),
                writes=[sdR], slot=sdR, tag="sdld")
            for dc in range(8):
                mm(PS[dc][:, 0:TILE], f"ps{dc}",
                   sdw[:, sl * 1024 + dc * 128: sl * 1024 + dc * 128 + 128],
                   hsT[:, isc * TILE:(isc + 1) * TILE],
                   isc == 0, isc == NISC - 1, [sdR, "hsT"], "sdmm")
        for dc in range(8):
            ssl = dc % 2
            ev(shev[:, ssl * TILE:(ssl + 1) * TILE], PS[dc][:, 0:TILE],
               [f"ps{dc}"], [f"shev{ssl}"], "shevc", wout=512)
            em.dma("sp", lambda e, j=j, dc=dc, ssl=ssl: e.dma_start(
                out=shTh[j, dc], in_=shev[:, ssl * TILE:(ssl + 1) * TILE]),
                reads=[f"shev{ssl}"], writes=[f"h:shT:{j}"], slot=f"shev{ssl}",
                tag="shst")

    while p3_thunks:
        p3_thunks.pop(0)()

    b1stack.close()
    em.new_block()

    cstack = contextlib.ExitStack()
    csb = mksb(cstack)
    iota768 = csb("iota768", (128, K2), F32)
    em.dma("sp", lambda e: e.dma_start(out=iota768[:], in_=c_iota768.ap()),
           writes=["iota768"], slot="p5c", policy="total", tag="cload5")
    x1t_sb = [csb(f"x1t_sb{i}", (128, 4 * D), BF16) for i in range(2)]
    r1b = csb("r1b", (128, 4 * 128), BF16)
    r2b = csb("r2b", (128, NT * K2), BF16)
    q2scr = csb("q2scr", (128, 128), F32)
    q2b6 = csb("q2b6", (128, (K2 // 128) * 128), BF16)
    zb = csb("zb", (128, NT * D), BF16)
    zwb = csb("zwb", (128, NT), BF16)
    xgT = csb("xgT", (128, 8 * K2), BF16)
    wgv = [csb(f"wgv{i}", (128, 1024), BF16) for i in range(2)]
    wuv_ = [csb(f"wuv_{i}", (128, 1024), BF16) for i in range(2)]
    wdv = [csb(f"wdv{i}", (128, 512), BF16) for i in range(4)]
    hT = csb("hT", (128, NIC * K2), BF16)
    ggx = csb("ggx", (128, K2), BF16)
    eoT = csb("eoT", (128, (K2 // 128) * D), BF16)
    wgc = csb("wgc", (128, 8), F32)
    wrow = csb("wrow", (1, K2), F32)
    s2ev = csb("s2ev", (128, 2 * D), BF16)

    for ei in range(E):
        for j in range(NT):
            p = 16 * j + ei
            xs = x1t_sb[j % 2]
            xsR = f"x1t_sb{j % 2}"
            em.dma_burst("sp", [
                (lambda e, j=j, tc=tc, xs=xs: e.dma_start(
                    out=xs[:, tc * D:(tc + 1) * D], in_=x1tokh[j, tc]),
                 (f"h:x1tok:{j}:{tc}",)) for tc in range(4)],
                writes=[xsR], slot=xsR, tag="x1tld")
            for tsub in range(4):
                em.op("v", lambda e, p=p, tsub=tsub: e.tensor_scalar(
                    r1b[:, tsub * 128:(tsub + 1) * 128], iota_row[:],
                    r1tok[:, tsub * 128 + p: tsub * 128 + p + 1], None,
                    ALU.is_equal),
                    reads=["r1tok", "iota_row"], writes=["r1b"], tag="r1bld", wout=512)
            for dh in range(2):
                pb, pbR = (PS[0], "ps0") if dh == 0 else (PS[1], "ps1")
                for tsub in range(4):
                    mm(pb[:, 0:512], pbR,
                       r1b[:, tsub * 128:(tsub + 1) * 128],
                       xs[:, tsub * D + dh * 512: tsub * D + dh * 512 + 512],
                       tsub == 0, tsub == 3, ["r1b", xsR], "zmm")
                em.op("v", lambda e, j=j, dh=dh, pb=pb: e.tensor_copy(
                    zb[:, j * D + dh * 512: j * D + dh * 512 + 512],
                    pb[:, 0:512]),
                    reads=[pbR], writes=["zb"], tag="zev", wout=512)
            for tsub in range(4):
                mm(PS[2][:, 0:1], "ps2",
                   r1b[:, tsub * 128:(tsub + 1) * 128],
                   wtok[:, tsub * 128 + p: tsub * 128 + p + 1],
                   tsub == 0, tsub == 3, ["r1b", "wtok"], "zwmm")
            em.op("v", lambda e, j=j: e.tensor_copy(
                zwb[:, j: j + 1], PS[2][:, 0:1]),
                reads=["ps2"], writes=["zwb"], tag="zwev")
            em.op("v", lambda e, p=p, j=j: e.tensor_scalar(
                r2b[:, j * K2:(j + 1) * K2], iota768[:],
                r2tok[:, p: p + 1], None, ALU.is_equal),
                reads=["r2tok", "iota768"], writes=["r2b"], tag="r2bld", wout=512)
        for dc in range(8):
            pa, paR = (PS[3], "ps3") if dc % 2 == 0 else (PS[5], "ps5")
            pb2, pbR2 = (PS[4], "ps4") if dc % 2 == 0 else (PS[6], "ps6")
            for j in range(NT):
                mm(pa[:, 0:512], paR,
                   zb[:, j * D + dc * 128: j * D + dc * 128 + 128],
                   r2b[:, j * K2: j * K2 + 512],
                   j == 0, j == NT - 1, ["zb", "r2b"], "xgmmA")
                mm(pb2[:, 0:K2 - 512], pbR2,
                   zb[:, j * D + dc * 128: j * D + dc * 128 + 128],
                   r2b[:, j * K2 + 512: j * K2 + K2],
                   j == 0, j == NT - 1, ["zb", "r2b"], "xgmmB")
            em.op("v", lambda e, dc=dc, pa=pa: e.tensor_copy(
                xgT[:, dc * K2: dc * K2 + 512], pa[:, 0:512]),
                reads=[paR], writes=["xgT"], tag="xgevA", wout=512)
            em.op("v", lambda e, dc=dc, pb2=pb2: e.tensor_copy(
                xgT[:, dc * K2 + 512: dc * K2 + K2], pb2[:, 0:K2 - 512]),
                reads=[pbR2], writes=["xgT"], tag="xgevB", wout=512)
        for j in range(NT):
            mm(PS[7][0:1, 0:512], "ps7", zwb[:, j: j + 1],
               r2b[:, j * K2: j * K2 + 512],
               j == 0, j == NT - 1, ["zwb", "r2b"], "wgmA")
            mm(PS[2][0:1, 0:K2 - 512], "ps2", zwb[:, j: j + 1],
               r2b[:, j * K2 + 512: j * K2 + K2],
               j == 0, j == NT - 1, ["zwb", "r2b"], "wgmB")
        em.op("v", lambda e: e.tensor_copy(wrow[0:1, 0:512], PS[7][0:1, 0:512]),
              reads=["ps7"], writes=["wrow"], tag="wrevA", wout=512)
        em.op("v", lambda e: e.tensor_copy(wrow[0:1, 512:K2],
                                           PS[2][0:1, 0:K2 - 512]),
              reads=["ps2"], writes=["wrow"], tag="wrevB", wout=512)
        for cc in range(K2 // 128):
            mm(PS[3][:, 0:1], "ps3", wrow[0:1, cc * 128:(cc + 1) * 128],
               ones1f[0:1, 0:1], True, True, ["wrow", "ones1f"], "wgtr")
            em.op("v", lambda e, cc=cc: e.tensor_copy(
                wgc[:, cc: cc + 1], PS[3][:, 0:1]),
                reads=["ps3"], writes=["wgc"], tag="wgtrev")
        for ic in range(NIC):
            sl = ic % 2
            wgR, wuR = f"wgv{sl}", f"wuv_{sl}"
            em.dma("g", lambda e, ei=ei, ic=ic, sl=sl: e.dma_start(
                out=wgv[sl][:], in_=wg_dev[ei, ic]),
                writes=[wgR], slot=wgR, tag="wgld")
            em.dma("g", lambda e, ei=ei, ic=ic, sl=sl: e.dma_start(
                out=wuv_[sl][:], in_=wu_dev[ei, ic]),
                writes=[wuR], slot=wuR, tag="wuld")
            pg, pgR = (PS[0], "ps0") if sl == 0 else (PS[4], "ps4")
            pg2, pg2R = (PS[1], "ps1") if sl == 0 else (PS[5], "ps5")
            pu, puR = (PS[2], "ps2") if sl == 0 else (PS[6], "ps6")
            pu2, pu2R = (PS[3], "ps3") if sl == 0 else (PS[7], "ps7")
            for dc in range(8):
                mm(pg[:, 0:512], pgR, wgv[sl][:, dc * 128:(dc + 1) * 128],
                   xgT[:, dc * K2: dc * K2 + 512],
                   dc == 0, dc == 7, [wgR, "xgT"], "upgA")
                mm(pg2[:, 0:K2 - 512], pg2R, wgv[sl][:, dc * 128:(dc + 1) * 128],
                   xgT[:, dc * K2 + 512: dc * K2 + K2],
                   dc == 0, dc == 7, [wgR, "xgT"], "upgB")
            for dc in range(8):
                mm(pu[:, 0:512], puR, wuv_[sl][:, dc * 128:(dc + 1) * 128],
                   xgT[:, dc * K2: dc * K2 + 512],
                   dc == 0, dc == 7, [wuR, "xgT"], "upuA")
                mm(pu2[:, 0:K2 - 512], pu2R, wuv_[sl][:, dc * 128:(dc + 1) * 128],
                   xgT[:, dc * K2 + 512: dc * K2 + K2],
                   dc == 0, dc == 7, [wuR, "xgT"], "upuB")
            em.op("s", lambda e, pg=pg: e.activation(
                ggx[:, 0:512], pg[:, 0:512], AF.Gelu),
                reads=[pgR], writes=["ggx"], tag="mgeluA", wout=512)
            em.op("s", lambda e, pg2=pg2: e.activation(
                ggx[:, 512:K2], pg2[:, 0:K2 - 512], AF.Gelu),
                reads=[pg2R], writes=["ggx"], tag="mgeluB", wout=512)
            em.op("v", lambda e, ic=ic, pu=pu: e.tensor_tensor(
                hT[:, ic * K2: ic * K2 + 512], ggx[:, 0:512], pu[:, 0:512],
                ALU.mult),
                reads=["ggx", puR], writes=["hT"], tag="hmulA", wout=512)
            em.op("v", lambda e, ic=ic, pu2=pu2: e.tensor_tensor(
                hT[:, ic * K2 + 512: ic * K2 + K2], ggx[:, 512:K2],
                pu2[:, 0:K2 - 512], ALU.mult),
                reads=["ggx", pu2R], writes=["hT"], tag="hmulB", wout=512)
        for dh in range(2):
            for ic in range(NIC):
                sl = ic % 4
                wdR = f"wdv{sl}"
                em.dma("g", lambda e, ei=ei, dh=dh, ic=ic, sl=sl: e.dma_start(
                    out=wdv[sl][:], in_=wd_dev[ei, dh, ic]),
                    writes=[wdR], slot=wdR, tag="wdld")
                for cc in range(K2 // 128):
                    mm(PS[cc][:, 0:512], f"ps{cc}",
                       hT[:, ic * K2 + cc * 128: ic * K2 + cc * 128 + 128],
                       wdv[sl][:],
                       ic == 0, ic == NIC - 1, ["hT", wdR], "dnmm")
            for cc in range(K2 // 128):
                em.op("v", lambda e, cc=cc, dh=dh: e.tensor_scalar(
                    eoT[:, cc * D + dh * 512: cc * D + dh * 512 + 512],
                    PS[cc][:, 0:512], wgc[:, cc: cc + 1], None, ALU.mult),
                    reads=[f"ps{cc}", "wgc"], writes=["eoT"], tag="eoev", wout=512)
        for j in range(NT):
            p = 16 * j + ei
            for cc in range(K2 // 128):
                em.op("v", lambda e, p=p, cc=cc: e.tensor_scalar(
                    q2scr[:], iota_row[:], float(cc * 128),
                    r2tok[:, p: p + 1], ALU.add, ALU.is_equal),
                    reads=["r2tok", "iota_row"], writes=["q2scr"], tag="q2eq", wout=512)
                em.op("t", lambda e: e.transpose(
                    PS[2][:, 0:128], q2scr[:], ident[:]),
                    reads=["q2scr", "ident"], writes=["ps2"], tag="q2tr")
                em.op("v", lambda e, cc=cc: e.tensor_copy(
                    q2b6[:, cc * 128:(cc + 1) * 128], PS[2][:, 0:128]),
                    reads=["ps2"], writes=["q2b6"], tag="q2ev", wout=512)
            ssl = j % 2
            for dh in range(2):
                pb, pbR = (PS[0], "ps0") if dh == 0 else (PS[1], "ps1")
                for cc in range(K2 // 128):
                    mm(pb[:, 0:512], pbR,
                       q2b6[:, cc * 128:(cc + 1) * 128],
                       eoT[:, cc * D + dh * 512: cc * D + dh * 512 + 512],
                       cc == 0, cc == K2 // 128 - 1, ["q2b6", "eoT"], "s2mm")
                em.op("v", lambda e, dh=dh, pb=pb, ssl=ssl: e.tensor_copy(
                    s2ev[:, ssl * D + dh * 512: ssl * D + (dh + 1) * 512],
                    pb[:, 0:512]),
                    reads=[pbR], writes=[f"s2ev{ssl}"], tag="s2evc", wout=512)
            em.dma("sp", lambda e, ei=ei, j=j, ssl=ssl: e.dma_start(
                out=s2h[ei, j], in_=s2ev[:, ssl * D:(ssl + 1) * D]),
                reads=[f"s2ev{ssl}"], writes=[f"h:s2:{ei}:{j}"],
                slot=f"s2ev{ssl}", tag="s2st")

    cstack.close()
    em.new_block()

    dstack = contextlib.ExitStack()
    dsb = mksb(dstack)
    s2sl = [dsb(f"s2sl{i}", (128, D), BF16) for i in range(4)]
    q1all = dsb("q1all", (128, E * TILE), BF16)
    x1ld = dsb("x1ld", (128, 8 * TILE), F32)
    shld = dsb("shld", (128, 8 * TILE), F32)
    x2p = dsb("x2p", (128, 8 * TILE), F32)
    x2tok = dsb("x2tok", (128, 4 * D), F32)
    sqscr = dsb("sqscr", (128, D), F32)
    sttb = dsb("sttb", (128, 4), F32)
    bcq = dsb("bcq", (128, TILE), F32)

    for j in range(NT):
        em.dma_burst("sp", [
            (lambda e, j=j, dc=dc: e.dma_start(
                out=x1ld[:, dc * TILE:(dc + 1) * TILE], in_=x1T_f32[j, dc]),
             (f"h:x1f:{j}",)) for dc in range(8)],
            writes=["x1ld"], slot="x1ld", tag="x1rld")
        em.dma_burst("sp", [
            (lambda e, j=j, dc=dc: e.dma_start(
                out=shld[:, dc * TILE:(dc + 1) * TILE], in_=shTh[j, dc]),
             (f"h:shT:{j}",)) for dc in range(8)],
            writes=["shld"], slot="shld", tag="shrld")
        for ei in range(E):
            p = 16 * j + ei
            for tsub in range(4):
                pb, pbR = (PS[0], "ps0") if tsub % 2 == 0 else (PS[1], "ps1")
                em.op("v", lambda e, p=p, tsub=tsub: e.tensor_scalar(
                    bcq[:, 0:128], iota_row[:],
                    r1tok[:, tsub * 128 + p: tsub * 128 + p + 1], None,
                    ALU.is_equal),
                    reads=["r1tok", "iota_row"], writes=["bcq"], tag="q1eq", wout=512)
                em.op("t", lambda e, pb=pb: e.transpose(
                    pb[:, 0:128], bcq[:, 0:128], ident[:]),
                    reads=["bcq", "ident"], writes=[pbR], tag="q1tr")
                em.op("v", lambda e, ei=ei, tsub=tsub, pb=pb: e.tensor_copy(
                    q1all[:, ei * TILE + tsub * 128:
                           ei * TILE + tsub * 128 + 128],
                    pb[:, 0:128]),
                    reads=[pbR], writes=["q1all"], tag="q1ev", wout=512)
        for ei in range(E):
            sl = ei % 4
            s2R = f"s2sl{sl}"
            em.dma("sp", lambda e, ei=ei, j=j, sl=sl: e.dma_start(
                out=s2sl[sl][:], in_=s2h[ei, j]),
                reads=[f"h:s2:{ei}:{j}"], writes=[s2R], slot=s2R, tag="s2ld")
            for dc in range(8):
                mm(PS[dc][:, 0:TILE], f"ps{dc}",
                   s2sl[sl][:, dc * 128:(dc + 1) * 128],
                   q1all[:, ei * TILE:(ei + 1) * TILE],
                   ei == 0, ei == E - 1, [s2R, "q1all"], "scmm")
        for dc in range(8):
            em.op("v", lambda e, dc=dc: e.scalar_tensor_tensor(
                x2p[:, dc * TILE:(dc + 1) * TILE], PS[dc][:, 0:TILE], 0.0,
                x1ld[:, dc * TILE:(dc + 1) * TILE], ALU.add, ALU.add),
                reads=[f"ps{dc}", "x1ld"], writes=["x2p"], tag="x2a", wout=512)
            em.op("v", lambda e, dc=dc: e.tensor_add(
                x2p[:, dc * TILE:(dc + 1) * TILE],
                x2p[:, dc * TILE:(dc + 1) * TILE],
                shld[:, dc * TILE:(dc + 1) * TILE]),
                reads=["x2p", "shld"], writes=["x2p"], tag="x2b", wout=512)
        for tc in range(4):
            for dc in range(8):
                pb, pbR = (PS[0], "ps0") if dc % 2 == 0 else (PS[1], "ps1")
                em.op("t", lambda e, tc=tc, dc=dc, pb=pb: e.transpose(
                    pb[:, 0:128],
                    x2p[:, dc * TILE + tc * 128: dc * TILE + tc * 128 + 128],
                    ident[:]),
                    reads=["x2p", "ident"], writes=[pbR], tag="x2tr")
                em.op("v", lambda e, tc=tc, dc=dc, pb=pb: e.tensor_copy(
                    x2tok[:, tc * D + dc * 128: tc * D + dc * 128 + 128],
                    pb[:, 0:128]),
                    reads=[pbR], writes=["x2tok"], tag="x2trev", wout=512)
        for tc in range(4):
            em.op("v", lambda e, tc=tc: e.tensor_reduce(
                sttb[:, 0:1], x2tok[:, tc * D:(tc + 1) * D], AX.X, ALU.add),
                reads=["x2tok"], writes=["sttb"], tag="mean1")
            em.op("v", lambda e: e.tensor_scalar(
                sttb[:, 0:1], sttb[:, 0:1], 1.0 / D, None, ALU.mult),
                reads=["sttb"], writes=["sttb"], tag="mean2")
            em.op("v", lambda e, tc=tc: e.tensor_scalar(
                x2tok[:, tc * D:(tc + 1) * D], x2tok[:, tc * D:(tc + 1) * D],
                sttb[:, 0:1], None, ALU.subtract),
                reads=["x2tok", "sttb"], writes=["x2tok"], tag="center", wout=512)
            em.op("s", lambda e, tc=tc: e.activation(
                sqscr[:], x2tok[:, tc * D:(tc + 1) * D], AF.Square,
                accum_out=sttb[:, 1:2]),
                reads=["x2tok"], writes=["sqscr", "sttb"], tag="sqacc", wout=512)
            em.op("v", lambda e: e.tensor_scalar(
                sttb[:, 1:2], sttb[:, 1:2], 1.0 / D, EPS, ALU.mult, ALU.add),
                reads=["sttb"], writes=["sttb"], tag="var2")
            em.op("s", lambda e: e.activation(
                sttb[:, 2:3], sttb[:, 1:2], AF.Ln),
                reads=["sttb"], writes=["sttb"], tag="lnv2")
            em.op("s", lambda e: e.activation(
                sttb[:, 2:3], sttb[:, 2:3], AF.Exp, scale=-0.5),
                reads=["sttb"], writes=["sttb"], tag="rstd2")
            em.op("v", lambda e, tc=tc: e.tensor_scalar(
                x2tok[:, tc * D:(tc + 1) * D], x2tok[:, tc * D:(tc + 1) * D],
                sttb[:, 2:3], None, ALU.mult),
                reads=["x2tok", "sttb"], writes=["x2tok"], tag="lnap2", wout=512)
            em.dma("sp", lambda e, j=j, tc=tc: e.dma_start(
                out=out[j * TILE + tc * 128: j * TILE + tc * 128 + 128, :],
                in_=x2tok[:, tc * D:(tc + 1) * D]),
                reads=["x2tok"], writes=[f"h:out:{j}:{tc}"], slot="x2tok",
                tag="outst")
    if DEV_DUMPS:
        em.dma("sp", lambda e: e.dma_start(
            out=d_x1.tensor.ap(), in_=x1T_f32.tensor.ap()),
            reads=[f"h:x1f:{j}" for j in range(NT)], writes=["h:d1"],
            slot="dev", policy="total", tag="dump")
        em.dma("sp", lambda e: e.dma_start(
            out=d_sh.tensor.ap(), in_=shTh.tensor.ap()),
            reads=[f"h:shT:{j}" for j in range(NT)], writes=["h:d2"],
            slot="dev", policy="total", tag="dump")
        em.dma("sp", lambda e: e.dma_start(
            out=d_sc.tensor.ap(), in_=sc_outh.ap()),
            reads=["h:sc_out"], writes=["h:d3"], slot="dev", policy="total",
            tag="dump")
        em.dma("sp", lambda e: e.dma_start(out=d_th.tensor.ap(), in_=th[:]),
               reads=["th"], writes=["h:d4"], slot="dev", policy="total",
               tag="dump")
        em.dma("sp", lambda e: e.dma_start(out=d_rt.tensor.ap(), in_=rt[:]),
               reads=["rt"], writes=["h:d5"], slot="dev", policy="total",
               tag="dump")
        em.dma("g", lambda e: e.dma_start(
            out=d_s2.tensor.ap(), in_=s2h.tensor.ap()),
            reads=[f"h:s2:{ei}:{j}" for ei in range(E) for j in range(NT)],
            writes=["h:d6"], slot="dev2", policy="total", tag="dump")
        em.op("sp", lambda e: e.nop(),
              reads=["h:d1", "h:d2", "h:d3", "h:d4", "h:d5"], tag="findev")
        em.op("g", lambda e: e.nop(), reads=["h:d6"], tag="findev2")
    em.op("sp", lambda e: e.nop(),
          reads=[f"h:out:{j}:{tc}" for j in range(NT) for tc in range(4)],
          tag="fin")

    dstack.close()
    em.emit()
    lgstack.close()
    gstack.close()
    return nc


_NC_CACHE = None


def _get_nc():
    global _NC_CACHE
    if _NC_CACHE is None:
        _NC_CACHE = build_program()
    return _NC_CACHE


def _host_layout(inputs):
    f32 = np.float32

    def c(a):
        return np.ascontiguousarray(a, dtype=f32)

    W_dq = np.asarray(inputs["L0_attn_W_dq"])
    W_uq = np.asarray(inputs["L0_attn_W_uq"])
    W_dkv = np.asarray(inputs["L0_attn_W_dkv"])
    W_uk = np.asarray(inputs["L0_attn_W_uk"])
    W_uv = np.asarray(inputs["L0_attn_W_uv"])
    W_o = np.asarray(inputs["L0_attn_W_o"])
    gate_w = np.asarray(inputs["L0_ffn_gate_w"])
    Wg = np.asarray(inputs["L0_ffn_Wg"])
    Wu = np.asarray(inputs["L0_ffn_Wu"])
    Wd = np.asarray(inputs["L0_ffn_Wd"])
    Sg = np.asarray(inputs["L0_ffn_Sg"])
    Su = np.asarray(inputs["L0_ffn_Su"])
    Sd = np.asarray(inputs["L0_ffn_Sd"])

    shared = {
        "wdq": c(W_dq.reshape(4, 128, D)),
        "wdqT": c(W_dq.T.reshape(8, 128, QL)),
        "wuq": c(W_uq.reshape(8, 128, QL)),
        "wuqT": c(W_uq.T.reshape(4, 128, D)),
        "wuk": c(W_uk.reshape(8, 128, L)),
        "wuv": c(W_uv.reshape(8, 128, L)),
        "woT": c(W_o.T.reshape(8, 128, D)),
        "wdkvT": c(W_dkv.T.reshape(8, 128, L)),
        "gwT": c(gate_w.T.reshape(8, 128, E)),
        "sg_dev": c(Sg.T.reshape(8, 128, NISC, 128).transpose(2, 1, 0, 3)
                    .reshape(NISC, 128, 1024)),
        "su_dev": c(Su.T.reshape(8, 128, NISC, 128).transpose(2, 1, 0, 3)
                    .reshape(NISC, 128, 1024)),
        "sd_dev": c(Sd.T.reshape(NISC, 128, 1024)),
        "wg_dev": c(np.stack([
            Wg[e_].T.reshape(8, 128, NIC, 128).transpose(2, 1, 0, 3)
            .reshape(NIC, 128, 1024) for e_ in range(E)])),
        "wu_dev": c(np.stack([
            Wu[e_].T.reshape(8, 128, NIC, 128).transpose(2, 1, 0, 3)
            .reshape(NIC, 128, 1024) for e_ in range(E)])),
        "wd_dev": c(np.stack([
            Wd[e_].T.reshape(NIC, 128, 2, 512).transpose(2, 0, 1, 3)
            for e_ in range(E)])),
    }
    return shared


def kernel(**inputs):
    x = np.asarray(inputs["L0_attn_x"])
    nc = _get_nc()
    shared = _host_layout(inputs)
    in_maps = []
    for cid in range(NCORE):
        m = dict(shared)
        m["xT"] = np.ascontiguousarray(x[cid].T.reshape(8, 128, T),
                                       dtype=np.float32)
        in_maps.append(m)
    res = run_bass_kernel_spmd(nc, in_maps, core_ids=list(range(NCORE)))
    return np.stack([res.results[cid]["out"] for cid in range(NCORE)], axis=0)


if __name__ == "__main__":
    build_program()
    print("build ok")


# version 07
# speedup vs baseline: 1.0110x; 1.0110x over previous
import contextlib
import math

import numpy as np
import ml_dtypes

import concourse.bass as bass
import concourse.mybir as mybir
from concourse.bass_utils import run_bass_kernel_spmd

F32 = mybir.dt.float32
BF16 = mybir.dt.bfloat16
AF = mybir.ActivationFunctionType
ALU = mybir.AluOpType
AX = mybir.AxisListType

NCORE = 8
T = 4096
D = 1024
NH = 16
HS = 64
L = 512
QL = 512
TILE = 512
NT = T // TILE
E = 16
IFF = 4096
ISH = 2048
NIC = IFF // 128
NISC = ISH // 128
K2 = 640
K1 = 128
CGLOB = 4096
EPS = 1e-5
SCALE = 1.0 / math.sqrt(HS)
DEV_DUMPS = False


class _Op:
    __slots__ = ("eng", "fn", "reads", "writes", "slot", "dma_val", "deps",
                 "needs_inc", "inc_val", "block", "kind", "tag", "_waits",
                 "wout", "eng_seq", "_drain")

    def __init__(self, eng, fn, reads, writes, slot, block, kind, tag):
        self.eng = eng
        self.fn = fn
        self.reads = reads
        self.writes = writes
        self.slot = slot
        self.dma_val = 0
        self.deps = []
        self.needs_inc = False
        self.inc_val = 0
        self.block = block
        self.kind = kind
        self.tag = tag
        self.wout = 0
        self.eng_seq = 0


class Em:
    ENGS = ("t", "v", "s", "g", "sp")

    def __init__(self, nc, es):
        self.nc = nc
        self.ops = []
        self.blocks = 1
        self.last_writer = {}
        self.readers = {}
        self.slot_counts = {}
        self.slot_policy = {}
        self.exec_sems = {e: es.enter_context(nc.semaphore(f"ex_{e}"))
                         for e in self.ENGS}
        self.slot_sems = {}
        self.cc_sem = es.enter_context(nc.semaphore("cc_sem"))
        self.cc_count = 0
        self._eseq = {}
        self._es = es

    def op(self, eng, fn, reads=(), writes=(), tag="", wout=0):
        o = self._rec(eng, fn, reads, writes, None, "op", tag)
        o.wout = wout

    def dma(self, eng, fn, reads=(), writes=(), slot=None, policy="seq", tag=""):
        assert slot is not None
        if slot not in self.slot_sems:
            self.slot_sems[slot] = self._es.enter_context(
                self.nc.semaphore(f"dm_{slot}"))
        self.slot_policy.setdefault(slot, policy)
        assert self.slot_policy[slot] == policy, slot
        self._rec(eng, fn, reads, writes, slot, "dma", tag)

    def cc(self, fn, reads=(), writes=(), tag="cc"):
        self._rec("g", fn, reads, writes, None, "cc", tag)

    def dma_burst(self, eng, fns_reads, writes, slot, policy="seq", tag=""):
        assert slot is not None
        if slot not in self.slot_sems:
            self.slot_sems[slot] = self._es.enter_context(
                self.nc.semaphore(f"dm_{slot}"))
        self.slot_policy.setdefault(slot, policy)
        prev_deps = []
        for w_ in writes:
            pw = self.last_writer.get(w_)
            if pw is not None:
                prev_deps.append((pw, False))
            prev_deps.extend((rd, False) for rd in self.readers.get(w_, ()))
        ops = []
        for fn, reads in fns_reads:
            o = _Op(eng, fn, tuple(reads), (), slot, self.blocks, "dma", tag)
            self.slot_counts[slot] = self.slot_counts.get(slot, 0) + 1
            o.dma_val = 16 * self.slot_counts[slot]
            o.deps.extend(prev_deps)
            for r in o.reads:
                w = self.last_writer.get(r)
                if w is not None:
                    o.deps.append((w, True))
                self.readers.setdefault(r, []).append(o)
            o.eng_seq = self._eseq.get(eng, 0)
            self._eseq[eng] = o.eng_seq + 1
            self.ops.append(o)
            ops.append(o)
        for w_ in writes:
            self.last_writer[w_] = ops[-1]
            self.readers[w_] = []

    def _rec(self, eng, fn, reads, writes, slot, kind, tag):
        o = _Op(eng, fn, tuple(reads), tuple(writes), slot, self.blocks, kind, tag)
        if kind == "dma":
            self.slot_counts[slot] = self.slot_counts.get(slot, 0) + 1
            o.dma_val = 16 * self.slot_counts[slot]
        elif kind == "cc":
            self.cc_count += 1
            o.dma_val = self.cc_count
        deps = o.deps
        for r in o.reads:
            w = self.last_writer.get(r)
            if w is not None:
                deps.append((w, True))
            self.readers.setdefault(r, []).append(o)
        for w_ in o.writes:
            pw = self.last_writer.get(w_)
            if pw is not None:
                deps.append((pw, False))
            for rd in self.readers.get(w_, ()):
                if rd is not o:
                    deps.append((rd, False))
            self.last_writer[w_] = o
            self.readers[w_] = [o] if w_ in o.reads else []
        o.eng_seq = self._eseq.get(eng, 0)
        self._eseq[eng] = o.eng_seq + 1
        self.ops.append(o)
        return o

    def new_block(self):
        self.blocks += 1

    def emit(self):
        for o in self.ops:
            for d, _raw in o.deps:
                if d.kind == "op" and d.eng != o.eng:
                    d.needs_inc = True
        counts = {e: 0 for e in self.ENGS}
        for o in self.ops:
            if o.kind == "op" and o.needs_inc:
                counts[o.eng] += 1
                o.inc_val = counts[o.eng]
        for o in self.ops:
            waits = {}
            need_drain = 0
            for d, raw in o.deps:
                if d is o:
                    continue
                if d.kind == "dma":
                    sem = self.slot_sems[d.slot]
                    val = (16 * self.slot_counts[d.slot]
                           if self.slot_policy[d.slot] == "total" else d.dma_val)
                elif d.kind == "cc":
                    sem, val = self.cc_sem, d.dma_val
                else:
                    if d.eng == o.eng:
                        if (raw and o.kind == "op" and d.kind == "op"
                                and d.wout < 256):
                            need_drain = max(need_drain, d.eng_seq + 1)
                        continue
                    sem, val = self.exec_sems[d.eng], d.inc_val
                k = sem.name
                if k not in waits or waits[k][1] < val:
                    waits[k] = (sem, val)
            o._waits = list(waits.values())
            o._drain = need_drain
        nc = self.nc
        watermark = {e: {} for e in self.ENGS}
        drained = {}
        for b in range(1, self.blocks + 1):
            ops_b = [o for o in self.ops if o.block == b]
            if not ops_b:
                continue
            per_eng = {e: [o for o in ops_b if o.eng == e] for e in self.ENGS}
            with nc.Block() as block:
                for ename, section in (("g", block.gpsimd), ("s", block.scalar),
                                       ("t", block.tensor), ("v", block.vector),
                                       ("sp", block.sync)):
                    lst = per_eng[ename]
                    if not lst:
                        continue

                    def body(eng, lst=lst, ename=ename):
                        wm = watermark[ename]
                        for o in lst:
                            if getattr(o, "_drain", 0) > drained.get(ename, 0):
                                eng.drain()
                                drained[ename] = o.eng_seq
                            for sem, val in o._waits:
                                if wm.get(sem.name, -1) >= val:
                                    continue
                                eng.wait_ge(sem, val)
                                wm[sem.name] = val
                            inst = o.fn(eng)
                            if o.kind == "dma":
                                inst.then_inc(self.slot_sems[o.slot], 16)
                            elif o.kind == "cc":
                                inst.then_inc(self.cc_sem, 1)
                            elif o.needs_inc:
                                inst.then_inc(self.exec_sems[ename], 1)
                    section(body)


def build_program():
    nc = bass.Bass("TRN2", target_bir_lowering=False)

    def din(name, shape, dtype=F32):
        return nc.dram_tensor(name, list(shape), dtype, kind="ExternalInput").ap()

    xT = din("xT", (8, 128, T))
    wdq = din("wdq", (4, 128, D))
    wdqT = din("wdqT", (8, 128, QL))
    wuq = din("wuq", (8, 128, QL))
    wuqT = din("wuqT", (4, 128, D))
    wuk = din("wuk", (8, 128, L))
    wuv = din("wuv", (8, 128, L))
    woT = din("woT", (8, 128, D))
    wdkvT = din("wdkvT", (8, 128, L))
    gwT = din("gwT", (8, 128, E))
    sg_dev = din("sg_dev", (NISC, 128, 1024))
    su_dev = din("su_dev", (NISC, 128, 1024))
    sd_dev = din("sd_dev", (NISC, 128, 1024))
    wg_dev = din("wg_dev", (E, NIC, 128, 1024))
    wu_dev = din("wu_dev", (E, NIC, 128, 1024))
    wd_dev = din("wd_dev", (E, 2, NIC, 128, 512))

    out = nc.dram_tensor("out", [T, D], F32, kind="ExternalOutput").ap()
    if DEV_DUMPS:
        d_x1 = nc.dram_tensor("d_x1", [NT, 8, 128, TILE], F32,
                              kind="ExternalOutput").ap()
        d_sh = nc.dram_tensor("d_sh", [NT, 8, 128, TILE], F32,
                              kind="ExternalOutput").ap()
        d_sc = nc.dram_tensor("d_sc", [128, T], F32,
                              kind="ExternalOutput").ap()
        d_th = nc.dram_tensor("d_th", [128, 8], F32,
                              kind="ExternalOutput").ap()
        d_rt = nc.dram_tensor("d_rt", [128, 6 * TILE], F32,
                              kind="ExternalOutput").ap()
        d_s2 = nc.dram_tensor("d_s2", [E, NT, 128, D], F32,
                              kind="ExternalOutput").ap()

    ckvTh = nc.dram_tensor("ckvTh", [NT, 128, 4 * TILE], BF16).ap()
    ckvKh = nc.dram_tensor("ckvKh", [NT, 128, 4 * TILE], BF16).ap()
    x1T_f32 = nc.dram_tensor("x1T_f32", [NT, 8, 128, TILE], F32).ap()
    x1T_b16 = nc.dram_tensor("x1T_b16", [NT, 8, 128, TILE], BF16).ap()
    x1tokh = nc.dram_tensor("x1tokh", [NT, 4, 128, D], BF16).ap()
    shTh = nc.dram_tensor("shTh", [NT, 8, 128, TILE], F32).ap()
    s2h = nc.dram_tensor("s2h", [E, NT, 128, D], BF16).ap()
    sc_inh = nc.dram_tensor("sc_inh", [E, T], F32)
    sc_outh = nc.dram_tensor("sc_outh", [128, T], F32)

    def const(name, arr):
        return nc.inline_tensor(np.ascontiguousarray(arr), name)

    c_ones_bf = const("c_ones_bf", np.ones((128, 1), ml_dtypes.bfloat16))
    c_ones1f = const("c_ones1f", np.ones((128, 1), np.float32))
    c_onesr_f = const("c_onesr_f", np.ones((1, 128), np.float32))
    c_ident = const("c_ident", np.eye(128, dtype=np.float32))
    c_iota_row = const("c_iota_row",
                       np.broadcast_to(np.arange(128, dtype=np.float32),
                                       (128, 128)))
    c_iota768 = const("c_iota768",
                      np.broadcast_to(np.arange(K2, dtype=np.float32), (128, K2)))
    mk_np = np.zeros((128, 4 * TILE), np.float32)
    for kc in range(4):
        kk = kc * 128 + np.arange(128)[:, None]
        qq = np.arange(TILE)[None, :]
        mk_np[:, kc * TILE:(kc + 1) * TILE] = np.where(kk <= qq, 0.0, -1.0e9)
    c_mask = const("c_mask", mk_np)
    g2_np = (np.arange(128)[None, :] % 16 == np.arange(16)[:, None]).astype(
        np.float32)
    c_g2 = const("c_g2", g2_np)
    c_g16 = const("c_g16", g2_np.T.copy())
    hm = np.zeros((128, 2), np.float32)
    hm[0:64, 0] = 1.0
    hm[64:128, 1] = 1.0
    c_halfmask = const("c_halfmask", hm)
    pj = np.arange(128)
    trij_np = ((pj[None, :] % 16 == pj[:, None] % 16)
               & (pj[:, None] // 16 < pj[None, :] // 16)).astype(np.float32)
    c_trij = const("c_trij", trij_np)

    gstack = contextlib.ExitStack()

    def mksb(stack):
        def sb(name, shape, dtype):
            return stack.enter_context(nc.sbuf_tensor(name, list(shape), dtype))
        return sb

    gsb = mksb(gstack)

    ones_bf = gsb("ones_bf", (128, 1), BF16)
    ones1f = gsb("ones1f", (128, 1), F32)
    onesr_f = gsb("onesr_f", (1, 128), F32)
    ident = gsb("ident", (128, 128), F32)
    iota_row = gsb("iota_row", (128, 128), F32)
    sco = gsb("sco", (128, TILE), F32)

    PS = [gstack.enter_context(nc.psum_tensor(f"ps{i}", [128, 512], F32))
          for i in range(8)]

    em = Em(nc, gstack)

    def mm(ps_ap, ps_reg, lhsT_ap, rhs_ap, start, stop, reads, tag=""):
        em.op("t", lambda e: e.matmul(ps_ap, lhsT_ap, rhs_ap,
                                      start=start, stop=stop),
              reads=reads, writes=[ps_reg], tag=tag)

    def ev(dst_ap, src_ap, reads, writes, tag="", wout=0):
        em.op("v", lambda e: e.tensor_copy(dst_ap, src_ap),
              reads=reads, writes=writes, tag=tag, wout=wout)

    astack = contextlib.ExitStack()
    asb = mksb(astack)

    maskt = asb("maskt", (128, 4 * TILE), F32)
    halfmask = asb("halfmask", (128, 2), F32)
    kerow2 = asb("kerow2", (128, 16), BF16)
    ch16row = asb("ch16row", (1, 16), F32)
    KEs = asb("KEs", (128, NH * 4 * 128), BF16)
    VEb = asb("VEb", (128, 4 * D), BF16)
    wdkvT_b = asb("wdkvT_b", (128, 8 * L), BF16)
    wdqT_b = asb("wdqT_b", (128, 8 * QL), BF16)
    wuqT_b = asb("wuqT_b", (128, 4 * D), BF16)
    gwT_b = asb("gwT_b", (128, 8 * E), BF16)

    p0stack = contextlib.ExitStack()
    p0sb = mksb(p0stack)
    wuq_b = p0sb("wuq_b", (128, 8 * QL), BF16)
    wuk_b = p0sb("wuk_b", (128, 8 * L), BF16)
    wuv_b = p0sb("wuv_b", (128, 8 * L), BF16)
    woT_b = p0sb("woT_b", (128, 8 * D), BF16)
    wdq_b = p0sb("wdq_b", (128, 4 * D), BF16)
    Mb = p0sb("Mb", (128, 4 * L), BF16)
    sq_p0 = p0sb("sq_p0", (128, L), BF16)
    kerowc = p0sb("kerowc", (128, 8), F32)
    KEb = p0sb("KEb", (128, 8 * L), BF16)

    _cl = [(ones_bf, c_ones_bf, "ones_bf"), (ones1f, c_ones1f, "ones1f"),
           (onesr_f, c_onesr_f, "onesr_f"), (ident, c_ident, "ident"),
           (iota_row, c_iota_row, "iota_row"),
           (maskt, c_mask, "maskt"), (halfmask, c_halfmask, "halfmask")]
    em.dma_burst("sp", [(lambda e, d=d_, s=s_: e.dma_start(out=d[:], in_=s.ap()),
                         ()) for d_, s_, _ in _cl],
                 writes=[r for _, _, r in _cl], slot="p0c", policy="total",
                 tag="cload")

    _wl = []
    _wlregs = set()
    for dc in range(8):
        for dst_ap, src_ap, reg in (
                (wuq_b[:, dc * QL:(dc + 1) * QL], wuq[dc], "wuq_b"),
                (wuk_b[:, dc * L:(dc + 1) * L], wuk[dc], "wuk_b"),
                (wuv_b[:, dc * L:(dc + 1) * L], wuv[dc], "wuv_b"),
                (woT_b[:, dc * D:(dc + 1) * D], woT[dc], "woT_b"),
                (wdkvT_b[:, dc * L:(dc + 1) * L], wdkvT[dc], "wdkvT_b"),
                (wdqT_b[:, dc * QL:(dc + 1) * QL], wdqT[dc], "wdqT_b"),
                (gwT_b[:, dc * E:(dc + 1) * E], gwT[dc], "gwT_b")):
            _wl.append((lambda e, d=dst_ap, s=src_ap: e.dma_start(out=d, in_=s),
                        ()))
            _wlregs.add(reg)
    for qc in range(4):
        for dst_ap, src_ap, reg in (
                (wdq_b[:, qc * D:(qc + 1) * D], wdq[qc], "wdq_b"),
                (wuqT_b[:, qc * D:(qc + 1) * D], wuqT[qc], "wuqT_b")):
            _wl.append((lambda e, d=dst_ap, s=src_ap: e.dma_start(out=d, in_=s),
                        ()))
            _wlregs.add(reg)
    em.dma_burst("g", _wl, writes=sorted(_wlregs), slot="p0w", policy="total",
                 tag="wload")

    for qlc in range(4):
        for dc in range(8):
            mm(PS[0][:, 0:L], "ps0",
               wuq_b[:, dc * QL + qlc * 128: dc * QL + qlc * 128 + 128],
               wuk_b[:, dc * L:(dc + 1) * L],
               dc == 0, dc == 7, ["wuq_b", "wuk_b"], "Mmm")
        ev(Mb[:, qlc * L:(qlc + 1) * L], PS[0][:, 0:L], ["ps0"], ["Mb"], "Mev", wout=512)
    for dc in range(8):
        for qlc in range(4):
            mm(PS[1][:, 0:L], "ps1",
               wdq_b[:, qlc * D + dc * 128: qlc * D + dc * 128 + 128],
               Mb[:, qlc * L:(qlc + 1) * L],
               qlc == 0, qlc == 3, ["wdq_b", "Mb"], "KEmm")
        ev(KEb[:, dc * L:(dc + 1) * L], PS[1][:, 0:L], ["ps1"], ["KEb"], "KEev", wout=512)
    for dc in range(8):
        em.op("v", lambda e, dc=dc: e.scalar_tensor_tensor(
            sq_p0[:], KEb[:, dc * L:(dc + 1) * L], -1.0,
            KEb[:, dc * L:(dc + 1) * L], ALU.mult, ALU.max),
            reads=["KEb"], writes=["sq_p0"], tag="keabs", wout=512)
        em.op("v", lambda e, dc=dc: e.tensor_reduce(
            kerowc[:, dc: dc + 1], sq_p0[:], AX.X, ALU.add),
            reads=["sq_p0"], writes=["kerowc"], tag="kerow")
    for dc in range(8):
        for r_ in range(2):
            em.op("v", lambda e, dc=dc, r_=r_: e.tensor_mul(
                kerow2[:, 2 * dc + r_: 2 * dc + r_ + 1],
                halfmask[:, r_: r_ + 1], kerowc[:, dc: dc + 1]),
                reads=["halfmask", "kerowc"], writes=["kerow2"], tag="ker2")
    em.op("v", lambda e: e.memset(KEs[:], 0.0), writes=["KEs"], tag="KEsz")
    for h in range(NH):
        dc = h // 2
        r0 = 64 * (h % 2)
        for lc in range(4):
            ev(KEs[r0:r0 + 64, (h * 4 + lc) * 128:(h * 4 + lc) * 128 + 128],
               KEb[r0:r0 + 64, dc * L + lc * 128: dc * L + lc * 128 + 128],
               ["KEb"], ["KEs"], "KEsc")
    for lc in range(4):
        for half in range(2):
            for dc in range(8):
                mm(PS[2][:, 0:512], "ps2",
                   wuv_b[:, dc * L + lc * 128: dc * L + lc * 128 + 128],
                   woT_b[:, dc * D + half * 512: dc * D + half * 512 + 512],
                   dc == 0, dc == 7, ["wuv_b", "woT_b"], "VEmm")
            ev(VEb[:, lc * D + half * 512: lc * D + half * 512 + 512],
               PS[2][:, 0:512], ["ps2"], ["VEb"], "VEev", wout=512)

    p0stack.close()
    em.new_block()

    xt_f = [asb(f"xt_f{i}", (128, 8 * TILE), F32) for i in range(2)]
    xt_b = asb("xt_b", (128, 8 * TILE), BF16)
    ckvstT = asb("ckvstT", (128, 4 * TILE), BF16)
    ckvstK = asb("ckvstK", (128, 4 * TILE), BF16)
    ckvT_s = [asb(f"ckvT_s{i}", (128, 4 * TILE), BF16) for i in range(3)]
    ckvK_s = [asb(f"ckvK_s{i}", (128, 4 * TILE), BF16) for i in range(3)]
    qx_b = asb("qx_b", (128, 4 * TILE), BF16)
    q_b = asb("q_b", (128, 8 * TILE), BF16)
    qlat = [asb(f"qlat{i}", (128, 4 * TILE), BF16) for i in range(2)]
    qla = asb("qla", (128, TILE), BF16)
    pT = [asb(f"pT{i}", (128, TILE), BF16) for i in range(3)]
    ylat_b = asb("ylat_b", (128, 4 * TILE), BF16)
    yT = asb("yT", (128, 8 * TILE), F32)
    x1b = asb("x1b", (128, 8 * TILE), BF16)
    x1tok_sb = asb("x1tok_sb", (128, 2 * D), BF16)
    bca = asb("bca", (128, TILE), F32)
    bcb = asb("bcb", (128, TILE), F32)
    sq_t = asb("sq_t", (128, TILE), F32)
    srow = asb("srow", (1, 2 * TILE + 16), F32)
    s1row = asb("s1row", (1, TILE), F32)
    cmaxb = asb("cmaxb", (1, 8), F32)
    cmcol = asb("cmcol", (128, 2), F32)
    chb = asb("chb", (128, 1), F32)
    smt = asb("smt", (16, TILE), F32)
    scn = asb("scn", (16, TILE), F32)

    def p1_xload(j):
        cur = xt_f[j % 2]
        curR = f"xt_f{j % 2}"
        em.dma_burst("sp", [
            (lambda e, j=j, dc=dc, cur=cur: e.dma_start(
                out=cur[:, dc * TILE:(dc + 1) * TILE],
                in_=xT[dc, :, j * TILE:(j + 1) * TILE]), ())
            for dc in range(8)], writes=[curR], slot=curR, tag="xload")

    def p1_xcast(j):
        cur = xt_f[j % 2]
        curR = f"xt_f{j % 2}"
        for dc in range(8):
            ev(xt_b[:, dc * TILE:(dc + 1) * TILE],
               cur[:, dc * TILE:(dc + 1) * TILE], [curR], ["xt_b"], "xcast", wout=512)

    def p1_ckv(j):
        cur = xt_f[j % 2]
        curR = f"xt_f{j % 2}"
        for lc in range(4):
            for dc in range(8):
                mm(PS[4][:, 0:TILE], "ps4",
                   wdkvT_b[:, dc * L + lc * 128: dc * L + lc * 128 + 128],
                   xt_b[:, dc * TILE:(dc + 1) * TILE],
                   dc == 0, dc == 7, ["wdkvT_b", "xt_b"], "ckvT")
            ev(ckvstT[:, lc * TILE:(lc + 1) * TILE], PS[4][:, 0:TILE],
               ["ps4"], ["ckvstT"], "ckvTev", wout=512)
        em.dma("sp", lambda e, j=j: e.dma_start(out=ckvTh[j], in_=ckvstT[:]),
               reads=["ckvstT"], writes=[f"h:ckvT:{j}"], slot="ckvstT",
               tag="ckvTst")
        for tc in range(4):
            for dc in range(8):
                mm(PS[5][:, 0:L], "ps5",
                   xt_b[:, dc * TILE + tc * 128: dc * TILE + tc * 128 + 128],
                   wdkvT_b[:, dc * L:(dc + 1) * L],
                   dc == 0, dc == 7, ["xt_b", "wdkvT_b"], "ckvK")
            ev(ckvstK[:, tc * L:(tc + 1) * L], PS[5][:, 0:L],
               ["ps5"], ["ckvstK"], "ckvKev", wout=512)
        em.dma("sp", lambda e, j=j: e.dma_start(out=ckvKh[j], in_=ckvstK[:]),
               reads=["ckvstK"], writes=[f"h:ckvK:{j}"], slot="ckvstK",
               tag="ckvKst")
        em.op("v", lambda e: e.tensor_reduce(
            cmcol[:, 0:1], ckvstT[:], AX.X, ALU.max,
            apply_absolute_value=True),
            reads=["ckvstT"], writes=["cmcol"], tag="cabs")
        mm(PS[6][0:1, 0:128], "ps6", cmcol[:, 0:1], ident[:],
           True, True, ["cmcol", "ident"], "cmtr")
        em.op("v", lambda e: e.tensor_reduce(
            cmaxb[0:1, 1:2], PS[6][0:1, 0:128], AX.X, ALU.max),
            reads=["ps6"], writes=["cmaxb1"], tag="cmred")
        if j == 0:
            ev(cmaxb[0:1, 0:1], cmaxb[0:1, 1:2], ["cmaxb1"], ["cmaxb"], "cm0")
        else:
            em.op("v", lambda e: e.tensor_max(
                cmaxb[0:1, 0:1], cmaxb[0:1, 0:1], cmaxb[0:1, 1:2]),
                reads=["cmaxb", "cmaxb1"], writes=["cmaxb"], tag="cmu")

    def p1_qchain(j):
        cur = xt_f[j % 2]
        curR = f"xt_f{j % 2}"
        for qlc in range(4):
            for dc in range(8):
                mm(PS[6][:, 0:TILE], "ps6",
                   wdqT_b[:, dc * QL + qlc * 128: dc * QL + qlc * 128 + 128],
                   xt_b[:, dc * TILE:(dc + 1) * TILE],
                   dc == 0, dc == 7, ["wdqT_b", "xt_b"], "qx")
            ev(qx_b[:, qlc * TILE:(qlc + 1) * TILE], PS[6][:, 0:TILE],
               ["ps6"], ["qx_b"], "qxev", wout=512)
        for dc in range(8):
            for qlc in range(4):
                mm(PS[7][:, 0:TILE], "ps7",
                   wuqT_b[:, qlc * D + dc * 128: qlc * D + dc * 128 + 128],
                   qx_b[:, qlc * TILE:(qlc + 1) * TILE],
                   qlc == 0, qlc == 3, ["wuqT_b", "qx_b"], "qmm")
            ev(q_b[:, dc * TILE:(dc + 1) * TILE], PS[7][:, 0:TILE],
               ["ps7"], ["q_b"], "qev", wout=512)

    def p1_bounds(j):
        cur = xt_f[j % 2]
        curR = f"xt_f{j % 2}"
        em.op("v", lambda e: e.tensor_scalar(
            cmaxb[0:1, 3:4], cmaxb[0:1, 0:1], -SCALE, None, ALU.mult),
            reads=["cmaxb"], writes=["cmaxb3"], tag="cmneg")
        for dc in range(8):
            em.op("v", lambda e, dc=dc: e.scalar_tensor_tensor(
                qla[:], q_b[:, dc * TILE:(dc + 1) * TILE], -1.0,
                q_b[:, dc * TILE:(dc + 1) * TILE], ALU.mult, ALU.max),
                reads=["q_b"], writes=["qla"], tag="qabs", wout=512)
            for r_ in range(2):
                mm(PS[3][0:1, 0:TILE], "ps3",
                   kerow2[:, 2 * dc + r_: 2 * dc + r_ + 1], qla[:],
                   True, True, ["qla", "kerow2"], "bndmm")
                em.op("v", lambda e, dc=dc, r_=r_: e.tensor_reduce(
                    ch16row[0:1, 2 * dc + r_: 2 * dc + r_ + 1],
                    PS[3][0:1, 0:TILE], AX.X, ALU.max),
                    reads=["ps3"], writes=["ch16row"], tag="bndred")

    def p1_mkhead(j):
        def emit_qlat(h):
            ql_s = qlat[h % 2]
            qlR = f"qlat{h % 2}"
            dch = h // 2
            for lc in range(4):
                mm(PS[6][:, 0:TILE], "ps6",
                   KEs[:, (h * 4 + lc) * 128:(h * 4 + lc) * 128 + 128],
                   q_b[:, dch * TILE:(dch + 1) * TILE],
                   True, True, ["KEs", "q_b"], "qlat")
                em.op("v", lambda e, lc=lc, ql_s=ql_s: e.tensor_copy(
                    ql_s[:, lc * TILE:(lc + 1) * TILE], PS[6][:, 0:TILE]),
                    reads=["ps6"], writes=[qlR], tag="qlatev", wout=512)

        def emit_ch(h):
            em.op("v", lambda e, h=h: e.tensor_tensor(
                cmaxb[0:1, 4:5], ch16row[0:1, h: h + 1], cmaxb[0:1, 3:4],
                ALU.mult),
                reads=["ch16row", "cmaxb3"], writes=["cmaxb4"], tag="chmul")
            mm(PS[3][:, 0:1], "ps3", onesr_f[:], cmaxb[0:1, 4:5], True, True,
               ["cmaxb4", "onesr_f"], "chbc")
            ev(chb[:, 0:1], PS[3][:, 0:1], ["ps3"], ["chb"], "chbev")

        def emit_kloop(h, j=j):
            ql_s = qlat[h % 2]
            qlR = f"qlat{h % 2}"
            nkc = j + 1
            nkg = nkc * 4

            def emit_S(kg):
                kc, ksc = kg // 4, kg % 4
                sl = kc % 3
                ctR = f"ckvT_s{sl}"
                if ksc == 0:
                    em.dma("sp", lambda e, kc=kc, sl=sl: e.dma_start(
                        out=ckvT_s[sl][:], in_=ckvTh[kc]),
                        reads=[f"h:ckvT:{kc}"], writes=[ctR], slot=ctR,
                        tag="ckvTld")
                    em.dma("g", lambda e, kc=kc, sl=sl: e.dma_start(
                        out=ckvK_s[sl][:], in_=ckvKh[kc]),
                        reads=[f"h:ckvK:{kc}"], writes=[f"ckvK_s{sl}"],
                        slot=f"ckvK_s{sl}", tag="ckvKld")
                sps = PS[4] if (kg % 2 == 0) else PS[5]
                spsR = "ps4" if (kg % 2 == 0) else "ps5"
                for lc in range(4):
                    mm(sps[:, 0:TILE], spsR,
                       ckvT_s[sl][:, lc * TILE + ksc * 128:
                                   lc * TILE + ksc * 128 + 128],
                       ql_s[:, lc * TILE:(lc + 1) * TILE],
                       lc == 0, lc == 3, [ctR, qlR], "Smm")
                if kc == j:
                    em.op("v", lambda e, sps=sps, ksc=ksc, spsR=spsR:
                          e.tensor_tensor(
                              sps[:, 0:TILE], sps[:, 0:TILE],
                              maskt[:, ksc * TILE:(ksc + 1) * TILE],
                              ALU.add),
                          reads=[spsR, "maskt"], writes=[spsR],
                          tag="maskadd", wout=512)
                pbuf = pT[kg % 3]
                pbR = f"pT{kg % 3}"
                em.op("s", lambda e, sps=sps, pbuf=pbuf: e.activation(
                    pbuf[:], sps[:, 0:TILE], AF.Exp,
                    bias=chb[:, 0:1], scale=SCALE),
                    reads=[spsR, "chb"], writes=[pbR], tag="exp", wout=512)

            def emit_PV(kg):
                kc, ksc = kg // 4, kg % 4
                sl = kc % 3
                ckR = f"ckvK_s{sl}"
                pbuf = pT[kg % 3]
                pbR = f"pT{kg % 3}"
                mm(PS[3][0:1, 0:TILE], "ps3", ones_bf[:], pbuf[:],
                   kg == 0, kg == nkg - 1, ["ones_bf", pbR], "denmm")
                for lc in range(4):
                    tgt, tgtR = ((PS[lc], f"ps{lc}") if lc < 3
                                 else (PS[7], "ps7"))
                    mm(tgt[:, 0:TILE], tgtR,
                       ckvK_s[sl][:, ksc * TILE + lc * 128:
                                  ksc * TILE + lc * 128 + 128],
                       pbuf[:],
                       kg == 0, kg == nkg - 1, [ckR, pbR], "PVmm")

            for kg in range(nkg):
                emit_S(kg)
                if kg > 0:
                    emit_PV(kg - 1)
            emit_PV(nkg - 1)

        def emit_tail(h):
            em.op("v", lambda e: e.reciprocal(s1row[0:1, :],
                                              PS[3][0:1, 0:TILE]),
                  reads=["ps3"], writes=["s1row"], tag="recip", wout=512)
            mm(PS[3][:, 0:TILE], "ps3", onesr_f[:], s1row[0:1, :], True, True,
               ["s1row", "onesr_f"], "rbc")
            ev(bca[:], PS[3][:, 0:TILE], ["ps3"], ["bca"], "rbcev", wout=512)
            for lc in range(4):
                src_, srcR = (PS[lc], f"ps{lc}") if lc < 3 else (PS[7], "ps7")
                ev(ylat_b[:, lc * TILE:(lc + 1) * TILE], src_[:, 0:TILE],
                   [srcR], ["ylat_b"], "ylatev", wout=512)
            for lc in range(4):
                mm(PS[3][0:64, 0:TILE], "ps3",
                   VEb[:, lc * D + h * 64: lc * D + h * 64 + 64],
                   ylat_b[:, lc * TILE:(lc + 1) * TILE],
                   lc == 0, lc == 3, ["VEb", "ylat_b"], "yfmm")
            r0 = 64 * (h % 2)
            em.op("v", lambda e, h=h, r0=r0: e.tensor_tensor(
                yT[r0:r0 + 64, (h // 2) * TILE:(h // 2 + 1) * TILE],
                PS[3][0:64, 0:TILE], bca[0:64, :], ALU.mult),
                reads=["ps3", "bca"], writes=["yT"], tag="yev", wout=512)

        return emit_qlat, emit_ch, emit_kloop, emit_tail

    def p1_stage(j, h):
        emit_qlat, emit_ch, emit_kloop, emit_tail = p1_mkhead(j)
        emit_qlat(h)
        if h > 0:
            emit_tail(h - 1)
        emit_ch(h)
        emit_kloop(h)

    def p1_tail_final(j):
        emit_qlat, emit_ch, emit_kloop, emit_tail = p1_mkhead(j)
        emit_tail(NH - 1)

    def p1_epilog(j):
        cur = xt_f[j % 2]
        curR = f"xt_f{j % 2}"
        for dc in range(8):
            em.op("v", lambda e, dc=dc, cur=cur: e.tensor_add(
                yT[:, dc * TILE:(dc + 1) * TILE],
                yT[:, dc * TILE:(dc + 1) * TILE],
                cur[:, dc * TILE:(dc + 1) * TILE]),
                reads=["yT", curR], writes=["yT"], tag="resid", wout=512)
        for dc in range(8):
            mm(PS[4][0:1, 0:TILE], "ps4", ones1f[:],
               yT[:, dc * TILE:(dc + 1) * TILE],
               dc == 0, dc == 7, ["yT", "ones1f"], "s1mm")
        for dc in range(8):
            em.op("s", lambda e, dc=dc: e.activation(
                sq_t[:], yT[:, dc * TILE:(dc + 1) * TILE], AF.Square),
                reads=["yT"], writes=["sq_t"], tag="sq", wout=512)
            mm(PS[5][0:1, 0:TILE], "ps5", ones1f[:], sq_t[:],
               dc == 0, dc == 7, ["sq_t", "ones1f"], "s2mm")
        em.op("v", lambda e: e.tensor_scalar(
            s1row[0:1, :], PS[4][0:1, 0:TILE], 1.0 / D, None, ALU.mult),
            reads=["ps4"], writes=["s1row"], tag="mu", wout=512)
        em.op("v", lambda e: e.tensor_scalar(
            srow[0:1, 0:TILE], PS[5][0:1, 0:TILE], 1.0 / D, None, ALU.mult),
            reads=["ps5"], writes=["srow"], tag="ex2", wout=512)
        em.op("v", lambda e: e.tensor_mul(
            srow[0:1, TILE:2 * TILE], s1row[0:1, :], s1row[0:1, :]),
            reads=["s1row"], writes=["srow"], tag="mu2", wout=512)
        em.op("v", lambda e: e.tensor_sub(
            srow[0:1, 0:TILE], srow[0:1, 0:TILE], srow[0:1, TILE:2 * TILE]),
            reads=["srow"], writes=["srow"], tag="var", wout=512)
        em.op("v", lambda e: e.tensor_scalar(
            srow[0:1, 0:TILE], srow[0:1, 0:TILE], EPS, None, ALU.add),
            reads=["srow"], writes=["srow"], tag="vareps", wout=512)
        em.op("s", lambda e: e.activation(
            srow[0:1, TILE:2 * TILE], srow[0:1, 0:TILE], AF.Ln),
            reads=["srow"], writes=["srow"], tag="lnv", wout=512)
        em.op("s", lambda e: e.activation(
            srow[0:1, 0:TILE], srow[0:1, TILE:2 * TILE], AF.Exp, scale=-0.5),
            reads=["srow"], writes=["srow"], tag="rstd", wout=512)
        mm(PS[4][:, 0:TILE], "ps4", onesr_f[:], s1row[0:1, :], True, True,
           ["s1row", "onesr_f"], "mubc")
        ev(bca[:], PS[4][:, 0:TILE], ["ps4"], ["bca"], "mubcev", wout=512)
        mm(PS[5][:, 0:TILE], "ps5", onesr_f[:], srow[0:1, 0:TILE], True, True,
           ["srow", "onesr_f"], "rsbc")
        ev(bcb[:], PS[5][:, 0:TILE], ["ps5"], ["bcb"], "rsbcev", wout=512)
        for dc in range(8):
            em.op("v", lambda e, dc=dc: e.tensor_sub(
                yT[:, dc * TILE:(dc + 1) * TILE],
                yT[:, dc * TILE:(dc + 1) * TILE], bca[:]),
                reads=["yT", "bca"], writes=["yT"], tag="lnsub", wout=512)
            em.op("v", lambda e, dc=dc: e.tensor_mul(
                yT[:, dc * TILE:(dc + 1) * TILE],
                yT[:, dc * TILE:(dc + 1) * TILE], bcb[:]),
                reads=["yT", "bcb"], writes=["yT"], tag="lnmul", wout=512)
            em.op("s", lambda e, dc=dc: e.copy(
                x1b[:, dc * TILE:(dc + 1) * TILE],
                yT[:, dc * TILE:(dc + 1) * TILE]),
                reads=["yT"], writes=["x1b"], tag="x1cast", wout=512)
        em.dma_burst("sp", [
            (lambda e, j=j, dc=dc: e.dma_start(
                out=x1T_f32[j, dc], in_=yT[:, dc * TILE:(dc + 1) * TILE]),
             ("yT",)) for dc in range(8)],
            writes=[f"h:x1f:{j}"], slot="yT", tag="x1fst")
        em.dma_burst("sp", [
            (lambda e, j=j, dc=dc: e.dma_start(
                out=x1T_b16[j, dc], in_=x1b[:, dc * TILE:(dc + 1) * TILE]),
             ("x1b",)) for dc in range(8)],
            writes=[f"h:x1b:{j}"], slot="x1b", tag="x1bst")
        for tc in range(4):
            tsl = tc % 2
            for dc in range(8):
                em.op("t", lambda e, tc=tc, dc=dc: e.transpose(
                    PS[6][:, 0:128],
                    yT[:, dc * TILE + tc * 128: dc * TILE + tc * 128 + 128],
                    ident[:]),
                    reads=["yT", "ident"], writes=["ps6"], tag="x1tr")
                em.op("v", lambda e, tsl=tsl, dc=dc: e.tensor_copy(
                    x1tok_sb[:, tsl * D + dc * 128: tsl * D + dc * 128 + 128],
                    PS[6][:, 0:128]),
                    reads=["ps6"], writes=[f"x1tok_sb{tsl}"], tag="x1trev", wout=512)
            em.dma("sp", lambda e, j=j, tc=tc, tsl=tsl: e.dma_start(
                out=x1tokh[j, tc], in_=x1tok_sb[:, tsl * D:(tsl + 1) * D]),
                reads=[f"x1tok_sb{tsl}"], writes=[f"h:x1tok:{j}:{tc}"],
                slot=f"x1tok_sb{tsl}", tag="x1tokst")
        for dc in range(8):
            mm(PS[4][0:16, 0:TILE], "ps4", gwT_b[:, dc * E:(dc + 1) * E],
               x1b[:, dc * TILE:(dc + 1) * TILE],
               dc == 0, dc == 7, ["gwT_b", "x1b"], "logit")
        em.op("s", lambda e: e.activation(smt[:, :], PS[4][0:16, 0:TILE],
                                          AF.Exp),
              reads=["ps4"], writes=["smt"], tag="gexp", wout=512)
        mm(PS[5][0:1, 0:TILE], "ps5", ones1f[0:16, :], smt[:, :], True, True,
           ["smt", "ones1f"], "gsum")
        em.op("v", lambda e: e.reciprocal(s1row[0:1, :], PS[5][0:1, 0:TILE]),
              reads=["ps5"], writes=["s1row"], tag="grecip", wout=512)
        mm(PS[6][0:16, 0:TILE], "ps6", onesr_f[:, 0:16], s1row[0:1, :],
           True, True, ["s1row", "onesr_f"], "gbc")
        em.op("v", lambda e, j=j: e.tensor_tensor(
            scn[:, :], smt[:, :], PS[6][0:16, 0:TILE], ALU.mult),
            reads=["smt", "ps6"], writes=["scn"], tag="gmul", wout=512)
        em.dma("sp", lambda e, j=j: e.dma_start(
            out=sc_inh.ap()[:, j * TILE:(j + 1) * TILE], in_=scn[:, :]),
            reads=["scn"], writes=["h:sc_in"], slot="scn", tag="scst")
        em.dma("sp", lambda e, j=j: e.dma_start(
            out=sco[16 * j:16 * j + 16, :], in_=scn[:, :]),
            reads=["scn"], writes=["sco"], slot="scn", tag="sccp")

    p1_xload(0)
    p1_xcast(0)
    p1_ckv(0)
    p1_qchain(0)
    p1_bounds(0)
    p1_stage(0, 0)
    for j in range(NT):
        if j + 1 < NT:
            p1_xload(j + 1)
        for h in range(1, NH):
            p1_stage(j, h)
        p1_tail_final(j)
        if j + 1 < NT:
            p1_xcast(j + 1)
            p1_ckv(j + 1)
            p1_qchain(j + 1)
            p1_bounds(j + 1)
            p1_stage(j + 1, 0)
        p1_epilog(j)

    em.cc(lambda e: e.collective_compute(
        "AllGather", ALU.bypass, replica_groups=[list(range(NCORE))],
        ins=[sc_inh.ap().opt()], outs=[sc_outh.ap().opt()]),
        reads=["h:sc_in"], writes=["h:sc_out"], tag="ag")

    astack.close()
    em.new_block()

    lgstack = contextlib.ExitStack()
    lsb = mksb(lgstack)
    rt = lsb("rt", (128, 6 * TILE), F32)
    r1tok = lsb("r1tok", (128, 4 * 128), F32)
    wtok = lsb("wtok", (128, 4 * 128), BF16)
    r2v = lsb("r2v", (128, 128), F32)
    r2tok = lsb("r2tok", (128, 128), F32)
    th = lsb("th", (128, 8), F32)
    th16 = lsb("th16", (16, 4), F32)
    g2 = lsb("g2", (16, 128), F32)
    g16 = lsb("g16", (128, 16), F32)
    trij = lsb("trij", (128, 128), F32)
    em.dma_burst("sp", [
        (lambda e, d=d_, s=s_: e.dma_start(out=d[:], in_=s.ap()), ())
        for d_, s_ in ((g2, c_g2), (g16, c_g16), (trij, c_trij))],
        writes=["g2", "g16", "trij"], slot="p3c", policy="total", tag="cload3")

    b1stack = contextlib.ExitStack()
    b1sb = mksb(b1stack)
    sc_all = b1sb("sc_all", (128, T), F32)
    bis = b1sb("bis", (128, T), BF16)
    sgw = b1sb("sgw", (128, 2 * 1024), BF16)
    suw = b1sb("suw", (128, 2 * 1024), BF16)
    sdw = b1sb("sdw", (128, 2 * 1024), BF16)
    x1bt = [b1sb(f"x1bt{i}", (128, 8 * TILE), BF16) for i in range(2)]
    hsT = b1sb("hsT", (128, NISC * TILE), BF16)
    gg = b1sb("gg", (128, TILE), BF16)
    shev = b1sb("shev", (128, 2 * TILE), F32)

    em.dma("sp", lambda e: e.dma_start(out=sc_all[:], in_=sc_outh.ap()),
           reads=["h:sc_out"], writes=["sc_all"], slot="sc_all", tag="scld")

    p3_thunks = []
    em.op("v", lambda e: e.memset(th[:, 0:1], 0.0), writes=["th"], tag="bis0")
    em.op("v", lambda e: e.memset(th[:, 1:2], 1.0), writes=["th"], tag="bis1")
    def p3_iter(it):
        em.op("v", lambda e: e.tensor_add(th[:, 2:3], th[:, 0:1], th[:, 1:2]),
              reads=["th"], writes=["th"], tag="bmid1")
        em.op("v", lambda e: e.tensor_scalar(
            th[:, 2:3], th[:, 2:3], 0.5, None, ALU.mult),
            reads=["th"], writes=["th"], tag="bmid2")
        em.op("v", lambda e: e.tensor_scalar(
            bis[:], sc_all[:], th[:, 2:3], 0.0, ALU.is_ge, ALU.add,
            accum_out=th[:, 3:4]),
            reads=["sc_all", "th"], writes=["bis", "th"], tag="bcmp", wout=512)
        mm(PS[0][0:16, 0:1], "ps0", g16[:], th[:, 3:4], True, True,
           ["th", "g16"], "bcnt")
        em.op("v", lambda e: e.tensor_scalar(
            th16[:, 0:1], PS[0][0:16, 0:1], float(CGLOB), None, ALU.is_ge),
            reads=["ps0"], writes=["th16"], tag="bdir")
        mm(PS[1][:, 0:1], "ps1", g2[:], th16[:, 0:1], True, True,
           ["th16", "g2"], "bdirbc")
        ev(th[:, 4:5], PS[1][:, 0:1], ["ps1"], ["th"], "bdirev")
        em.op("v", lambda e: e.tensor_sub(th[:, 5:6], th[:, 2:3], th[:, 0:1]),
              reads=["th"], writes=["th"], tag="blo1")
        em.op("v", lambda e: e.tensor_mul(th[:, 5:6], th[:, 5:6], th[:, 4:5]),
              reads=["th"], writes=["th"], tag="blo2")
        em.op("v", lambda e: e.tensor_add(th[:, 0:1], th[:, 0:1], th[:, 5:6]),
              reads=["th"], writes=["th"], tag="blo3")
        em.op("v", lambda e: e.tensor_sub(th[:, 6:7], th[:, 1:2], th[:, 2:3]),
              reads=["th"], writes=["th"], tag="bhi1")
        em.op("v", lambda e: e.tensor_mul(th[:, 6:7], th[:, 6:7], th[:, 4:5]),
              reads=["th"], writes=["th"], tag="bhi2")
        em.op("v", lambda e: e.tensor_add(th[:, 1:2], th[:, 2:3], th[:, 6:7]),
              reads=["th"], writes=["th"], tag="bhi3")

    for _it in range(44):
        p3_thunks.append(lambda _it=_it: p3_iter(_it))

    def p3_rank_all():
        em.op("v", lambda e: e.tensor_scalar(
            rt[:, 1024:1536], sco[:], th[:, 0:1], None, ALU.is_ge),
            reads=["sco", "th"], writes=["rt"], tag="mask", wout=512)
        em.op("v", lambda e: e.tensor_mul(
            rt[:, 512:1024], sco[:], rt[:, 1024:1536]),
            reads=["sco", "rt"], writes=["rt"], tag="wmask", wout=512)
        em.op("v", lambda e: e.memset(rt[:, 2048:2560], 1.0),
              writes=["rt"], tag="sone")
        em.op("v", lambda e: e.tensor_tensor_scan(
            rt[:, 0:512], rt[:, 2048:2560], rt[:, 1024:1536], 0.0,
            ALU.mult, ALU.add),
            reads=["rt"], writes=["rt"], tag="scan", wout=512)
        em.op("v", lambda e: e.tensor_sub(rt[:, 0:512], rt[:, 0:512],
                                          rt[:, 1024:1536]),
              reads=["rt"], writes=["rt"], tag="rexcl", wout=512)
        em.op("v", lambda e: e.tensor_scalar(
            rt[:, 1536:2048], rt[:, 0:512], float(K1), None, ALU.is_lt),
            reads=["rt"], writes=["rt"], tag="rlt", wout=512)
        em.op("v", lambda e: e.tensor_mul(
            rt[:, 1536:2048], rt[:, 1536:2048], rt[:, 1024:1536]),
            reads=["rt"], writes=["rt"], tag="rvalid", wout=512)
        em.op("v", lambda e: e.tensor_mul(rt[:, 0:512], rt[:, 0:512],
                                          rt[:, 1536:2048]),
              reads=["rt"], writes=["rt"], tag="rv1", wout=512)
        em.op("v", lambda e: e.tensor_scalar(
            rt[:, 2048:2560], rt[:, 1536:2048], -1.0, None, ALU.add),
            reads=["rt"], writes=["rt"], tag="rv2", wout=512)
        em.op("v", lambda e: e.tensor_add(rt[:, 0:512], rt[:, 0:512],
                                          rt[:, 2048:2560]),
              reads=["rt"], writes=["rt"], tag="rank1v", wout=512)
        em.op("v", lambda e: e.tensor_reduce(
            th[:, 5:6], rt[:, 1536:2048], AX.X, ALU.add),
            reads=["rt"], writes=["th"], tag="tilecnt")
        mm(PS[0][:, 0:1], "ps0", trij[:], th[:, 5:6], True, True,
           ["th", "trij"], "base2")
        ev(th[:, 6:7], PS[0][:, 0:1], ["ps0"], ["th"], "base2ev")
        em.op("v", lambda e: e.tensor_scalar(
            r2v[:], iota_row[:], th[:, 6:7], None, ALU.add),
            reads=["th", "iota_row"], writes=["r2v"], tag="r2a")
        em.op("v", lambda e: e.tensor_scalar(
            rt[:, 1536:1664], iota_row[:, 0:128], th[:, 5:6], None, ALU.is_lt),
            reads=["th", "iota_row"], writes=["rt"], tag="r2m1")
        em.op("v", lambda e: e.tensor_scalar(
            rt[:, 1664:1792], r2v[:], float(K2), None, ALU.is_lt),
            reads=["r2v"], writes=["rt"], tag="r2m2")
        em.op("v", lambda e: e.tensor_mul(
            rt[:, 1536:1664], rt[:, 1536:1664], rt[:, 1664:1792]),
            reads=["rt"], writes=["rt"], tag="r2m3")
        em.op("v", lambda e: e.tensor_mul(r2v[:], r2v[:], rt[:, 1536:1664]),
              reads=["rt", "r2v"], writes=["r2v"], tag="r2m4")
        em.op("v", lambda e: e.tensor_scalar(
            rt[:, 1664:1792], rt[:, 1536:1664], -1.0, None, ALU.add),
            reads=["rt"], writes=["rt"], tag="r2m5")
        em.op("v", lambda e: e.tensor_add(r2v[:], r2v[:], rt[:, 1664:1792]),
              reads=["rt", "r2v"], writes=["r2v"], tag="rank2v")
        for tsub in range(4):
            em.op("t", lambda e, tsub=tsub: e.transpose(
                PS[2][:, 0:128], rt[:, tsub * 128:(tsub + 1) * 128], ident[:]),
                reads=["rt", "ident"], writes=["ps2"], tag="r1tr")
            ev(r1tok[:, tsub * 128:(tsub + 1) * 128], PS[2][:, 0:128],
               ["ps2"], ["r1tok"], "r1trev", wout=512)
            em.op("t", lambda e, tsub=tsub: e.transpose(
                PS[2][:, 0:128], rt[:, 512 + tsub * 128: 512 + (tsub + 1) * 128],
                ident[:]),
                reads=["rt", "ident"], writes=["ps2"], tag="wtr")
            ev(wtok[:, tsub * 128:(tsub + 1) * 128], PS[2][:, 0:128],
               ["ps2"], ["wtok"], "wtrev", wout=512)
        em.op("t", lambda e: e.transpose(PS[2][:, 0:128], r2v[:], ident[:]),
              reads=["r2v", "ident"], writes=["ps2"], tag="r2tr")
        ev(r2tok[:], PS[2][:, 0:128], ["ps2"], ["r2tok"], "r2trev", wout=512)

    p3_thunks.append(p3_rank_all)

    for j in range(NT):
        xsl = x1bt[j % 2]
        xR = f"x1bt{j % 2}"
        em.dma_burst("sp", [
            (lambda e, j=j, dc=dc, xsl=xsl: e.dma_start(
                out=xsl[:, dc * TILE:(dc + 1) * TILE], in_=x1T_b16[j, dc]),
             (f"h:x1b:{j}",)) for dc in range(8)],
            writes=[xR], slot=xR, tag="x1bld")
        for isc in range(NISC):
            sl = isc % 2
            sgR, suR = f"sgw{sl}", f"suw{sl}"
            em.dma("g", lambda e, isc=isc, sl=sl: e.dma_start(
                out=sgw[:, sl * 1024:(sl + 1) * 1024], in_=sg_dev[isc]),
                writes=[sgR], slot=sgR, tag="sgld")
            em.dma("g", lambda e, isc=isc, sl=sl: e.dma_start(
                out=suw[:, sl * 1024:(sl + 1) * 1024], in_=su_dev[isc]),
                writes=[suR], slot=suR, tag="suld")
            for dc in range(8):
                mm(PS[3][:, 0:TILE], "ps3",
                   sgw[:, sl * 1024 + dc * 128: sl * 1024 + dc * 128 + 128],
                   xsl[:, dc * TILE:(dc + 1) * TILE],
                   dc == 0, dc == 7, [sgR, xR], "sgmm")
            for dc in range(8):
                mm(PS[4][:, 0:TILE], "ps4",
                   suw[:, sl * 1024 + dc * 128: sl * 1024 + dc * 128 + 128],
                   xsl[:, dc * TILE:(dc + 1) * TILE],
                   dc == 0, dc == 7, [suR, xR], "summ")
            em.op("s", lambda e: e.activation(gg[:], PS[3][:, 0:TILE], AF.Gelu),
                  reads=["ps3"], writes=["gg"], tag="sgelu", wout=512)
            em.op("v", lambda e, isc=isc: e.tensor_tensor(
                hsT[:, isc * TILE:(isc + 1) * TILE], gg[:], PS[4][:, 0:TILE],
                ALU.mult),
                reads=["gg", "ps4"], writes=["hsT"], tag="smul", wout=512)
            if p3_thunks:
                p3_thunks.pop(0)()
        for isc in range(NISC):
            sl = isc % 2
            sdR = f"sdw{sl}"
            em.dma("g", lambda e, isc=isc, sl=sl: e.dma_start(
                out=sdw[:, sl * 1024:(sl + 1) * 1024], in_=sd_dev[isc]),
                writes=[sdR], slot=sdR, tag="sdld")
            for dc in range(8):
                mm(PS[dc][:, 0:TILE], f"ps{dc}",
                   sdw[:, sl * 1024 + dc * 128: sl * 1024 + dc * 128 + 128],
                   hsT[:, isc * TILE:(isc + 1) * TILE],
                   isc == 0, isc == NISC - 1, [sdR, "hsT"], "sdmm")
        for dc in range(8):
            ssl = dc % 2
            ev(shev[:, ssl * TILE:(ssl + 1) * TILE], PS[dc][:, 0:TILE],
               [f"ps{dc}"], [f"shev{ssl}"], "shevc", wout=512)
            em.dma("sp", lambda e, j=j, dc=dc, ssl=ssl: e.dma_start(
                out=shTh[j, dc], in_=shev[:, ssl * TILE:(ssl + 1) * TILE]),
                reads=[f"shev{ssl}"], writes=[f"h:shT:{j}"], slot=f"shev{ssl}",
                tag="shst")

    while p3_thunks:
        p3_thunks.pop(0)()

    b1stack.close()
    em.new_block()

    cstack = contextlib.ExitStack()
    csb = mksb(cstack)
    iota768 = csb("iota768", (128, K2), F32)
    em.dma("sp", lambda e: e.dma_start(out=iota768[:], in_=c_iota768.ap()),
           writes=["iota768"], slot="p5c", policy="total", tag="cload5")
    x1t_sb = [csb(f"x1t_sb{i}", (128, 4 * D), BF16) for i in range(2)]
    r1b = csb("r1b", (128, 4 * 128), BF16)
    r2b = csb("r2b", (128, NT * K2), BF16)
    q2scr = csb("q2scr", (128, 128), F32)
    q2b6 = csb("q2b6", (128, (K2 // 128) * 128), BF16)
    zb = csb("zb", (128, NT * D), BF16)
    zwb = csb("zwb", (128, NT), BF16)
    xgT = csb("xgT", (128, 8 * K2), BF16)
    wgv = [csb(f"wgv{i}", (128, 1024), BF16) for i in range(2)]
    wuv_ = [csb(f"wuv_{i}", (128, 1024), BF16) for i in range(2)]
    wdv = [csb(f"wdv{i}", (128, 512), BF16) for i in range(4)]
    hT = csb("hT", (128, NIC * K2), BF16)
    ggx = csb("ggx", (128, K2), BF16)
    eoT = csb("eoT", (128, (K2 // 128) * D), BF16)
    wgc = csb("wgc", (128, 8), F32)
    wrow = csb("wrow", (1, K2), F32)
    s2ev = csb("s2ev", (128, 2 * D), BF16)

    for ei in range(E):
        for j in range(NT):
            p = 16 * j + ei
            xs = x1t_sb[j % 2]
            xsR = f"x1t_sb{j % 2}"
            em.dma_burst("sp", [
                (lambda e, j=j, tc=tc, xs=xs: e.dma_start(
                    out=xs[:, tc * D:(tc + 1) * D], in_=x1tokh[j, tc]),
                 (f"h:x1tok:{j}:{tc}",)) for tc in range(4)],
                writes=[xsR], slot=xsR, tag="x1tld")
            for tsub in range(4):
                em.op("v", lambda e, p=p, tsub=tsub: e.tensor_scalar(
                    r1b[:, tsub * 128:(tsub + 1) * 128], iota_row[:],
                    r1tok[:, tsub * 128 + p: tsub * 128 + p + 1], None,
                    ALU.is_equal),
                    reads=["r1tok", "iota_row"], writes=["r1b"], tag="r1bld", wout=512)
            for dh in range(2):
                pb, pbR = (PS[0], "ps0") if dh == 0 else (PS[1], "ps1")
                for tsub in range(4):
                    mm(pb[:, 0:512], pbR,
                       r1b[:, tsub * 128:(tsub + 1) * 128],
                       xs[:, tsub * D + dh * 512: tsub * D + dh * 512 + 512],
                       tsub == 0, tsub == 3, ["r1b", xsR], "zmm")
                em.op("v", lambda e, j=j, dh=dh, pb=pb: e.tensor_copy(
                    zb[:, j * D + dh * 512: j * D + dh * 512 + 512],
                    pb[:, 0:512]),
                    reads=[pbR], writes=["zb"], tag="zev", wout=512)
            for tsub in range(4):
                mm(PS[2][:, 0:1], "ps2",
                   r1b[:, tsub * 128:(tsub + 1) * 128],
                   wtok[:, tsub * 128 + p: tsub * 128 + p + 1],
                   tsub == 0, tsub == 3, ["r1b", "wtok"], "zwmm")
            em.op("v", lambda e, j=j: e.tensor_copy(
                zwb[:, j: j + 1], PS[2][:, 0:1]),
                reads=["ps2"], writes=["zwb"], tag="zwev")
            em.op("v", lambda e, p=p, j=j: e.tensor_scalar(
                r2b[:, j * K2:(j + 1) * K2], iota768[:],
                r2tok[:, p: p + 1], None, ALU.is_equal),
                reads=["r2tok", "iota768"], writes=["r2b"], tag="r2bld", wout=512)
        for dc in range(8):
            pa, paR = (PS[3], "ps3") if dc % 2 == 0 else (PS[5], "ps5")
            pb2, pbR2 = (PS[4], "ps4") if dc % 2 == 0 else (PS[6], "ps6")
            for j in range(NT):
                mm(pa[:, 0:512], paR,
                   zb[:, j * D + dc * 128: j * D + dc * 128 + 128],
                   r2b[:, j * K2: j * K2 + 512],
                   j == 0, j == NT - 1, ["zb", "r2b"], "xgmmA")
                mm(pb2[:, 0:K2 - 512], pbR2,
                   zb[:, j * D + dc * 128: j * D + dc * 128 + 128],
                   r2b[:, j * K2 + 512: j * K2 + K2],
                   j == 0, j == NT - 1, ["zb", "r2b"], "xgmmB")
            em.op("v", lambda e, dc=dc, pa=pa: e.tensor_copy(
                xgT[:, dc * K2: dc * K2 + 512], pa[:, 0:512]),
                reads=[paR], writes=["xgT"], tag="xgevA", wout=512)
            em.op("v", lambda e, dc=dc, pb2=pb2: e.tensor_copy(
                xgT[:, dc * K2 + 512: dc * K2 + K2], pb2[:, 0:K2 - 512]),
                reads=[pbR2], writes=["xgT"], tag="xgevB", wout=512)
        for j in range(NT):
            mm(PS[7][0:1, 0:512], "ps7", zwb[:, j: j + 1],
               r2b[:, j * K2: j * K2 + 512],
               j == 0, j == NT - 1, ["zwb", "r2b"], "wgmA")
            mm(PS[2][0:1, 0:K2 - 512], "ps2", zwb[:, j: j + 1],
               r2b[:, j * K2 + 512: j * K2 + K2],
               j == 0, j == NT - 1, ["zwb", "r2b"], "wgmB")
        em.op("v", lambda e: e.tensor_copy(wrow[0:1, 0:512], PS[7][0:1, 0:512]),
              reads=["ps7"], writes=["wrow"], tag="wrevA", wout=512)
        em.op("v", lambda e: e.tensor_copy(wrow[0:1, 512:K2],
                                           PS[2][0:1, 0:K2 - 512]),
              reads=["ps2"], writes=["wrow"], tag="wrevB", wout=512)
        for cc in range(K2 // 128):
            mm(PS[3][:, 0:1], "ps3", wrow[0:1, cc * 128:(cc + 1) * 128],
               ones1f[0:1, 0:1], True, True, ["wrow", "ones1f"], "wgtr")
            em.op("v", lambda e, cc=cc: e.tensor_copy(
                wgc[:, cc: cc + 1], PS[3][:, 0:1]),
                reads=["ps3"], writes=["wgc"], tag="wgtrev")
        for ic in range(NIC):
            sl = ic % 2
            wgR, wuR = f"wgv{sl}", f"wuv_{sl}"
            em.dma("g", lambda e, ei=ei, ic=ic, sl=sl: e.dma_start(
                out=wgv[sl][:], in_=wg_dev[ei, ic]),
                writes=[wgR], slot=wgR, tag="wgld")
            em.dma("g", lambda e, ei=ei, ic=ic, sl=sl: e.dma_start(
                out=wuv_[sl][:], in_=wu_dev[ei, ic]),
                writes=[wuR], slot=wuR, tag="wuld")
            pg, pgR = (PS[0], "ps0") if sl == 0 else (PS[4], "ps4")
            pg2, pg2R = (PS[1], "ps1") if sl == 0 else (PS[5], "ps5")
            pu, puR = (PS[2], "ps2") if sl == 0 else (PS[6], "ps6")
            pu2, pu2R = (PS[3], "ps3") if sl == 0 else (PS[7], "ps7")
            for dc in range(8):
                mm(pg[:, 0:512], pgR, wgv[sl][:, dc * 128:(dc + 1) * 128],
                   xgT[:, dc * K2: dc * K2 + 512],
                   dc == 0, dc == 7, [wgR, "xgT"], "upgA")
                mm(pg2[:, 0:K2 - 512], pg2R, wgv[sl][:, dc * 128:(dc + 1) * 128],
                   xgT[:, dc * K2 + 512: dc * K2 + K2],
                   dc == 0, dc == 7, [wgR, "xgT"], "upgB")
            for dc in range(8):
                mm(pu[:, 0:512], puR, wuv_[sl][:, dc * 128:(dc + 1) * 128],
                   xgT[:, dc * K2: dc * K2 + 512],
                   dc == 0, dc == 7, [wuR, "xgT"], "upuA")
                mm(pu2[:, 0:K2 - 512], pu2R, wuv_[sl][:, dc * 128:(dc + 1) * 128],
                   xgT[:, dc * K2 + 512: dc * K2 + K2],
                   dc == 0, dc == 7, [wuR, "xgT"], "upuB")
            em.op("s", lambda e, pg=pg: e.activation(
                ggx[:, 0:512], pg[:, 0:512], AF.Gelu),
                reads=[pgR], writes=["ggx"], tag="mgeluA", wout=512)
            em.op("s", lambda e, pg2=pg2: e.activation(
                ggx[:, 512:K2], pg2[:, 0:K2 - 512], AF.Gelu),
                reads=[pg2R], writes=["ggx"], tag="mgeluB", wout=512)
            em.op("v", lambda e, ic=ic, pu=pu: e.tensor_tensor(
                hT[:, ic * K2: ic * K2 + 512], ggx[:, 0:512], pu[:, 0:512],
                ALU.mult),
                reads=["ggx", puR], writes=["hT"], tag="hmulA", wout=512)
            em.op("v", lambda e, ic=ic, pu2=pu2: e.tensor_tensor(
                hT[:, ic * K2 + 512: ic * K2 + K2], ggx[:, 512:K2],
                pu2[:, 0:K2 - 512], ALU.mult),
                reads=["ggx", pu2R], writes=["hT"], tag="hmulB", wout=512)
        for dh in range(2):
            for ic in range(NIC):
                sl = ic % 4
                wdR = f"wdv{sl}"
                em.dma("g", lambda e, ei=ei, dh=dh, ic=ic, sl=sl: e.dma_start(
                    out=wdv[sl][:], in_=wd_dev[ei, dh, ic]),
                    writes=[wdR], slot=wdR, tag="wdld")
                for cc in range(K2 // 128):
                    mm(PS[cc][:, 0:512], f"ps{cc}",
                       hT[:, ic * K2 + cc * 128: ic * K2 + cc * 128 + 128],
                       wdv[sl][:],
                       ic == 0, ic == NIC - 1, ["hT", wdR], "dnmm")
            for cc in range(K2 // 128):
                em.op("v", lambda e, cc=cc, dh=dh: e.tensor_scalar(
                    eoT[:, cc * D + dh * 512: cc * D + dh * 512 + 512],
                    PS[cc][:, 0:512], wgc[:, cc: cc + 1], None, ALU.mult),
                    reads=[f"ps{cc}", "wgc"], writes=["eoT"], tag="eoev", wout=512)
        for j in range(NT):
            p = 16 * j + ei
            for cc in range(K2 // 128):
                em.op("v", lambda e, p=p, cc=cc: e.tensor_scalar(
                    q2scr[:], iota_row[:], float(cc * 128),
                    r2tok[:, p: p + 1], ALU.add, ALU.is_equal),
                    reads=["r2tok", "iota_row"], writes=["q2scr"], tag="q2eq", wout=512)
                em.op("t", lambda e: e.transpose(
                    PS[2][:, 0:128], q2scr[:], ident[:]),
                    reads=["q2scr", "ident"], writes=["ps2"], tag="q2tr")
                em.op("v", lambda e, cc=cc: e.tensor_copy(
                    q2b6[:, cc * 128:(cc + 1) * 128], PS[2][:, 0:128]),
                    reads=["ps2"], writes=["q2b6"], tag="q2ev", wout=512)
            ssl = j % 2
            for dh in range(2):
                pb, pbR = (PS[0], "ps0") if dh == 0 else (PS[1], "ps1")
                for cc in range(K2 // 128):
                    mm(pb[:, 0:512], pbR,
                       q2b6[:, cc * 128:(cc + 1) * 128],
                       eoT[:, cc * D + dh * 512: cc * D + dh * 512 + 512],
                       cc == 0, cc == K2 // 128 - 1, ["q2b6", "eoT"], "s2mm")
                em.op("v", lambda e, dh=dh, pb=pb, ssl=ssl: e.tensor_copy(
                    s2ev[:, ssl * D + dh * 512: ssl * D + (dh + 1) * 512],
                    pb[:, 0:512]),
                    reads=[pbR], writes=[f"s2ev{ssl}"], tag="s2evc", wout=512)
            em.dma("sp", lambda e, ei=ei, j=j, ssl=ssl: e.dma_start(
                out=s2h[ei, j], in_=s2ev[:, ssl * D:(ssl + 1) * D]),
                reads=[f"s2ev{ssl}"], writes=[f"h:s2:{ei}:{j}"],
                slot=f"s2ev{ssl}", tag="s2st")

    cstack.close()
    em.new_block()

    dstack = contextlib.ExitStack()
    dsb = mksb(dstack)
    s2sl = [dsb(f"s2sl{i}", (128, D), BF16) for i in range(4)]
    q1all = dsb("q1all", (128, E * TILE), BF16)
    x1ld = dsb("x1ld", (128, 8 * TILE), F32)
    shld = dsb("shld", (128, 8 * TILE), F32)
    x2p = dsb("x2p", (128, 8 * TILE), F32)
    x2tok = dsb("x2tok", (128, 4 * D), F32)
    sqscr = dsb("sqscr", (128, D), F32)
    sttb = dsb("sttb", (128, 4), F32)
    bcq = dsb("bcq", (128, TILE), F32)

    for j in range(NT):
        em.dma_burst("sp", [
            (lambda e, j=j, dc=dc: e.dma_start(
                out=x1ld[:, dc * TILE:(dc + 1) * TILE], in_=x1T_f32[j, dc]),
             (f"h:x1f:{j}",)) for dc in range(8)],
            writes=["x1ld"], slot="x1ld", tag="x1rld")
        em.dma_burst("sp", [
            (lambda e, j=j, dc=dc: e.dma_start(
                out=shld[:, dc * TILE:(dc + 1) * TILE], in_=shTh[j, dc]),
             (f"h:shT:{j}",)) for dc in range(8)],
            writes=["shld"], slot="shld", tag="shrld")
        for ei in range(E):
            p = 16 * j + ei
            for tsub in range(4):
                pb, pbR = (PS[0], "ps0") if tsub % 2 == 0 else (PS[1], "ps1")
                em.op("v", lambda e, p=p, tsub=tsub: e.tensor_scalar(
                    bcq[:, 0:128], iota_row[:],
                    r1tok[:, tsub * 128 + p: tsub * 128 + p + 1], None,
                    ALU.is_equal),
                    reads=["r1tok", "iota_row"], writes=["bcq"], tag="q1eq", wout=512)
                em.op("t", lambda e, pb=pb: e.transpose(
                    pb[:, 0:128], bcq[:, 0:128], ident[:]),
                    reads=["bcq", "ident"], writes=[pbR], tag="q1tr")
                em.op("v", lambda e, ei=ei, tsub=tsub, pb=pb: e.tensor_copy(
                    q1all[:, ei * TILE + tsub * 128:
                           ei * TILE + tsub * 128 + 128],
                    pb[:, 0:128]),
                    reads=[pbR], writes=["q1all"], tag="q1ev", wout=512)
        for ei in range(E):
            sl = ei % 4
            s2R = f"s2sl{sl}"
            em.dma("sp", lambda e, ei=ei, j=j, sl=sl: e.dma_start(
                out=s2sl[sl][:], in_=s2h[ei, j]),
                reads=[f"h:s2:{ei}:{j}"], writes=[s2R], slot=s2R, tag="s2ld")
            for dc in range(8):
                mm(PS[dc][:, 0:TILE], f"ps{dc}",
                   s2sl[sl][:, dc * 128:(dc + 1) * 128],
                   q1all[:, ei * TILE:(ei + 1) * TILE],
                   ei == 0, ei == E - 1, [s2R, "q1all"], "scmm")
        for dc in range(8):
            em.op("v", lambda e, dc=dc: e.scalar_tensor_tensor(
                x2p[:, dc * TILE:(dc + 1) * TILE], PS[dc][:, 0:TILE], 0.0,
                x1ld[:, dc * TILE:(dc + 1) * TILE], ALU.add, ALU.add),
                reads=[f"ps{dc}", "x1ld"], writes=["x2p"], tag="x2a", wout=512)
            em.op("v", lambda e, dc=dc: e.tensor_add(
                x2p[:, dc * TILE:(dc + 1) * TILE],
                x2p[:, dc * TILE:(dc + 1) * TILE],
                shld[:, dc * TILE:(dc + 1) * TILE]),
                reads=["x2p", "shld"], writes=["x2p"], tag="x2b", wout=512)
        for tc in range(4):
            for dc in range(8):
                pb, pbR = (PS[0], "ps0") if dc % 2 == 0 else (PS[1], "ps1")
                em.op("t", lambda e, tc=tc, dc=dc, pb=pb: e.transpose(
                    pb[:, 0:128],
                    x2p[:, dc * TILE + tc * 128: dc * TILE + tc * 128 + 128],
                    ident[:]),
                    reads=["x2p", "ident"], writes=[pbR], tag="x2tr")
                em.op("v", lambda e, tc=tc, dc=dc, pb=pb: e.tensor_copy(
                    x2tok[:, tc * D + dc * 128: tc * D + dc * 128 + 128],
                    pb[:, 0:128]),
                    reads=[pbR], writes=["x2tok"], tag="x2trev", wout=512)
        for tc in range(4):
            em.op("v", lambda e, tc=tc: e.tensor_reduce(
                sttb[:, 0:1], x2tok[:, tc * D:(tc + 1) * D], AX.X, ALU.add),
                reads=["x2tok"], writes=["sttb"], tag="mean1")
            em.op("v", lambda e: e.tensor_scalar(
                sttb[:, 0:1], sttb[:, 0:1], 1.0 / D, None, ALU.mult),
                reads=["sttb"], writes=["sttb"], tag="mean2")
            em.op("v", lambda e, tc=tc: e.tensor_scalar(
                x2tok[:, tc * D:(tc + 1) * D], x2tok[:, tc * D:(tc + 1) * D],
                sttb[:, 0:1], None, ALU.subtract),
                reads=["x2tok", "sttb"], writes=["x2tok"], tag="center", wout=512)
            em.op("s", lambda e, tc=tc: e.activation(
                sqscr[:], x2tok[:, tc * D:(tc + 1) * D], AF.Square,
                accum_out=sttb[:, 1:2]),
                reads=["x2tok"], writes=["sqscr", "sttb"], tag="sqacc", wout=512)
            em.op("v", lambda e: e.tensor_scalar(
                sttb[:, 1:2], sttb[:, 1:2], 1.0 / D, EPS, ALU.mult, ALU.add),
                reads=["sttb"], writes=["sttb"], tag="var2")
            em.op("s", lambda e: e.activation(
                sttb[:, 2:3], sttb[:, 1:2], AF.Ln),
                reads=["sttb"], writes=["sttb"], tag="lnv2")
            em.op("s", lambda e: e.activation(
                sttb[:, 2:3], sttb[:, 2:3], AF.Exp, scale=-0.5),
                reads=["sttb"], writes=["sttb"], tag="rstd2")
            em.op("v", lambda e, tc=tc: e.tensor_scalar(
                x2tok[:, tc * D:(tc + 1) * D], x2tok[:, tc * D:(tc + 1) * D],
                sttb[:, 2:3], None, ALU.mult),
                reads=["x2tok", "sttb"], writes=["x2tok"], tag="lnap2", wout=512)
            em.dma("sp", lambda e, j=j, tc=tc: e.dma_start(
                out=out[j * TILE + tc * 128: j * TILE + tc * 128 + 128, :],
                in_=x2tok[:, tc * D:(tc + 1) * D]),
                reads=["x2tok"], writes=[f"h:out:{j}:{tc}"], slot="x2tok",
                tag="outst")
    if DEV_DUMPS:
        em.dma("sp", lambda e: e.dma_start(
            out=d_x1.tensor.ap(), in_=x1T_f32.tensor.ap()),
            reads=[f"h:x1f:{j}" for j in range(NT)], writes=["h:d1"],
            slot="dev", policy="total", tag="dump")
        em.dma("sp", lambda e: e.dma_start(
            out=d_sh.tensor.ap(), in_=shTh.tensor.ap()),
            reads=[f"h:shT:{j}" for j in range(NT)], writes=["h:d2"],
            slot="dev", policy="total", tag="dump")
        em.dma("sp", lambda e: e.dma_start(
            out=d_sc.tensor.ap(), in_=sc_outh.ap()),
            reads=["h:sc_out"], writes=["h:d3"], slot="dev", policy="total",
            tag="dump")
        em.dma("sp", lambda e: e.dma_start(out=d_th.tensor.ap(), in_=th[:]),
               reads=["th"], writes=["h:d4"], slot="dev", policy="total",
               tag="dump")
        em.dma("sp", lambda e: e.dma_start(out=d_rt.tensor.ap(), in_=rt[:]),
               reads=["rt"], writes=["h:d5"], slot="dev", policy="total",
               tag="dump")
        em.dma("g", lambda e: e.dma_start(
            out=d_s2.tensor.ap(), in_=s2h.tensor.ap()),
            reads=[f"h:s2:{ei}:{j}" for ei in range(E) for j in range(NT)],
            writes=["h:d6"], slot="dev2", policy="total", tag="dump")
        em.op("sp", lambda e: e.nop(),
              reads=["h:d1", "h:d2", "h:d3", "h:d4", "h:d5"], tag="findev")
        em.op("g", lambda e: e.nop(), reads=["h:d6"], tag="findev2")
    em.op("sp", lambda e: e.nop(),
          reads=[f"h:out:{j}:{tc}" for j in range(NT) for tc in range(4)],
          tag="fin")

    dstack.close()
    em.emit()
    lgstack.close()
    gstack.close()
    return nc


_NC_CACHE = None


def _get_nc():
    global _NC_CACHE
    if _NC_CACHE is None:
        _NC_CACHE = build_program()
    return _NC_CACHE


def _host_layout(inputs):
    f32 = np.float32

    def c(a):
        return np.ascontiguousarray(a, dtype=f32)

    W_dq = np.asarray(inputs["L0_attn_W_dq"])
    W_uq = np.asarray(inputs["L0_attn_W_uq"])
    W_dkv = np.asarray(inputs["L0_attn_W_dkv"])
    W_uk = np.asarray(inputs["L0_attn_W_uk"])
    W_uv = np.asarray(inputs["L0_attn_W_uv"])
    W_o = np.asarray(inputs["L0_attn_W_o"])
    gate_w = np.asarray(inputs["L0_ffn_gate_w"])
    Wg = np.asarray(inputs["L0_ffn_Wg"])
    Wu = np.asarray(inputs["L0_ffn_Wu"])
    Wd = np.asarray(inputs["L0_ffn_Wd"])
    Sg = np.asarray(inputs["L0_ffn_Sg"])
    Su = np.asarray(inputs["L0_ffn_Su"])
    Sd = np.asarray(inputs["L0_ffn_Sd"])

    shared = {
        "wdq": c(W_dq.reshape(4, 128, D)),
        "wdqT": c(W_dq.T.reshape(8, 128, QL)),
        "wuq": c(W_uq.reshape(8, 128, QL)),
        "wuqT": c(W_uq.T.reshape(4, 128, D)),
        "wuk": c(W_uk.reshape(8, 128, L)),
        "wuv": c(W_uv.reshape(8, 128, L)),
        "woT": c(W_o.T.reshape(8, 128, D)),
        "wdkvT": c(W_dkv.T.reshape(8, 128, L)),
        "gwT": c(gate_w.T.reshape(8, 128, E)),
        "sg_dev": c(Sg.T.reshape(8, 128, NISC, 128).transpose(2, 1, 0, 3)
                    .reshape(NISC, 128, 1024)),
        "su_dev": c(Su.T.reshape(8, 128, NISC, 128).transpose(2, 1, 0, 3)
                    .reshape(NISC, 128, 1024)),
        "sd_dev": c(Sd.T.reshape(NISC, 128, 1024)),
        "wg_dev": c(np.stack([
            Wg[e_].T.reshape(8, 128, NIC, 128).transpose(2, 1, 0, 3)
            .reshape(NIC, 128, 1024) for e_ in range(E)])),
        "wu_dev": c(np.stack([
            Wu[e_].T.reshape(8, 128, NIC, 128).transpose(2, 1, 0, 3)
            .reshape(NIC, 128, 1024) for e_ in range(E)])),
        "wd_dev": c(np.stack([
            Wd[e_].T.reshape(NIC, 128, 2, 512).transpose(2, 0, 1, 3)
            for e_ in range(E)])),
    }
    return shared


def kernel(**inputs):
    x = np.asarray(inputs["L0_attn_x"])
    nc = _get_nc()
    shared = _host_layout(inputs)
    in_maps = []
    for cid in range(NCORE):
        m = dict(shared)
        m["xT"] = np.ascontiguousarray(x[cid].T.reshape(8, 128, T),
                                       dtype=np.float32)
        in_maps.append(m)
    res = run_bass_kernel_spmd(nc, in_maps, core_ids=list(range(NCORE)))
    return np.stack([res.results[cid]["out"] for cid in range(NCORE)], axis=0)


if __name__ == "__main__":
    build_program()
    print("build ok")


# version 08 (final)
# speedup vs baseline: 1.0320x; 1.0208x over previous
import contextlib
import math

import numpy as np
import ml_dtypes

import concourse.bass as bass
import concourse.mybir as mybir
from concourse.bass_utils import run_bass_kernel_spmd

F32 = mybir.dt.float32
BF16 = mybir.dt.bfloat16
AF = mybir.ActivationFunctionType
ALU = mybir.AluOpType
AX = mybir.AxisListType

NCORE = 8
T = 4096
D = 1024
NH = 16
HS = 64
L = 512
QL = 512
TILE = 512
NT = T // TILE
E = 16
IFF = 4096
ISH = 2048
NIC = IFF // 128
NISC = ISH // 128
K2 = 640
K1 = 128
CGLOB = 4096
EPS = 1e-5
SCALE = 1.0 / math.sqrt(HS)
DEV_DUMPS = False


class _Op:
    __slots__ = ("eng", "fn", "reads", "writes", "slot", "dma_val", "deps",
                 "needs_inc", "inc_val", "block", "kind", "tag", "_waits",
                 "wout", "eng_seq", "_drain")

    def __init__(self, eng, fn, reads, writes, slot, block, kind, tag):
        self.eng = eng
        self.fn = fn
        self.reads = reads
        self.writes = writes
        self.slot = slot
        self.dma_val = 0
        self.deps = []
        self.needs_inc = False
        self.inc_val = 0
        self.block = block
        self.kind = kind
        self.tag = tag
        self.wout = 0
        self.eng_seq = 0


class Em:
    ENGS = ("t", "v", "s", "g", "sp")

    def __init__(self, nc, es):
        self.nc = nc
        self.ops = []
        self.blocks = 1
        self.last_writer = {}
        self.readers = {}
        self.slot_counts = {}
        self.slot_policy = {}
        self.exec_sems = {e: es.enter_context(nc.semaphore(f"ex_{e}"))
                         for e in self.ENGS}
        self.slot_sems = {}
        self.cc_sem = es.enter_context(nc.semaphore("cc_sem"))
        self.cc_count = 0
        self._eseq = {}
        self._es = es

    def op(self, eng, fn, reads=(), writes=(), tag="", wout=0):
        o = self._rec(eng, fn, reads, writes, None, "op", tag)
        o.wout = wout

    def dma(self, eng, fn, reads=(), writes=(), slot=None, policy="seq", tag=""):
        assert slot is not None
        if slot not in self.slot_sems:
            self.slot_sems[slot] = self._es.enter_context(
                self.nc.semaphore(f"dm_{slot}"))
        self.slot_policy.setdefault(slot, policy)
        assert self.slot_policy[slot] == policy, slot
        self._rec(eng, fn, reads, writes, slot, "dma", tag)

    def cc(self, fn, reads=(), writes=(), tag="cc"):
        self._rec("g", fn, reads, writes, None, "cc", tag)

    def dma_burst(self, eng, fns_reads, writes, slot, policy="seq", tag=""):
        assert slot is not None
        if slot not in self.slot_sems:
            self.slot_sems[slot] = self._es.enter_context(
                self.nc.semaphore(f"dm_{slot}"))
        self.slot_policy.setdefault(slot, policy)
        prev_deps = []
        for w_ in writes:
            pw = self.last_writer.get(w_)
            if pw is not None:
                prev_deps.append((pw, False))
            prev_deps.extend((rd, False) for rd in self.readers.get(w_, ()))
        ops = []
        for fn, reads in fns_reads:
            o = _Op(eng, fn, tuple(reads), (), slot, self.blocks, "dma", tag)
            self.slot_counts[slot] = self.slot_counts.get(slot, 0) + 1
            o.dma_val = 16 * self.slot_counts[slot]
            o.deps.extend(prev_deps)
            for r in o.reads:
                w = self.last_writer.get(r)
                if w is not None:
                    o.deps.append((w, True))
                self.readers.setdefault(r, []).append(o)
            o.eng_seq = self._eseq.get(eng, 0)
            self._eseq[eng] = o.eng_seq + 1
            self.ops.append(o)
            ops.append(o)
        for w_ in writes:
            self.last_writer[w_] = ops[-1]
            self.readers[w_] = []

    def _rec(self, eng, fn, reads, writes, slot, kind, tag):
        o = _Op(eng, fn, tuple(reads), tuple(writes), slot, self.blocks, kind, tag)
        if kind == "dma":
            self.slot_counts[slot] = self.slot_counts.get(slot, 0) + 1
            o.dma_val = 16 * self.slot_counts[slot]
        elif kind == "cc":
            self.cc_count += 1
            o.dma_val = self.cc_count
        deps = o.deps
        for r in o.reads:
            w = self.last_writer.get(r)
            if w is not None:
                deps.append((w, True))
            self.readers.setdefault(r, []).append(o)
        for w_ in o.writes:
            pw = self.last_writer.get(w_)
            if pw is not None:
                deps.append((pw, False))
            for rd in self.readers.get(w_, ()):
                if rd is not o:
                    deps.append((rd, False))
            self.last_writer[w_] = o
            self.readers[w_] = [o] if w_ in o.reads else []
        o.eng_seq = self._eseq.get(eng, 0)
        self._eseq[eng] = o.eng_seq + 1
        self.ops.append(o)
        return o

    def new_block(self):
        self.blocks += 1

    def emit(self):
        for o in self.ops:
            for d, _raw in o.deps:
                if d.kind == "op" and d.eng != o.eng:
                    d.needs_inc = True
        counts = {e: 0 for e in self.ENGS}
        for o in self.ops:
            if o.kind == "op" and o.needs_inc:
                counts[o.eng] += 1
                o.inc_val = counts[o.eng]
        for o in self.ops:
            waits = {}
            need_drain = 0
            for d, raw in o.deps:
                if d is o:
                    continue
                if d.kind == "dma":
                    sem = self.slot_sems[d.slot]
                    val = (16 * self.slot_counts[d.slot]
                           if self.slot_policy[d.slot] == "total" else d.dma_val)
                elif d.kind == "cc":
                    sem, val = self.cc_sem, d.dma_val
                else:
                    if d.eng == o.eng:
                        if (raw and o.kind == "op" and d.kind == "op"
                                and d.wout < 256):
                            need_drain = max(need_drain, d.eng_seq + 1)
                        continue
                    sem, val = self.exec_sems[d.eng], d.inc_val
                k = sem.name
                if k not in waits or waits[k][1] < val:
                    waits[k] = (sem, val)
            o._waits = list(waits.values())
            o._drain = need_drain
        nc = self.nc
        watermark = {e: {} for e in self.ENGS}
        drained = {}
        for b in range(1, self.blocks + 1):
            ops_b = [o for o in self.ops if o.block == b]
            if not ops_b:
                continue
            per_eng = {e: [o for o in ops_b if o.eng == e] for e in self.ENGS}
            with nc.Block() as block:
                for ename, section in (("g", block.gpsimd), ("s", block.scalar),
                                       ("t", block.tensor), ("v", block.vector),
                                       ("sp", block.sync)):
                    lst = per_eng[ename]
                    if not lst:
                        continue

                    def body(eng, lst=lst, ename=ename):
                        wm = watermark[ename]
                        for o in lst:
                            if getattr(o, "_drain", 0) > drained.get(ename, 0):
                                eng.drain()
                                drained[ename] = o.eng_seq
                            for sem, val in o._waits:
                                if wm.get(sem.name, -1) >= val:
                                    continue
                                eng.wait_ge(sem, val)
                                wm[sem.name] = val
                            inst = o.fn(eng)
                            if o.kind == "dma":
                                inst.then_inc(self.slot_sems[o.slot], 16)
                            elif o.kind == "cc":
                                inst.then_inc(self.cc_sem, 1)
                            elif o.needs_inc:
                                inst.then_inc(self.exec_sems[ename], 1)
                    section(body)


def build_program():
    nc = bass.Bass("TRN2", target_bir_lowering=False)

    def din(name, shape, dtype=F32):
        return nc.dram_tensor(name, list(shape), dtype, kind="ExternalInput").ap()

    xT = din("xT", (8, 128, T))
    wdq = din("wdq", (4, 128, D))
    wdqT = din("wdqT", (8, 128, QL))
    wuq = din("wuq", (8, 128, QL))
    wuqT = din("wuqT", (4, 128, D))
    wuk = din("wuk", (8, 128, L))
    wuv = din("wuv", (8, 128, L))
    woT = din("woT", (8, 128, D))
    wdkvT = din("wdkvT", (8, 128, L))
    gwT = din("gwT", (8, 128, E))
    sg_dev = din("sg_dev", (NISC, 128, 1024))
    su_dev = din("su_dev", (NISC, 128, 1024))
    sd_dev = din("sd_dev", (NISC, 128, 1024))
    wg_dev = din("wg_dev", (E, NIC, 128, 1024))
    wu_dev = din("wu_dev", (E, NIC, 128, 1024))
    wd_dev = din("wd_dev", (E, 2, NIC, 128, 512))

    out = nc.dram_tensor("out", [T, D], F32, kind="ExternalOutput").ap()
    if DEV_DUMPS:
        d_x1 = nc.dram_tensor("d_x1", [NT, 8, 128, TILE], F32,
                              kind="ExternalOutput").ap()
        d_sh = nc.dram_tensor("d_sh", [NT, 8, 128, TILE], F32,
                              kind="ExternalOutput").ap()
        d_sc = nc.dram_tensor("d_sc", [128, T], F32,
                              kind="ExternalOutput").ap()
        d_th = nc.dram_tensor("d_th", [128, 8], F32,
                              kind="ExternalOutput").ap()
        d_rt = nc.dram_tensor("d_rt", [128, 6 * TILE], F32,
                              kind="ExternalOutput").ap()
        d_s2 = nc.dram_tensor("d_s2", [E, NT, 128, D], F32,
                              kind="ExternalOutput").ap()

    ckvTh = nc.dram_tensor("ckvTh", [NT, 128, 4 * TILE], BF16).ap()
    ckvKh = nc.dram_tensor("ckvKh", [NT, 128, 4 * TILE], BF16).ap()
    x1T_f32 = nc.dram_tensor("x1T_f32", [NT, 8, 128, TILE], F32).ap()
    x1T_b16 = nc.dram_tensor("x1T_b16", [NT, 8, 128, TILE], BF16).ap()
    x1tokh = nc.dram_tensor("x1tokh", [NT, 4, 128, D], BF16).ap()
    shTh = nc.dram_tensor("shTh", [NT, 8, 128, TILE], F32).ap()
    s2h = nc.dram_tensor("s2h", [E, NT, 128, D], BF16).ap()
    sc_inh = nc.dram_tensor("sc_inh", [E, T], F32)
    sc_outh = nc.dram_tensor("sc_outh", [128, T], F32)

    def const(name, arr):
        return nc.inline_tensor(np.ascontiguousarray(arr), name)

    c_ones_bf = const("c_ones_bf", np.ones((128, 1), ml_dtypes.bfloat16))
    c_ones1f = const("c_ones1f", np.ones((128, 1), np.float32))
    c_onesr_f = const("c_onesr_f", np.ones((1, 128), np.float32))
    c_ident = const("c_ident", np.eye(128, dtype=np.float32))
    c_iota_row = const("c_iota_row",
                       np.broadcast_to(np.arange(128, dtype=np.float32),
                                       (128, 128)))
    c_iota768 = const("c_iota768",
                      np.broadcast_to(np.arange(K2, dtype=np.float32), (128, K2)))
    mk_np = np.zeros((128, 4 * TILE), np.float32)
    for kc in range(4):
        kk = kc * 128 + np.arange(128)[:, None]
        qq = np.arange(TILE)[None, :]
        mk_np[:, kc * TILE:(kc + 1) * TILE] = np.where(kk <= qq, 0.0, -1.0e9)
    c_mask = const("c_mask", mk_np)
    g2_np = (np.arange(128)[None, :] % 16 == np.arange(16)[:, None]).astype(
        np.float32)
    c_g2 = const("c_g2", g2_np)
    c_g16 = const("c_g16", g2_np.T.copy())
    hm = np.zeros((128, 2), np.float32)
    hm[0:64, 0] = 1.0
    hm[64:128, 1] = 1.0
    c_halfmask = const("c_halfmask", hm)
    pj = np.arange(128)
    trij_np = ((pj[None, :] % 16 == pj[:, None] % 16)
               & (pj[:, None] // 16 < pj[None, :] // 16)).astype(np.float32)
    c_trij = const("c_trij", trij_np)

    gstack = contextlib.ExitStack()

    def mksb(stack):
        def sb(name, shape, dtype):
            return stack.enter_context(nc.sbuf_tensor(name, list(shape), dtype))
        return sb

    gsb = mksb(gstack)

    ones_bf = gsb("ones_bf", (128, 1), BF16)
    ones1f = gsb("ones1f", (128, 1), F32)
    onesr_f = gsb("onesr_f", (1, 128), F32)
    ident = gsb("ident", (128, 128), F32)
    iota_row = gsb("iota_row", (128, 128), F32)
    sco = gsb("sco", (128, TILE), F32)

    PS = [gstack.enter_context(nc.psum_tensor(f"ps{i}", [128, 512], F32))
          for i in range(8)]

    em = Em(nc, gstack)

    def mm(ps_ap, ps_reg, lhsT_ap, rhs_ap, start, stop, reads, tag=""):
        em.op("t", lambda e: e.matmul(ps_ap, lhsT_ap, rhs_ap,
                                      start=start, stop=stop),
              reads=reads, writes=[ps_reg], tag=tag)

    def ev(dst_ap, src_ap, reads, writes, tag="", wout=0):
        em.op("v", lambda e: e.tensor_copy(dst_ap, src_ap),
              reads=reads, writes=writes, tag=tag, wout=wout)

    astack = contextlib.ExitStack()
    asb = mksb(astack)

    maskt = asb("maskt", (128, 4 * TILE), F32)
    halfmask = asb("halfmask", (128, 2), F32)
    kerow2 = asb("kerow2", (128, 16), BF16)
    ch16row = asb("ch16row", (1, 16), F32)
    KEs = asb("KEs", (128, NH * 4 * 128), BF16)
    VEb = asb("VEb", (128, 4 * D), BF16)
    wdkvT_b = asb("wdkvT_b", (128, 8 * L), BF16)
    wdqT_b = asb("wdqT_b", (128, 8 * QL), BF16)
    wuqT_b = asb("wuqT_b", (128, 4 * D), BF16)
    gwT_b = asb("gwT_b", (128, 8 * E), BF16)

    p0stack = contextlib.ExitStack()
    p0sb = mksb(p0stack)
    wuq_b = p0sb("wuq_b", (128, 8 * QL), BF16)
    wuk_b = p0sb("wuk_b", (128, 8 * L), BF16)
    wuv_b = p0sb("wuv_b", (128, 8 * L), BF16)
    woT_b = p0sb("woT_b", (128, 8 * D), BF16)
    wdq_b = p0sb("wdq_b", (128, 4 * D), BF16)
    Mb = p0sb("Mb", (128, 4 * L), BF16)
    sq_p0 = p0sb("sq_p0", (128, L), BF16)
    kerowc = p0sb("kerowc", (128, 8), F32)
    KEb = p0sb("KEb", (128, 8 * L), BF16)

    _cl = [(ones_bf, c_ones_bf, "ones_bf"), (ones1f, c_ones1f, "ones1f"),
           (onesr_f, c_onesr_f, "onesr_f"), (ident, c_ident, "ident"),
           (iota_row, c_iota_row, "iota_row"),
           (maskt, c_mask, "maskt"), (halfmask, c_halfmask, "halfmask")]
    em.dma_burst("sp", [(lambda e, d=d_, s=s_: e.dma_start(out=d[:], in_=s.ap()),
                         ()) for d_, s_, _ in _cl],
                 writes=[r for _, _, r in _cl], slot="p0c", policy="total",
                 tag="cload")

    _wl = []
    _wlregs = set()
    for dc in range(8):
        for dst_ap, src_ap, reg in (
                (wuq_b[:, dc * QL:(dc + 1) * QL], wuq[dc], "wuq_b"),
                (wuk_b[:, dc * L:(dc + 1) * L], wuk[dc], "wuk_b"),
                (wuv_b[:, dc * L:(dc + 1) * L], wuv[dc], "wuv_b"),
                (woT_b[:, dc * D:(dc + 1) * D], woT[dc], "woT_b"),
                (wdkvT_b[:, dc * L:(dc + 1) * L], wdkvT[dc], "wdkvT_b"),
                (wdqT_b[:, dc * QL:(dc + 1) * QL], wdqT[dc], "wdqT_b"),
                (gwT_b[:, dc * E:(dc + 1) * E], gwT[dc], "gwT_b")):
            _wl.append((lambda e, d=dst_ap, s=src_ap: e.dma_start(out=d, in_=s),
                        ()))
            _wlregs.add(reg)
    for qc in range(4):
        for dst_ap, src_ap, reg in (
                (wdq_b[:, qc * D:(qc + 1) * D], wdq[qc], "wdq_b"),
                (wuqT_b[:, qc * D:(qc + 1) * D], wuqT[qc], "wuqT_b")):
            _wl.append((lambda e, d=dst_ap, s=src_ap: e.dma_start(out=d, in_=s),
                        ()))
            _wlregs.add(reg)
    em.dma_burst("g", _wl, writes=sorted(_wlregs), slot="p0w", policy="total",
                 tag="wload")

    for qlc in range(4):
        for dc in range(8):
            mm(PS[0][:, 0:L], "ps0",
               wuq_b[:, dc * QL + qlc * 128: dc * QL + qlc * 128 + 128],
               wuk_b[:, dc * L:(dc + 1) * L],
               dc == 0, dc == 7, ["wuq_b", "wuk_b"], "Mmm")
        ev(Mb[:, qlc * L:(qlc + 1) * L], PS[0][:, 0:L], ["ps0"], ["Mb"], "Mev", wout=512)
    for dc in range(8):
        for qlc in range(4):
            mm(PS[1][:, 0:L], "ps1",
               wdq_b[:, qlc * D + dc * 128: qlc * D + dc * 128 + 128],
               Mb[:, qlc * L:(qlc + 1) * L],
               qlc == 0, qlc == 3, ["wdq_b", "Mb"], "KEmm")
        ev(KEb[:, dc * L:(dc + 1) * L], PS[1][:, 0:L], ["ps1"], ["KEb"], "KEev", wout=512)
    for dc in range(8):
        em.op("v", lambda e, dc=dc: e.scalar_tensor_tensor(
            sq_p0[:], KEb[:, dc * L:(dc + 1) * L], -1.0,
            KEb[:, dc * L:(dc + 1) * L], ALU.mult, ALU.max),
            reads=["KEb"], writes=["sq_p0"], tag="keabs", wout=512)
        em.op("v", lambda e, dc=dc: e.tensor_reduce(
            kerowc[:, dc: dc + 1], sq_p0[:], AX.X, ALU.add),
            reads=["sq_p0"], writes=["kerowc"], tag="kerow")
    for dc in range(8):
        for r_ in range(2):
            em.op("v", lambda e, dc=dc, r_=r_: e.tensor_mul(
                kerow2[:, 2 * dc + r_: 2 * dc + r_ + 1],
                halfmask[:, r_: r_ + 1], kerowc[:, dc: dc + 1]),
                reads=["halfmask", "kerowc"], writes=["kerow2"], tag="ker2")
    em.op("v", lambda e: e.memset(KEs[:], 0.0), writes=["KEs"], tag="KEsz")
    for h in range(NH):
        dc = h // 2
        r0 = 64 * (h % 2)
        for lc in range(4):
            ev(KEs[r0:r0 + 64, (h * 4 + lc) * 128:(h * 4 + lc) * 128 + 128],
               KEb[r0:r0 + 64, dc * L + lc * 128: dc * L + lc * 128 + 128],
               ["KEb"], ["KEs"], "KEsc")
    for lc in range(4):
        for half in range(2):
            for dc in range(8):
                mm(PS[2][:, 0:512], "ps2",
                   wuv_b[:, dc * L + lc * 128: dc * L + lc * 128 + 128],
                   woT_b[:, dc * D + half * 512: dc * D + half * 512 + 512],
                   dc == 0, dc == 7, ["wuv_b", "woT_b"], "VEmm")
            ev(VEb[:, lc * D + half * 512: lc * D + half * 512 + 512],
               PS[2][:, 0:512], ["ps2"], ["VEb"], "VEev", wout=512)

    p0stack.close()
    em.new_block()

    xt_f = [asb(f"xt_f{i}", (128, 8 * TILE), F32) for i in range(2)]
    xt_b = asb("xt_b", (128, 8 * TILE), BF16)
    ckvstT = asb("ckvstT", (128, 4 * TILE), BF16)
    ckvstK = asb("ckvstK", (128, 4 * TILE), BF16)
    ckvT_s = [asb(f"ckvT_s{i}", (128, 4 * TILE), BF16) for i in range(3)]
    ckvK_s = [asb(f"ckvK_s{i}", (128, 4 * TILE), BF16) for i in range(3)]
    qx_b = asb("qx_b", (128, 4 * TILE), BF16)
    q_b = asb("q_b", (128, 8 * TILE), BF16)
    qlat = [asb(f"qlat{i}", (128, 4 * TILE), BF16) for i in range(2)]
    qla = asb("qla", (128, TILE), BF16)
    pT = [asb(f"pT{i}", (128, TILE), BF16) for i in range(3)]
    ylat_b = asb("ylat_b", (128, 4 * TILE), BF16)
    yT = asb("yT", (128, 8 * TILE), F32)
    x1b = asb("x1b", (128, 8 * TILE), BF16)
    x1tok_sb = asb("x1tok_sb", (128, 2 * D), BF16)
    bca = asb("bca", (128, TILE), F32)
    bcb = asb("bcb", (128, TILE), F32)
    sq_t = asb("sq_t", (128, TILE), F32)
    srow = asb("srow", (1, 2 * TILE + 16), F32)
    s1row = asb("s1row", (1, TILE), F32)
    cmaxb = asb("cmaxb", (1, 8), F32)
    cmcol = asb("cmcol", (128, 2), F32)
    chb = asb("chb", (128, 1), F32)
    smt = asb("smt", (16, TILE), F32)
    scn = asb("scn", (16, TILE), F32)

    def p1_xload(j):
        cur = xt_f[j % 2]
        curR = f"xt_f{j % 2}"
        em.dma_burst("sp", [
            (lambda e, j=j, dc=dc, cur=cur: e.dma_start(
                out=cur[:, dc * TILE:(dc + 1) * TILE],
                in_=xT[dc, :, j * TILE:(j + 1) * TILE]), ())
            for dc in range(8)], writes=[curR], slot=curR, tag="xload")

    def p1_xcast(j):
        cur = xt_f[j % 2]
        curR = f"xt_f{j % 2}"
        for dc in range(8):
            ev(xt_b[:, dc * TILE:(dc + 1) * TILE],
               cur[:, dc * TILE:(dc + 1) * TILE], [curR], ["xt_b"], "xcast", wout=512)

    def p1_ckv(j):
        cur = xt_f[j % 2]
        curR = f"xt_f{j % 2}"
        for lc in range(4):
            for dc in range(8):
                mm(PS[4][:, 0:TILE], "ps4",
                   wdkvT_b[:, dc * L + lc * 128: dc * L + lc * 128 + 128],
                   xt_b[:, dc * TILE:(dc + 1) * TILE],
                   dc == 0, dc == 7, ["wdkvT_b", "xt_b"], "ckvT")
            ev(ckvstT[:, lc * TILE:(lc + 1) * TILE], PS[4][:, 0:TILE],
               ["ps4"], ["ckvstT"], "ckvTev", wout=512)
        em.dma("sp", lambda e, j=j: e.dma_start(out=ckvTh[j], in_=ckvstT[:]),
               reads=["ckvstT"], writes=[f"h:ckvT:{j}"], slot="ckvstT",
               tag="ckvTst")
        for tc in range(4):
            for dc in range(8):
                mm(PS[5][:, 0:L], "ps5",
                   xt_b[:, dc * TILE + tc * 128: dc * TILE + tc * 128 + 128],
                   wdkvT_b[:, dc * L:(dc + 1) * L],
                   dc == 0, dc == 7, ["xt_b", "wdkvT_b"], "ckvK")
            ev(ckvstK[:, tc * L:(tc + 1) * L], PS[5][:, 0:L],
               ["ps5"], ["ckvstK"], "ckvKev", wout=512)
        em.dma("sp", lambda e, j=j: e.dma_start(out=ckvKh[j], in_=ckvstK[:]),
               reads=["ckvstK"], writes=[f"h:ckvK:{j}"], slot="ckvstK",
               tag="ckvKst")
        em.op("v", lambda e: e.tensor_reduce(
            cmcol[:, 0:1], ckvstT[:], AX.X, ALU.max,
            apply_absolute_value=True),
            reads=["ckvstT"], writes=["cmcol"], tag="cabs")
        mm(PS[6][0:1, 0:128], "ps6", cmcol[:, 0:1], ident[:],
           True, True, ["cmcol", "ident"], "cmtr")
        em.op("v", lambda e: e.tensor_reduce(
            cmaxb[0:1, 1:2], PS[6][0:1, 0:128], AX.X, ALU.max),
            reads=["ps6"], writes=["cmaxb1"], tag="cmred")
        if j == 0:
            ev(cmaxb[0:1, 0:1], cmaxb[0:1, 1:2], ["cmaxb1"], ["cmaxb"], "cm0")
        else:
            em.op("v", lambda e: e.tensor_max(
                cmaxb[0:1, 0:1], cmaxb[0:1, 0:1], cmaxb[0:1, 1:2]),
                reads=["cmaxb", "cmaxb1"], writes=["cmaxb"], tag="cmu")

    def p1_qchain(j):
        cur = xt_f[j % 2]
        curR = f"xt_f{j % 2}"
        for qlc in range(4):
            for dc in range(8):
                mm(PS[6][:, 0:TILE], "ps6",
                   wdqT_b[:, dc * QL + qlc * 128: dc * QL + qlc * 128 + 128],
                   xt_b[:, dc * TILE:(dc + 1) * TILE],
                   dc == 0, dc == 7, ["wdqT_b", "xt_b"], "qx")
            ev(qx_b[:, qlc * TILE:(qlc + 1) * TILE], PS[6][:, 0:TILE],
               ["ps6"], ["qx_b"], "qxev", wout=512)
        for dc in range(8):
            for qlc in range(4):
                mm(PS[7][:, 0:TILE], "ps7",
                   wuqT_b[:, qlc * D + dc * 128: qlc * D + dc * 128 + 128],
                   qx_b[:, qlc * TILE:(qlc + 1) * TILE],
                   qlc == 0, qlc == 3, ["wuqT_b", "qx_b"], "qmm")
            ev(q_b[:, dc * TILE:(dc + 1) * TILE], PS[7][:, 0:TILE],
               ["ps7"], ["q_b"], "qev", wout=512)

    def p1_bounds(j):
        cur = xt_f[j % 2]
        curR = f"xt_f{j % 2}"
        em.op("v", lambda e: e.tensor_scalar(
            cmaxb[0:1, 3:4], cmaxb[0:1, 0:1], -SCALE, None, ALU.mult),
            reads=["cmaxb"], writes=["cmaxb3"], tag="cmneg")
        for dc in range(8):
            em.op("v", lambda e, dc=dc: e.scalar_tensor_tensor(
                qla[:], q_b[:, dc * TILE:(dc + 1) * TILE], -1.0,
                q_b[:, dc * TILE:(dc + 1) * TILE], ALU.mult, ALU.max),
                reads=["q_b"], writes=["qla"], tag="qabs", wout=512)
            for r_ in range(2):
                mm(PS[3][0:1, 0:TILE], "ps3",
                   kerow2[:, 2 * dc + r_: 2 * dc + r_ + 1], qla[:],
                   True, True, ["qla", "kerow2"], "bndmm")
                em.op("v", lambda e, dc=dc, r_=r_: e.tensor_reduce(
                    ch16row[0:1, 2 * dc + r_: 2 * dc + r_ + 1],
                    PS[3][0:1, 0:TILE], AX.X, ALU.max),
                    reads=["ps3"], writes=["ch16row"], tag="bndred")

    def p1_mkhead(j):
        def emit_qlat(h):
            ql_s = qlat[h % 2]
            qlR = f"qlat{h % 2}"
            dch = h // 2
            for lc in range(4):
                mm(PS[6][:, 0:TILE], "ps6",
                   KEs[:, (h * 4 + lc) * 128:(h * 4 + lc) * 128 + 128],
                   q_b[:, dch * TILE:(dch + 1) * TILE],
                   True, True, ["KEs", "q_b"], "qlat")
                em.op("v", lambda e, lc=lc, ql_s=ql_s: e.tensor_copy(
                    ql_s[:, lc * TILE:(lc + 1) * TILE], PS[6][:, 0:TILE]),
                    reads=["ps6"], writes=[qlR], tag="qlatev", wout=512)

        def emit_ch(h):
            em.op("v", lambda e, h=h: e.tensor_tensor(
                cmaxb[0:1, 4:5], ch16row[0:1, h: h + 1], cmaxb[0:1, 3:4],
                ALU.mult),
                reads=["ch16row", "cmaxb3"], writes=["cmaxb4"], tag="chmul")
            mm(PS[3][:, 0:1], "ps3", onesr_f[:], cmaxb[0:1, 4:5], True, True,
               ["cmaxb4", "onesr_f"], "chbc")
            ev(chb[:, 0:1], PS[3][:, 0:1], ["ps3"], ["chb"], "chbev")

        def emit_kloop(h, j=j):
            ql_s = qlat[h % 2]
            qlR = f"qlat{h % 2}"
            nkc = j + 1
            nkg = nkc * 4

            def emit_S(kg):
                kc, ksc = kg // 4, kg % 4
                sl = kc % 3
                ctR = f"ckvT_s{sl}"
                if ksc == 0:
                    em.dma("sp", lambda e, kc=kc, sl=sl: e.dma_start(
                        out=ckvT_s[sl][:], in_=ckvTh[kc]),
                        reads=[f"h:ckvT:{kc}"], writes=[ctR], slot=ctR,
                        tag="ckvTld")
                    em.dma("g", lambda e, kc=kc, sl=sl: e.dma_start(
                        out=ckvK_s[sl][:], in_=ckvKh[kc]),
                        reads=[f"h:ckvK:{kc}"], writes=[f"ckvK_s{sl}"],
                        slot=f"ckvK_s{sl}", tag="ckvKld")
                sps = PS[4] if (kg % 2 == 0) else PS[5]
                spsR = "ps4" if (kg % 2 == 0) else "ps5"
                for lc in range(4):
                    mm(sps[:, 0:TILE], spsR,
                       ckvT_s[sl][:, lc * TILE + ksc * 128:
                                   lc * TILE + ksc * 128 + 128],
                       ql_s[:, lc * TILE:(lc + 1) * TILE],
                       lc == 0, lc == 3, [ctR, qlR], "Smm")
                if kc == j:
                    em.op("v", lambda e, sps=sps, ksc=ksc, spsR=spsR:
                          e.tensor_tensor(
                              sps[:, 0:TILE], sps[:, 0:TILE],
                              maskt[:, ksc * TILE:(ksc + 1) * TILE],
                              ALU.add),
                          reads=[spsR, "maskt"], writes=[spsR],
                          tag="maskadd", wout=512)
                pbuf = pT[kg % 3]
                pbR = f"pT{kg % 3}"
                em.op("s", lambda e, sps=sps, pbuf=pbuf: e.activation(
                    pbuf[:], sps[:, 0:TILE], AF.Exp,
                    bias=chb[:, 0:1], scale=SCALE),
                    reads=[spsR, "chb"], writes=[pbR], tag="exp", wout=512)

            def emit_PV(kg):
                kc, ksc = kg // 4, kg % 4
                sl = kc % 3
                ckR = f"ckvK_s{sl}"
                pbuf = pT[kg % 3]
                pbR = f"pT{kg % 3}"
                mm(PS[3][0:1, 0:TILE], "ps3", ones_bf[:], pbuf[:],
                   kg == 0, kg == nkg - 1, ["ones_bf", pbR], "denmm")
                for lc in range(4):
                    tgt, tgtR = ((PS[lc], f"ps{lc}") if lc < 3
                                 else (PS[7], "ps7"))
                    mm(tgt[:, 0:TILE], tgtR,
                       ckvK_s[sl][:, ksc * TILE + lc * 128:
                                  ksc * TILE + lc * 128 + 128],
                       pbuf[:],
                       kg == 0, kg == nkg - 1, [ckR, pbR], "PVmm")

            for kg in range(nkg):
                emit_S(kg)
                if kg > 0:
                    emit_PV(kg - 1)
            emit_PV(nkg - 1)

        def emit_tail(h):
            em.op("v", lambda e: e.reciprocal(s1row[0:1, :],
                                              PS[3][0:1, 0:TILE]),
                  reads=["ps3"], writes=["s1row"], tag="recip", wout=512)
            mm(PS[3][:, 0:TILE], "ps3", onesr_f[:], s1row[0:1, :], True, True,
               ["s1row", "onesr_f"], "rbc")
            ev(bca[:], PS[3][:, 0:TILE], ["ps3"], ["bca"], "rbcev", wout=512)
            for lc in range(4):
                src_, srcR = (PS[lc], f"ps{lc}") if lc < 3 else (PS[7], "ps7")
                ev(ylat_b[:, lc * TILE:(lc + 1) * TILE], src_[:, 0:TILE],
                   [srcR], ["ylat_b"], "ylatev", wout=512)
            for lc in range(4):
                mm(PS[3][0:64, 0:TILE], "ps3",
                   VEb[:, lc * D + h * 64: lc * D + h * 64 + 64],
                   ylat_b[:, lc * TILE:(lc + 1) * TILE],
                   lc == 0, lc == 3, ["VEb", "ylat_b"], "yfmm")
            r0 = 64 * (h % 2)
            em.op("v", lambda e, h=h, r0=r0: e.tensor_tensor(
                yT[r0:r0 + 64, (h // 2) * TILE:(h // 2 + 1) * TILE],
                PS[3][0:64, 0:TILE], bca[0:64, :], ALU.mult),
                reads=["ps3", "bca"], writes=["yT"], tag="yev", wout=512)

        return emit_qlat, emit_ch, emit_kloop, emit_tail

    def p1_stage(j, h):
        emit_qlat, emit_ch, emit_kloop, emit_tail = p1_mkhead(j)
        emit_qlat(h)
        if h > 0:
            emit_tail(h - 1)
        emit_ch(h)
        emit_kloop(h)

    def p1_tail_final(j):
        emit_qlat, emit_ch, emit_kloop, emit_tail = p1_mkhead(j)
        emit_tail(NH - 1)

    def p1_epilog(j):
        cur = xt_f[j % 2]
        curR = f"xt_f{j % 2}"
        for dc in range(8):
            em.op("v", lambda e, dc=dc, cur=cur: e.tensor_add(
                yT[:, dc * TILE:(dc + 1) * TILE],
                yT[:, dc * TILE:(dc + 1) * TILE],
                cur[:, dc * TILE:(dc + 1) * TILE]),
                reads=["yT", curR], writes=["yT"], tag="resid", wout=512)
        for dc in range(8):
            mm(PS[6][0:1, 0:TILE], "ps6", ones1f[:],
               yT[:, dc * TILE:(dc + 1) * TILE],
               dc == 0, dc == 7, ["yT", "ones1f"], "s1mm")
        em.op("v", lambda e: e.tensor_scalar(
            s1row[0:1, :], PS[6][0:1, 0:TILE], 1.0 / D, None, ALU.mult),
            reads=["ps6"], writes=["s1row"], tag="mu", wout=512)
        for dc in range(8):
            em.op("s", lambda e, dc=dc: e.activation(
                sq_t[:], yT[:, dc * TILE:(dc + 1) * TILE], AF.Square),
                reads=["yT"], writes=["sq_t"], tag="sq", wout=512)
            mm(PS[6][0:1, 0:TILE], "ps6", ones1f[:], sq_t[:],
               dc == 0, dc == 7, ["sq_t", "ones1f"], "s2mm")
        em.op("v", lambda e: e.tensor_scalar(
            srow[0:1, 0:TILE], PS[6][0:1, 0:TILE], 1.0 / D, None, ALU.mult),
            reads=["ps6"], writes=["srow"], tag="ex2", wout=512)
        em.op("v", lambda e: e.tensor_mul(
            srow[0:1, TILE:2 * TILE], s1row[0:1, :], s1row[0:1, :]),
            reads=["s1row"], writes=["srow"], tag="mu2", wout=512)
        em.op("v", lambda e: e.tensor_sub(
            srow[0:1, 0:TILE], srow[0:1, 0:TILE], srow[0:1, TILE:2 * TILE]),
            reads=["srow"], writes=["srow"], tag="var", wout=512)
        em.op("v", lambda e: e.tensor_scalar(
            srow[0:1, 0:TILE], srow[0:1, 0:TILE], EPS, None, ALU.add),
            reads=["srow"], writes=["srow"], tag="vareps", wout=512)
        em.op("s", lambda e: e.activation(
            srow[0:1, TILE:2 * TILE], srow[0:1, 0:TILE], AF.Ln),
            reads=["srow"], writes=["srow"], tag="lnv", wout=512)
        em.op("s", lambda e: e.activation(
            srow[0:1, 0:TILE], srow[0:1, TILE:2 * TILE], AF.Exp, scale=-0.5),
            reads=["srow"], writes=["srow"], tag="rstd", wout=512)
        mm(PS[6][:, 0:TILE], "ps6", onesr_f[:], s1row[0:1, :], True, True,
           ["s1row", "onesr_f"], "mubc")
        ev(bca[:], PS[6][:, 0:TILE], ["ps6"], ["bca"], "mubcev", wout=512)
        mm(PS[6][:, 0:TILE], "ps6", onesr_f[:], srow[0:1, 0:TILE], True, True,
           ["srow", "onesr_f"], "rsbc")
        ev(bcb[:], PS[6][:, 0:TILE], ["ps6"], ["bcb"], "rsbcev", wout=512)
        for dc in range(8):
            em.op("v", lambda e, dc=dc: e.tensor_sub(
                yT[:, dc * TILE:(dc + 1) * TILE],
                yT[:, dc * TILE:(dc + 1) * TILE], bca[:]),
                reads=["yT", "bca"], writes=["yT"], tag="lnsub", wout=512)
            em.op("v", lambda e, dc=dc: e.tensor_mul(
                yT[:, dc * TILE:(dc + 1) * TILE],
                yT[:, dc * TILE:(dc + 1) * TILE], bcb[:]),
                reads=["yT", "bcb"], writes=["yT"], tag="lnmul", wout=512)
            em.op("s", lambda e, dc=dc: e.copy(
                x1b[:, dc * TILE:(dc + 1) * TILE],
                yT[:, dc * TILE:(dc + 1) * TILE]),
                reads=["yT"], writes=["x1b"], tag="x1cast", wout=512)
        em.dma_burst("sp", [
            (lambda e, j=j, dc=dc: e.dma_start(
                out=x1T_f32[j, dc], in_=yT[:, dc * TILE:(dc + 1) * TILE]),
             ("yT",)) for dc in range(8)],
            writes=[f"h:x1f:{j}"], slot="yT", tag="x1fst")
        em.dma_burst("sp", [
            (lambda e, j=j, dc=dc: e.dma_start(
                out=x1T_b16[j, dc], in_=x1b[:, dc * TILE:(dc + 1) * TILE]),
             ("x1b",)) for dc in range(8)],
            writes=[f"h:x1b:{j}"], slot="x1b", tag="x1bst")
        for tc in range(4):
            tsl = tc % 2
            for dc in range(8):
                em.op("t", lambda e, tc=tc, dc=dc: e.transpose(
                    PS[6][:, 0:128],
                    yT[:, dc * TILE + tc * 128: dc * TILE + tc * 128 + 128],
                    ident[:]),
                    reads=["yT", "ident"], writes=["ps6"], tag="x1tr")
                em.op("v", lambda e, tsl=tsl, dc=dc: e.tensor_copy(
                    x1tok_sb[:, tsl * D + dc * 128: tsl * D + dc * 128 + 128],
                    PS[6][:, 0:128]),
                    reads=["ps6"], writes=[f"x1tok_sb{tsl}"], tag="x1trev", wout=512)
            em.dma("sp", lambda e, j=j, tc=tc, tsl=tsl: e.dma_start(
                out=x1tokh[j, tc], in_=x1tok_sb[:, tsl * D:(tsl + 1) * D]),
                reads=[f"x1tok_sb{tsl}"], writes=[f"h:x1tok:{j}:{tc}"],
                slot=f"x1tok_sb{tsl}", tag="x1tokst")
        for dc in range(8):
            mm(PS[6][0:16, 0:TILE], "ps6", gwT_b[:, dc * E:(dc + 1) * E],
               x1b[:, dc * TILE:(dc + 1) * TILE],
               dc == 0, dc == 7, ["gwT_b", "x1b"], "logit")
        em.op("s", lambda e: e.activation(smt[:, :], PS[6][0:16, 0:TILE],
                                          AF.Exp),
              reads=["ps6"], writes=["smt"], tag="gexp", wout=512)
        mm(PS[6][0:1, 0:TILE], "ps6", ones1f[0:16, :], smt[:, :], True, True,
           ["smt", "ones1f"], "gsum")
        em.op("v", lambda e: e.reciprocal(s1row[0:1, :], PS[6][0:1, 0:TILE]),
              reads=["ps6"], writes=["s1row"], tag="grecip", wout=512)
        mm(PS[6][0:16, 0:TILE], "ps6", onesr_f[:, 0:16], s1row[0:1, :],
           True, True, ["s1row", "onesr_f"], "gbc")
        em.op("v", lambda e, j=j: e.tensor_tensor(
            scn[:, :], smt[:, :], PS[6][0:16, 0:TILE], ALU.mult),
            reads=["smt", "ps6"], writes=["scn"], tag="gmul", wout=512)
        em.dma("sp", lambda e, j=j: e.dma_start(
            out=sc_inh.ap()[:, j * TILE:(j + 1) * TILE], in_=scn[:, :]),
            reads=["scn"], writes=["h:sc_in"], slot="scn", tag="scst")
        em.dma("sp", lambda e, j=j: e.dma_start(
            out=sco[16 * j:16 * j + 16, :], in_=scn[:, :]),
            reads=["scn"], writes=["sco"], slot="scn", tag="sccp")

    p1_xload(0)
    p1_xcast(0)
    p1_ckv(0)
    p1_qchain(0)
    p1_bounds(0)
    p1_stage(0, 0)
    for j in range(NT):
        if j + 1 < NT:
            p1_xload(j + 1)
        for h in range(1, NH):
            p1_stage(j, h)
        p1_tail_final(j)
        if j + 1 < NT:
            p1_xcast(j + 1)
            p1_ckv(j + 1)
            p1_qchain(j + 1)
            p1_bounds(j + 1)
            p1_stage(j + 1, 0)
        p1_epilog(j)

    em.cc(lambda e: e.collective_compute(
        "AllGather", ALU.bypass, replica_groups=[list(range(NCORE))],
        ins=[sc_inh.ap().opt()], outs=[sc_outh.ap().opt()]),
        reads=["h:sc_in"], writes=["h:sc_out"], tag="ag")

    astack.close()
    em.new_block()

    lgstack = contextlib.ExitStack()
    lsb = mksb(lgstack)
    rt = lsb("rt", (128, 6 * TILE), F32)
    r1tok = lsb("r1tok", (128, 4 * 128), F32)
    wtok = lsb("wtok", (128, 4 * 128), BF16)
    r2v = lsb("r2v", (128, 128), F32)
    r2tok = lsb("r2tok", (128, 128), F32)
    th = lsb("th", (128, 8), F32)
    th16 = lsb("th16", (16, 4), F32)
    g2 = lsb("g2", (16, 128), F32)
    g16 = lsb("g16", (128, 16), F32)
    trij = lsb("trij", (128, 128), F32)
    em.dma_burst("sp", [
        (lambda e, d=d_, s=s_: e.dma_start(out=d[:], in_=s.ap()), ())
        for d_, s_ in ((g2, c_g2), (g16, c_g16), (trij, c_trij))],
        writes=["g2", "g16", "trij"], slot="p3c", policy="total", tag="cload3")

    b1stack = contextlib.ExitStack()
    b1sb = mksb(b1stack)
    sc_all = b1sb("sc_all", (128, T), F32)
    bis = b1sb("bis", (128, T), BF16)
    sgw = b1sb("sgw", (128, 2 * 1024), BF16)
    suw = b1sb("suw", (128, 2 * 1024), BF16)
    sdw = b1sb("sdw", (128, 2 * 1024), BF16)
    x1bt = [b1sb(f"x1bt{i}", (128, 8 * TILE), BF16) for i in range(2)]
    hsT = b1sb("hsT", (128, NISC * TILE), BF16)
    gg = b1sb("gg", (128, TILE), BF16)
    shev = b1sb("shev", (128, 2 * TILE), F32)

    em.dma("sp", lambda e: e.dma_start(out=sc_all[:], in_=sc_outh.ap()),
           reads=["h:sc_out"], writes=["sc_all"], slot="sc_all", tag="scld")

    p3_thunks = []
    em.op("v", lambda e: e.memset(th[:, 0:1], 0.0), writes=["th"], tag="bis0")
    em.op("v", lambda e: e.memset(th[:, 1:2], 1.0), writes=["th"], tag="bis1")
    def p3_iter(it):
        em.op("v", lambda e: e.tensor_add(th[:, 2:3], th[:, 0:1], th[:, 1:2]),
              reads=["th"], writes=["th"], tag="bmid1")
        em.op("v", lambda e: e.tensor_scalar(
            th[:, 2:3], th[:, 2:3], 0.5, None, ALU.mult),
            reads=["th"], writes=["th"], tag="bmid2")
        em.op("v", lambda e: e.tensor_scalar(
            bis[:], sc_all[:], th[:, 2:3], 0.0, ALU.is_ge, ALU.add,
            accum_out=th[:, 3:4]),
            reads=["sc_all", "th"], writes=["bis", "th"], tag="bcmp", wout=512)
        mm(PS[0][0:16, 0:1], "ps0", g16[:], th[:, 3:4], True, True,
           ["th", "g16"], "bcnt")
        em.op("v", lambda e: e.tensor_scalar(
            th16[:, 0:1], PS[0][0:16, 0:1], float(CGLOB), None, ALU.is_ge),
            reads=["ps0"], writes=["th16"], tag="bdir")
        mm(PS[1][:, 0:1], "ps1", g2[:], th16[:, 0:1], True, True,
           ["th16", "g2"], "bdirbc")
        ev(th[:, 4:5], PS[1][:, 0:1], ["ps1"], ["th"], "bdirev")
        em.op("v", lambda e: e.tensor_sub(th[:, 5:6], th[:, 2:3], th[:, 0:1]),
              reads=["th"], writes=["th"], tag="blo1")
        em.op("v", lambda e: e.tensor_mul(th[:, 5:6], th[:, 5:6], th[:, 4:5]),
              reads=["th"], writes=["th"], tag="blo2")
        em.op("v", lambda e: e.tensor_add(th[:, 0:1], th[:, 0:1], th[:, 5:6]),
              reads=["th"], writes=["th"], tag="blo3")
        em.op("v", lambda e: e.tensor_sub(th[:, 6:7], th[:, 1:2], th[:, 2:3]),
              reads=["th"], writes=["th"], tag="bhi1")
        em.op("v", lambda e: e.tensor_mul(th[:, 6:7], th[:, 6:7], th[:, 4:5]),
              reads=["th"], writes=["th"], tag="bhi2")
        em.op("v", lambda e: e.tensor_add(th[:, 1:2], th[:, 2:3], th[:, 6:7]),
              reads=["th"], writes=["th"], tag="bhi3")

    for _it in range(44):
        p3_thunks.append(lambda _it=_it: p3_iter(_it))

    def p3_rank_all():
        em.op("v", lambda e: e.tensor_scalar(
            rt[:, 1024:1536], sco[:], th[:, 0:1], None, ALU.is_ge),
            reads=["sco", "th"], writes=["rt"], tag="mask", wout=512)
        em.op("v", lambda e: e.tensor_mul(
            rt[:, 512:1024], sco[:], rt[:, 1024:1536]),
            reads=["sco", "rt"], writes=["rt"], tag="wmask", wout=512)
        em.op("v", lambda e: e.memset(rt[:, 2048:2560], 1.0),
              writes=["rt"], tag="sone")
        em.op("v", lambda e: e.tensor_tensor_scan(
            rt[:, 0:512], rt[:, 2048:2560], rt[:, 1024:1536], 0.0,
            ALU.mult, ALU.add),
            reads=["rt"], writes=["rt"], tag="scan", wout=512)
        em.op("v", lambda e: e.tensor_sub(rt[:, 0:512], rt[:, 0:512],
                                          rt[:, 1024:1536]),
              reads=["rt"], writes=["rt"], tag="rexcl", wout=512)
        em.op("v", lambda e: e.tensor_scalar(
            rt[:, 1536:2048], rt[:, 0:512], float(K1), None, ALU.is_lt),
            reads=["rt"], writes=["rt"], tag="rlt", wout=512)
        em.op("v", lambda e: e.tensor_mul(
            rt[:, 1536:2048], rt[:, 1536:2048], rt[:, 1024:1536]),
            reads=["rt"], writes=["rt"], tag="rvalid", wout=512)
        em.op("v", lambda e: e.tensor_mul(rt[:, 0:512], rt[:, 0:512],
                                          rt[:, 1536:2048]),
              reads=["rt"], writes=["rt"], tag="rv1", wout=512)
        em.op("v", lambda e: e.tensor_scalar(
            rt[:, 2048:2560], rt[:, 1536:2048], -1.0, None, ALU.add),
            reads=["rt"], writes=["rt"], tag="rv2", wout=512)
        em.op("v", lambda e: e.tensor_add(rt[:, 0:512], rt[:, 0:512],
                                          rt[:, 2048:2560]),
              reads=["rt"], writes=["rt"], tag="rank1v", wout=512)
        em.op("v", lambda e: e.tensor_reduce(
            th[:, 5:6], rt[:, 1536:2048], AX.X, ALU.add),
            reads=["rt"], writes=["th"], tag="tilecnt")
        mm(PS[0][:, 0:1], "ps0", trij[:], th[:, 5:6], True, True,
           ["th", "trij"], "base2")
        ev(th[:, 6:7], PS[0][:, 0:1], ["ps0"], ["th"], "base2ev")
        em.op("v", lambda e: e.tensor_scalar(
            r2v[:], iota_row[:], th[:, 6:7], None, ALU.add),
            reads=["th", "iota_row"], writes=["r2v"], tag="r2a")
        em.op("v", lambda e: e.tensor_scalar(
            rt[:, 1536:1664], iota_row[:, 0:128], th[:, 5:6], None, ALU.is_lt),
            reads=["th", "iota_row"], writes=["rt"], tag="r2m1")
        em.op("v", lambda e: e.tensor_scalar(
            rt[:, 1664:1792], r2v[:], float(K2), None, ALU.is_lt),
            reads=["r2v"], writes=["rt"], tag="r2m2")
        em.op("v", lambda e: e.tensor_mul(
            rt[:, 1536:1664], rt[:, 1536:1664], rt[:, 1664:1792]),
            reads=["rt"], writes=["rt"], tag="r2m3")
        em.op("v", lambda e: e.tensor_mul(r2v[:], r2v[:], rt[:, 1536:1664]),
              reads=["rt", "r2v"], writes=["r2v"], tag="r2m4")
        em.op("v", lambda e: e.tensor_scalar(
            rt[:, 1664:1792], rt[:, 1536:1664], -1.0, None, ALU.add),
            reads=["rt"], writes=["rt"], tag="r2m5")
        em.op("v", lambda e: e.tensor_add(r2v[:], r2v[:], rt[:, 1664:1792]),
              reads=["rt", "r2v"], writes=["r2v"], tag="rank2v")
        for tsub in range(4):
            em.op("t", lambda e, tsub=tsub: e.transpose(
                PS[2][:, 0:128], rt[:, tsub * 128:(tsub + 1) * 128], ident[:]),
                reads=["rt", "ident"], writes=["ps2"], tag="r1tr")
            ev(r1tok[:, tsub * 128:(tsub + 1) * 128], PS[2][:, 0:128],
               ["ps2"], ["r1tok"], "r1trev", wout=512)
            em.op("t", lambda e, tsub=tsub: e.transpose(
                PS[2][:, 0:128], rt[:, 512 + tsub * 128: 512 + (tsub + 1) * 128],
                ident[:]),
                reads=["rt", "ident"], writes=["ps2"], tag="wtr")
            ev(wtok[:, tsub * 128:(tsub + 1) * 128], PS[2][:, 0:128],
               ["ps2"], ["wtok"], "wtrev", wout=512)
        em.op("t", lambda e: e.transpose(PS[2][:, 0:128], r2v[:], ident[:]),
              reads=["r2v", "ident"], writes=["ps2"], tag="r2tr")
        ev(r2tok[:], PS[2][:, 0:128], ["ps2"], ["r2tok"], "r2trev", wout=512)

    p3_thunks.append(p3_rank_all)

    for j in range(NT):
        xsl = x1bt[j % 2]
        xR = f"x1bt{j % 2}"
        em.dma_burst("sp", [
            (lambda e, j=j, dc=dc, xsl=xsl: e.dma_start(
                out=xsl[:, dc * TILE:(dc + 1) * TILE], in_=x1T_b16[j, dc]),
             (f"h:x1b:{j}",)) for dc in range(8)],
            writes=[xR], slot=xR, tag="x1bld")
        for isc in range(NISC):
            sl = isc % 2
            sgR, suR = f"sgw{sl}", f"suw{sl}"
            em.dma("g", lambda e, isc=isc, sl=sl: e.dma_start(
                out=sgw[:, sl * 1024:(sl + 1) * 1024], in_=sg_dev[isc]),
                writes=[sgR], slot=sgR, tag="sgld")
            em.dma("g", lambda e, isc=isc, sl=sl: e.dma_start(
                out=suw[:, sl * 1024:(sl + 1) * 1024], in_=su_dev[isc]),
                writes=[suR], slot=suR, tag="suld")
            for dc in range(8):
                mm(PS[3][:, 0:TILE], "ps3",
                   sgw[:, sl * 1024 + dc * 128: sl * 1024 + dc * 128 + 128],
                   xsl[:, dc * TILE:(dc + 1) * TILE],
                   dc == 0, dc == 7, [sgR, xR], "sgmm")
            for dc in range(8):
                mm(PS[4][:, 0:TILE], "ps4",
                   suw[:, sl * 1024 + dc * 128: sl * 1024 + dc * 128 + 128],
                   xsl[:, dc * TILE:(dc + 1) * TILE],
                   dc == 0, dc == 7, [suR, xR], "summ")
            em.op("s", lambda e: e.activation(gg[:], PS[3][:, 0:TILE], AF.Gelu),
                  reads=["ps3"], writes=["gg"], tag="sgelu", wout=512)
            em.op("v", lambda e, isc=isc: e.tensor_tensor(
                hsT[:, isc * TILE:(isc + 1) * TILE], gg[:], PS[4][:, 0:TILE],
                ALU.mult),
                reads=["gg", "ps4"], writes=["hsT"], tag="smul", wout=512)
            if p3_thunks:
                p3_thunks.pop(0)()
        for isc in range(NISC):
            sl = isc % 2
            sdR = f"sdw{sl}"
            em.dma("g", lambda e, isc=isc, sl=sl: e.dma_start(
                out=sdw[:, sl * 1024:(sl + 1) * 1024], in_=sd_dev[isc]),
                writes=[sdR], slot=sdR, tag="sdld")
            for dc in range(8):
                mm(PS[dc][:, 0:TILE], f"ps{dc}",
                   sdw[:, sl * 1024 + dc * 128: sl * 1024 + dc * 128 + 128],
                   hsT[:, isc * TILE:(isc + 1) * TILE],
                   isc == 0, isc == NISC - 1, [sdR, "hsT"], "sdmm")
        for dc in range(8):
            ssl = dc % 2
            ev(shev[:, ssl * TILE:(ssl + 1) * TILE], PS[dc][:, 0:TILE],
               [f"ps{dc}"], [f"shev{ssl}"], "shevc", wout=512)
            em.dma("sp", lambda e, j=j, dc=dc, ssl=ssl: e.dma_start(
                out=shTh[j, dc], in_=shev[:, ssl * TILE:(ssl + 1) * TILE]),
                reads=[f"shev{ssl}"], writes=[f"h:shT:{j}"], slot=f"shev{ssl}",
                tag="shst")

    while p3_thunks:
        p3_thunks.pop(0)()

    b1stack.close()
    em.new_block()

    cstack = contextlib.ExitStack()
    csb = mksb(cstack)
    iota768 = csb("iota768", (128, K2), F32)
    em.dma("sp", lambda e: e.dma_start(out=iota768[:], in_=c_iota768.ap()),
           writes=["iota768"], slot="p5c", policy="total", tag="cload5")
    x1t_sb = [csb(f"x1t_sb{i}", (128, 4 * D), BF16) for i in range(2)]
    r1b = csb("r1b", (128, 4 * 128), BF16)
    r2b = csb("r2b", (128, NT * K2), BF16)
    q2scr = csb("q2scr", (128, 128), F32)
    q2b6 = csb("q2b6", (128, (K2 // 128) * 128), BF16)
    zb = csb("zb", (128, NT * D), BF16)
    zwb = csb("zwb", (128, NT), BF16)
    xgT = csb("xgT", (128, 8 * K2), BF16)
    wgv = [csb(f"wgv{i}", (128, 1024), BF16) for i in range(2)]
    wuv_ = [csb(f"wuv_{i}", (128, 1024), BF16) for i in range(2)]
    wdv = [csb(f"wdv{i}", (128, 512), BF16) for i in range(4)]
    hT = csb("hT", (128, NIC * K2), BF16)
    ggx = csb("ggx", (128, K2), BF16)
    eoT = csb("eoT", (128, (K2 // 128) * D), BF16)
    wgc = csb("wgc", (128, 8), F32)
    wrow = csb("wrow", (1, K2), F32)
    s2ev = csb("s2ev", (128, 2 * D), BF16)

    for ei in range(E):
        for j in range(NT):
            p = 16 * j + ei
            xs = x1t_sb[j % 2]
            xsR = f"x1t_sb{j % 2}"
            em.dma_burst("sp", [
                (lambda e, j=j, tc=tc, xs=xs: e.dma_start(
                    out=xs[:, tc * D:(tc + 1) * D], in_=x1tokh[j, tc]),
                 (f"h:x1tok:{j}:{tc}",)) for tc in range(4)],
                writes=[xsR], slot=xsR, tag="x1tld")
            for tsub in range(4):
                em.op("v", lambda e, p=p, tsub=tsub: e.tensor_scalar(
                    r1b[:, tsub * 128:(tsub + 1) * 128], iota_row[:],
                    r1tok[:, tsub * 128 + p: tsub * 128 + p + 1], None,
                    ALU.is_equal),
                    reads=["r1tok", "iota_row"], writes=["r1b"], tag="r1bld", wout=512)
            for dh in range(2):
                pb, pbR = (PS[0], "ps0") if dh == 0 else (PS[1], "ps1")
                for tsub in range(4):
                    mm(pb[:, 0:512], pbR,
                       r1b[:, tsub * 128:(tsub + 1) * 128],
                       xs[:, tsub * D + dh * 512: tsub * D + dh * 512 + 512],
                       tsub == 0, tsub == 3, ["r1b", xsR], "zmm")
                em.op("v", lambda e, j=j, dh=dh, pb=pb: e.tensor_copy(
                    zb[:, j * D + dh * 512: j * D + dh * 512 + 512],
                    pb[:, 0:512]),
                    reads=[pbR], writes=["zb"], tag="zev", wout=512)
            for tsub in range(4):
                mm(PS[2][:, 0:1], "ps2",
                   r1b[:, tsub * 128:(tsub + 1) * 128],
                   wtok[:, tsub * 128 + p: tsub * 128 + p + 1],
                   tsub == 0, tsub == 3, ["r1b", "wtok"], "zwmm")
            em.op("v", lambda e, j=j: e.tensor_copy(
                zwb[:, j: j + 1], PS[2][:, 0:1]),
                reads=["ps2"], writes=["zwb"], tag="zwev")
            em.op("v", lambda e, p=p, j=j: e.tensor_scalar(
                r2b[:, j * K2:(j + 1) * K2], iota768[:],
                r2tok[:, p: p + 1], None, ALU.is_equal),
                reads=["r2tok", "iota768"], writes=["r2b"], tag="r2bld", wout=512)
        for dc in range(8):
            pa, paR = (PS[3], "ps3") if dc % 2 == 0 else (PS[5], "ps5")
            pb2, pbR2 = (PS[4], "ps4") if dc % 2 == 0 else (PS[6], "ps6")
            for j in range(NT):
                mm(pa[:, 0:512], paR,
                   zb[:, j * D + dc * 128: j * D + dc * 128 + 128],
                   r2b[:, j * K2: j * K2 + 512],
                   j == 0, j == NT - 1, ["zb", "r2b"], "xgmmA")
                mm(pb2[:, 0:K2 - 512], pbR2,
                   zb[:, j * D + dc * 128: j * D + dc * 128 + 128],
                   r2b[:, j * K2 + 512: j * K2 + K2],
                   j == 0, j == NT - 1, ["zb", "r2b"], "xgmmB")
            em.op("v", lambda e, dc=dc, pa=pa: e.tensor_copy(
                xgT[:, dc * K2: dc * K2 + 512], pa[:, 0:512]),
                reads=[paR], writes=["xgT"], tag="xgevA", wout=512)
            em.op("v", lambda e, dc=dc, pb2=pb2: e.tensor_copy(
                xgT[:, dc * K2 + 512: dc * K2 + K2], pb2[:, 0:K2 - 512]),
                reads=[pbR2], writes=["xgT"], tag="xgevB", wout=512)
        for j in range(NT):
            mm(PS[7][0:1, 0:512], "ps7", zwb[:, j: j + 1],
               r2b[:, j * K2: j * K2 + 512],
               j == 0, j == NT - 1, ["zwb", "r2b"], "wgmA")
            mm(PS[2][0:1, 0:K2 - 512], "ps2", zwb[:, j: j + 1],
               r2b[:, j * K2 + 512: j * K2 + K2],
               j == 0, j == NT - 1, ["zwb", "r2b"], "wgmB")
        em.op("v", lambda e: e.tensor_copy(wrow[0:1, 0:512], PS[7][0:1, 0:512]),
              reads=["ps7"], writes=["wrow"], tag="wrevA", wout=512)
        em.op("v", lambda e: e.tensor_copy(wrow[0:1, 512:K2],
                                           PS[2][0:1, 0:K2 - 512]),
              reads=["ps2"], writes=["wrow"], tag="wrevB", wout=512)
        for cc in range(K2 // 128):
            mm(PS[3][:, 0:1], "ps3", wrow[0:1, cc * 128:(cc + 1) * 128],
               ones1f[0:1, 0:1], True, True, ["wrow", "ones1f"], "wgtr")
            em.op("v", lambda e, cc=cc: e.tensor_copy(
                wgc[:, cc: cc + 1], PS[3][:, 0:1]),
                reads=["ps3"], writes=["wgc"], tag="wgtrev")
        for ic in range(NIC):
            sl = ic % 2
            wgR, wuR = f"wgv{sl}", f"wuv_{sl}"
            em.dma("g", lambda e, ei=ei, ic=ic, sl=sl: e.dma_start(
                out=wgv[sl][:], in_=wg_dev[ei, ic]),
                writes=[wgR], slot=wgR, tag="wgld")
            em.dma("g", lambda e, ei=ei, ic=ic, sl=sl: e.dma_start(
                out=wuv_[sl][:], in_=wu_dev[ei, ic]),
                writes=[wuR], slot=wuR, tag="wuld")
            pg, pgR = (PS[0], "ps0") if sl == 0 else (PS[4], "ps4")
            pg2, pg2R = (PS[1], "ps1") if sl == 0 else (PS[5], "ps5")
            pu, puR = (PS[2], "ps2") if sl == 0 else (PS[6], "ps6")
            pu2, pu2R = (PS[3], "ps3") if sl == 0 else (PS[7], "ps7")
            for dc in range(8):
                mm(pg[:, 0:512], pgR, wgv[sl][:, dc * 128:(dc + 1) * 128],
                   xgT[:, dc * K2: dc * K2 + 512],
                   dc == 0, dc == 7, [wgR, "xgT"], "upgA")
                mm(pg2[:, 0:K2 - 512], pg2R, wgv[sl][:, dc * 128:(dc + 1) * 128],
                   xgT[:, dc * K2 + 512: dc * K2 + K2],
                   dc == 0, dc == 7, [wgR, "xgT"], "upgB")
            for dc in range(8):
                mm(pu[:, 0:512], puR, wuv_[sl][:, dc * 128:(dc + 1) * 128],
                   xgT[:, dc * K2: dc * K2 + 512],
                   dc == 0, dc == 7, [wuR, "xgT"], "upuA")
                mm(pu2[:, 0:K2 - 512], pu2R, wuv_[sl][:, dc * 128:(dc + 1) * 128],
                   xgT[:, dc * K2 + 512: dc * K2 + K2],
                   dc == 0, dc == 7, [wuR, "xgT"], "upuB")
            em.op("s", lambda e, pg=pg: e.activation(
                ggx[:, 0:512], pg[:, 0:512], AF.Gelu),
                reads=[pgR], writes=["ggx"], tag="mgeluA", wout=512)
            em.op("s", lambda e, pg2=pg2: e.activation(
                ggx[:, 512:K2], pg2[:, 0:K2 - 512], AF.Gelu),
                reads=[pg2R], writes=["ggx"], tag="mgeluB", wout=512)
            em.op("v", lambda e, ic=ic, pu=pu: e.tensor_tensor(
                hT[:, ic * K2: ic * K2 + 512], ggx[:, 0:512], pu[:, 0:512],
                ALU.mult),
                reads=["ggx", puR], writes=["hT"], tag="hmulA", wout=512)
            em.op("v", lambda e, ic=ic, pu2=pu2: e.tensor_tensor(
                hT[:, ic * K2 + 512: ic * K2 + K2], ggx[:, 512:K2],
                pu2[:, 0:K2 - 512], ALU.mult),
                reads=["ggx", pu2R], writes=["hT"], tag="hmulB", wout=512)
        for dh in range(2):
            for ic in range(NIC):
                sl = ic % 4
                wdR = f"wdv{sl}"
                em.dma("g", lambda e, ei=ei, dh=dh, ic=ic, sl=sl: e.dma_start(
                    out=wdv[sl][:], in_=wd_dev[ei, dh, ic]),
                    writes=[wdR], slot=wdR, tag="wdld")
                for cc in range(K2 // 128):
                    mm(PS[cc][:, 0:512], f"ps{cc}",
                       hT[:, ic * K2 + cc * 128: ic * K2 + cc * 128 + 128],
                       wdv[sl][:],
                       ic == 0, ic == NIC - 1, ["hT", wdR], "dnmm")
            for cc in range(K2 // 128):
                em.op("v", lambda e, cc=cc, dh=dh: e.tensor_scalar(
                    eoT[:, cc * D + dh * 512: cc * D + dh * 512 + 512],
                    PS[cc][:, 0:512], wgc[:, cc: cc + 1], None, ALU.mult),
                    reads=[f"ps{cc}", "wgc"], writes=["eoT"], tag="eoev", wout=512)
        for j in range(NT):
            p = 16 * j + ei
            for cc in range(K2 // 128):
                em.op("v", lambda e, p=p, cc=cc: e.tensor_scalar(
                    q2scr[:], iota_row[:], float(cc * 128),
                    r2tok[:, p: p + 1], ALU.add, ALU.is_equal),
                    reads=["r2tok", "iota_row"], writes=["q2scr"], tag="q2eq", wout=512)
                em.op("t", lambda e: e.transpose(
                    PS[2][:, 0:128], q2scr[:], ident[:]),
                    reads=["q2scr", "ident"], writes=["ps2"], tag="q2tr")
                em.op("v", lambda e, cc=cc: e.tensor_copy(
                    q2b6[:, cc * 128:(cc + 1) * 128], PS[2][:, 0:128]),
                    reads=["ps2"], writes=["q2b6"], tag="q2ev", wout=512)
            ssl = j % 2
            for dh in range(2):
                pb, pbR = (PS[0], "ps0") if dh == 0 else (PS[1], "ps1")
                for cc in range(K2 // 128):
                    mm(pb[:, 0:512], pbR,
                       q2b6[:, cc * 128:(cc + 1) * 128],
                       eoT[:, cc * D + dh * 512: cc * D + dh * 512 + 512],
                       cc == 0, cc == K2 // 128 - 1, ["q2b6", "eoT"], "s2mm")
                em.op("v", lambda e, dh=dh, pb=pb, ssl=ssl: e.tensor_copy(
                    s2ev[:, ssl * D + dh * 512: ssl * D + (dh + 1) * 512],
                    pb[:, 0:512]),
                    reads=[pbR], writes=[f"s2ev{ssl}"], tag="s2evc", wout=512)
            em.dma("sp", lambda e, ei=ei, j=j, ssl=ssl: e.dma_start(
                out=s2h[ei, j], in_=s2ev[:, ssl * D:(ssl + 1) * D]),
                reads=[f"s2ev{ssl}"], writes=[f"h:s2:{ei}:{j}"],
                slot=f"s2ev{ssl}", tag="s2st")

    cstack.close()
    em.new_block()

    dstack = contextlib.ExitStack()
    dsb = mksb(dstack)
    s2sl = [dsb(f"s2sl{i}", (128, D), BF16) for i in range(4)]
    q1all = dsb("q1all", (128, E * TILE), BF16)
    x1ld = dsb("x1ld", (128, 8 * TILE), F32)
    shld = dsb("shld", (128, 8 * TILE), F32)
    x2p = dsb("x2p", (128, 8 * TILE), F32)
    x2tok = dsb("x2tok", (128, 4 * D), F32)
    sqscr = dsb("sqscr", (128, D), F32)
    sttb = dsb("sttb", (128, 4), F32)
    bcq = dsb("bcq", (128, TILE), F32)

    for j in range(NT):
        em.dma_burst("sp", [
            (lambda e, j=j, dc=dc: e.dma_start(
                out=x1ld[:, dc * TILE:(dc + 1) * TILE], in_=x1T_f32[j, dc]),
             (f"h:x1f:{j}",)) for dc in range(8)],
            writes=["x1ld"], slot="x1ld", tag="x1rld")
        em.dma_burst("sp", [
            (lambda e, j=j, dc=dc: e.dma_start(
                out=shld[:, dc * TILE:(dc + 1) * TILE], in_=shTh[j, dc]),
             (f"h:shT:{j}",)) for dc in range(8)],
            writes=["shld"], slot="shld", tag="shrld")
        for ei in range(E):
            p = 16 * j + ei
            for tsub in range(4):
                pb, pbR = (PS[0], "ps0") if tsub % 2 == 0 else (PS[1], "ps1")
                em.op("v", lambda e, p=p, tsub=tsub: e.tensor_scalar(
                    bcq[:, 0:128], iota_row[:],
                    r1tok[:, tsub * 128 + p: tsub * 128 + p + 1], None,
                    ALU.is_equal),
                    reads=["r1tok", "iota_row"], writes=["bcq"], tag="q1eq", wout=512)
                em.op("t", lambda e, pb=pb: e.transpose(
                    pb[:, 0:128], bcq[:, 0:128], ident[:]),
                    reads=["bcq", "ident"], writes=[pbR], tag="q1tr")
                em.op("v", lambda e, ei=ei, tsub=tsub, pb=pb: e.tensor_copy(
                    q1all[:, ei * TILE + tsub * 128:
                           ei * TILE + tsub * 128 + 128],
                    pb[:, 0:128]),
                    reads=[pbR], writes=["q1all"], tag="q1ev", wout=512)
        for ei in range(E):
            sl = ei % 4
            s2R = f"s2sl{sl}"
            em.dma("sp", lambda e, ei=ei, j=j, sl=sl: e.dma_start(
                out=s2sl[sl][:], in_=s2h[ei, j]),
                reads=[f"h:s2:{ei}:{j}"], writes=[s2R], slot=s2R, tag="s2ld")
            for dc in range(8):
                mm(PS[dc][:, 0:TILE], f"ps{dc}",
                   s2sl[sl][:, dc * 128:(dc + 1) * 128],
                   q1all[:, ei * TILE:(ei + 1) * TILE],
                   ei == 0, ei == E - 1, [s2R, "q1all"], "scmm")
        for dc in range(8):
            em.op("v", lambda e, dc=dc: e.scalar_tensor_tensor(
                x2p[:, dc * TILE:(dc + 1) * TILE], PS[dc][:, 0:TILE], 0.0,
                x1ld[:, dc * TILE:(dc + 1) * TILE], ALU.add, ALU.add),
                reads=[f"ps{dc}", "x1ld"], writes=["x2p"], tag="x2a", wout=512)
            em.op("v", lambda e, dc=dc: e.tensor_add(
                x2p[:, dc * TILE:(dc + 1) * TILE],
                x2p[:, dc * TILE:(dc + 1) * TILE],
                shld[:, dc * TILE:(dc + 1) * TILE]),
                reads=["x2p", "shld"], writes=["x2p"], tag="x2b", wout=512)
        for tc in range(4):
            for dc in range(8):
                pb, pbR = (PS[0], "ps0") if dc % 2 == 0 else (PS[1], "ps1")
                em.op("t", lambda e, tc=tc, dc=dc, pb=pb: e.transpose(
                    pb[:, 0:128],
                    x2p[:, dc * TILE + tc * 128: dc * TILE + tc * 128 + 128],
                    ident[:]),
                    reads=["x2p", "ident"], writes=[pbR], tag="x2tr")
                em.op("v", lambda e, tc=tc, dc=dc, pb=pb: e.tensor_copy(
                    x2tok[:, tc * D + dc * 128: tc * D + dc * 128 + 128],
                    pb[:, 0:128]),
                    reads=[pbR], writes=["x2tok"], tag="x2trev", wout=512)
        for tc in range(4):
            em.op("v", lambda e, tc=tc: e.tensor_reduce(
                sttb[:, 0:1], x2tok[:, tc * D:(tc + 1) * D], AX.X, ALU.add),
                reads=["x2tok"], writes=["sttb"], tag="mean1")
            em.op("v", lambda e: e.tensor_scalar(
                sttb[:, 0:1], sttb[:, 0:1], 1.0 / D, None, ALU.mult),
                reads=["sttb"], writes=["sttb"], tag="mean2")
            em.op("v", lambda e, tc=tc: e.tensor_scalar(
                x2tok[:, tc * D:(tc + 1) * D], x2tok[:, tc * D:(tc + 1) * D],
                sttb[:, 0:1], None, ALU.subtract),
                reads=["x2tok", "sttb"], writes=["x2tok"], tag="center", wout=512)
            em.op("s", lambda e, tc=tc: e.activation(
                sqscr[:], x2tok[:, tc * D:(tc + 1) * D], AF.Square,
                accum_out=sttb[:, 1:2]),
                reads=["x2tok"], writes=["sqscr", "sttb"], tag="sqacc", wout=512)
            em.op("v", lambda e: e.tensor_scalar(
                sttb[:, 1:2], sttb[:, 1:2], 1.0 / D, EPS, ALU.mult, ALU.add),
                reads=["sttb"], writes=["sttb"], tag="var2")
            em.op("s", lambda e: e.activation(
                sttb[:, 2:3], sttb[:, 1:2], AF.Ln),
                reads=["sttb"], writes=["sttb"], tag="lnv2")
            em.op("s", lambda e: e.activation(
                sttb[:, 2:3], sttb[:, 2:3], AF.Exp, scale=-0.5),
                reads=["sttb"], writes=["sttb"], tag="rstd2")
            em.op("v", lambda e, tc=tc: e.tensor_scalar(
                x2tok[:, tc * D:(tc + 1) * D], x2tok[:, tc * D:(tc + 1) * D],
                sttb[:, 2:3], None, ALU.mult),
                reads=["x2tok", "sttb"], writes=["x2tok"], tag="lnap2", wout=512)
            em.dma("sp", lambda e, j=j, tc=tc: e.dma_start(
                out=out[j * TILE + tc * 128: j * TILE + tc * 128 + 128, :],
                in_=x2tok[:, tc * D:(tc + 1) * D]),
                reads=["x2tok"], writes=[f"h:out:{j}:{tc}"], slot="x2tok",
                tag="outst")
    if DEV_DUMPS:
        em.dma("sp", lambda e: e.dma_start(
            out=d_x1.tensor.ap(), in_=x1T_f32.tensor.ap()),
            reads=[f"h:x1f:{j}" for j in range(NT)], writes=["h:d1"],
            slot="dev", policy="total", tag="dump")
        em.dma("sp", lambda e: e.dma_start(
            out=d_sh.tensor.ap(), in_=shTh.tensor.ap()),
            reads=[f"h:shT:{j}" for j in range(NT)], writes=["h:d2"],
            slot="dev", policy="total", tag="dump")
        em.dma("sp", lambda e: e.dma_start(
            out=d_sc.tensor.ap(), in_=sc_outh.ap()),
            reads=["h:sc_out"], writes=["h:d3"], slot="dev", policy="total",
            tag="dump")
        em.dma("sp", lambda e: e.dma_start(out=d_th.tensor.ap(), in_=th[:]),
               reads=["th"], writes=["h:d4"], slot="dev", policy="total",
               tag="dump")
        em.dma("sp", lambda e: e.dma_start(out=d_rt.tensor.ap(), in_=rt[:]),
               reads=["rt"], writes=["h:d5"], slot="dev", policy="total",
               tag="dump")
        em.dma("g", lambda e: e.dma_start(
            out=d_s2.tensor.ap(), in_=s2h.tensor.ap()),
            reads=[f"h:s2:{ei}:{j}" for ei in range(E) for j in range(NT)],
            writes=["h:d6"], slot="dev2", policy="total", tag="dump")
        em.op("sp", lambda e: e.nop(),
              reads=["h:d1", "h:d2", "h:d3", "h:d4", "h:d5"], tag="findev")
        em.op("g", lambda e: e.nop(), reads=["h:d6"], tag="findev2")
    em.op("sp", lambda e: e.nop(),
          reads=[f"h:out:{j}:{tc}" for j in range(NT) for tc in range(4)],
          tag="fin")

    dstack.close()
    em.emit()
    lgstack.close()
    gstack.close()
    return nc


_NC_CACHE = None


def _get_nc():
    global _NC_CACHE
    if _NC_CACHE is None:
        _NC_CACHE = build_program()
    return _NC_CACHE


def _host_layout(inputs):
    f32 = np.float32

    def c(a):
        return np.ascontiguousarray(a, dtype=f32)

    W_dq = np.asarray(inputs["L0_attn_W_dq"])
    W_uq = np.asarray(inputs["L0_attn_W_uq"])
    W_dkv = np.asarray(inputs["L0_attn_W_dkv"])
    W_uk = np.asarray(inputs["L0_attn_W_uk"])
    W_uv = np.asarray(inputs["L0_attn_W_uv"])
    W_o = np.asarray(inputs["L0_attn_W_o"])
    gate_w = np.asarray(inputs["L0_ffn_gate_w"])
    Wg = np.asarray(inputs["L0_ffn_Wg"])
    Wu = np.asarray(inputs["L0_ffn_Wu"])
    Wd = np.asarray(inputs["L0_ffn_Wd"])
    Sg = np.asarray(inputs["L0_ffn_Sg"])
    Su = np.asarray(inputs["L0_ffn_Su"])
    Sd = np.asarray(inputs["L0_ffn_Sd"])

    shared = {
        "wdq": c(W_dq.reshape(4, 128, D)),
        "wdqT": c(W_dq.T.reshape(8, 128, QL)),
        "wuq": c(W_uq.reshape(8, 128, QL)),
        "wuqT": c(W_uq.T.reshape(4, 128, D)),
        "wuk": c(W_uk.reshape(8, 128, L)),
        "wuv": c(W_uv.reshape(8, 128, L)),
        "woT": c(W_o.T.reshape(8, 128, D)),
        "wdkvT": c(W_dkv.T.reshape(8, 128, L)),
        "gwT": c(gate_w.T.reshape(8, 128, E)),
        "sg_dev": c(Sg.T.reshape(8, 128, NISC, 128).transpose(2, 1, 0, 3)
                    .reshape(NISC, 128, 1024)),
        "su_dev": c(Su.T.reshape(8, 128, NISC, 128).transpose(2, 1, 0, 3)
                    .reshape(NISC, 128, 1024)),
        "sd_dev": c(Sd.T.reshape(NISC, 128, 1024)),
        "wg_dev": c(np.stack([
            Wg[e_].T.reshape(8, 128, NIC, 128).transpose(2, 1, 0, 3)
            .reshape(NIC, 128, 1024) for e_ in range(E)])),
        "wu_dev": c(np.stack([
            Wu[e_].T.reshape(8, 128, NIC, 128).transpose(2, 1, 0, 3)
            .reshape(NIC, 128, 1024) for e_ in range(E)])),
        "wd_dev": c(np.stack([
            Wd[e_].T.reshape(NIC, 128, 2, 512).transpose(2, 0, 1, 3)
            for e_ in range(E)])),
    }
    return shared


def kernel(**inputs):
    x = np.asarray(inputs["L0_attn_x"])
    nc = _get_nc()
    shared = _host_layout(inputs)
    in_maps = []
    for cid in range(NCORE):
        m = dict(shared)
        m["xT"] = np.ascontiguousarray(x[cid].T.reshape(8, 128, T),
                                       dtype=np.float32)
        in_maps.append(m)
    res = run_bass_kernel_spmd(nc, in_maps, core_ids=list(range(NCORE)))
    return np.stack([res.results[cid]["out"] for cid in range(NCORE)], axis=0)


if __name__ == "__main__":
    build_program()
    print("build ok")
